# version 26
# baseline (speedup 1.0000x reference)
import contextlib
import numpy as np
import ml_dtypes
import concourse.bass as bass
import concourse.mybir as mybir
from concourse.bass_utils import run_bass_kernel_spmd

F32 = mybir.dt.float32
BF16 = mybir.dt.bfloat16
I32 = mybir.dt.int32
AF = mybir.ActivationFunctionType
ALU = mybir.AluOpType
AXX = mybir.AxisListType.X

D = 1024
H = 16
DH = 64
MF = 256
DFF = 4096
AHALF = 4096
ETOK = 2048
EPS_K = 1e-4
LN_EPS = 1e-5
NORM = DH ** -0.25
RATIO = MF ** -0.5
LNRATIO = float(np.log(RATIO))
LNEPS = float(np.log(EPS_K))
RMAGIC = 12582912.0
FLAG_MAGIC = 1597463007
REPS = RATIO * EPS_K

ATILES = AHALF // 128
ETILES = ETOK // 128
EXW = 2048 + 32 + 8 + 16
SBUF_BYTES = 224 * 1024
SBUF_BASE = 17408

PS, PA, PX1, PX2, PE1, PE2 = 0, 1, 2, 3, 4, 5
DEBUG = False

_NC_CACHE = {}


class Alloc:

    def __init__(self):
        self.items = []

    def place(self, name, nbytes, lo, hi, align=64):
        nbytes = (nbytes + align - 1) // align * align
        spans = sorted((off, off + sz) for (n, off, sz, l, h) in self.items
                       if not (h < lo or l > hi))
        gaps = []
        cur = SBUF_BASE
        for st, en in spans:
            if st > cur:
                gaps.append((cur, st))
            cur = max(cur, en)
        if cur < SBUF_BYTES:
            gaps.append((cur, SBUF_BYTES))
        off = None
        best = None
        for gs, ge_ in gaps:
            if ge_ - gs >= nbytes and (best is None or ge_ - gs < best):
                best = ge_ - gs
                off = gs
        assert off is not None, \
            f"SBUF overflow placing {name} ({nbytes}B, [{lo},{hi}])"
        self.items.append((name, off, nbytes, lo, hi))
        return off


def strided(ap2d, p_cnt, d1_step, d1_cnt, d2_cnt, off):
    return bass.AP(ap2d.tensor, ap2d.offset + off,
                   [list(ap2d.ap[0])[:1] + [p_cnt], [d1_step, d1_cnt],
                    [1, d2_cnt]])


def bcast(ap2d, cnt, rep):
    return bass.AP(ap2d.tensor, ap2d.offset,
                   [list(ap2d.ap[0]), [1, cnt], [0, rep]])


class Prog:
    def __init__(self, nc):
        self.nc = nc
        self.cnt = {}
        self.hw = {}

    def mksem(self, stack, name):
        h = stack.enter_context(self.nc.semaphore(name))
        self.cnt[name] = [h, 0]

    def sem(self, name):
        return self.cnt[name][0]

    def inc(self, ins, name, amt):
        e = self.cnt[name]
        ins.then_inc(e[0], amt)
        e[1] += amt
        return (name, e[1])

    def tok(self, name):
        return (name, self.cnt[name][1])

    def wait(self, eng, *tokens):
        ek = id(eng)
        for tk in tokens:
            if tk is None:
                continue
            name, val = tk
            if val <= 0:
                continue
            k = (ek, name)
            if self.hw.get(k, -1) >= val:
                continue
            eng.wait_ge(self.cnt[name][0], val)
            self.hw[k] = val


class PG:

    def __init__(self, P):
        self.P = P
        self.last_pe = None
        self.last_rd = None

    def pe(self, eng):
        self.P.wait(eng, self.last_rd)

    def rd(self, eng):
        self.P.wait(eng, self.last_pe)

    def set_pe(self, tok):
        self.last_pe = tok

    def set_rd(self, tok):
        self.last_rd = tok


class PG2:
    def __init__(self, a, b):
        self.g = (a, b)

    def pe(self, eng):
        self.g[0].pe(eng)
        self.g[1].pe(eng)

    def rd(self, eng):
        self.g[0].rd(eng)
        self.g[1].rd(eng)

    def set_pe(self, tok):
        self.g[0].set_pe(tok)
        self.g[1].set_pe(tok)

    def set_rd(self, tok):
        self.g[0].set_rd(tok)
        self.g[1].set_rd(tok)


def build_nc():
    nc = bass.Bass("TRN2", target_bir_lowering=False, num_devices=1,
                   dynamic_dma_scratch_size=16384,
                   detect_race_conditions=False)

    def din(name, shape, dtype=F32):
        return nc.dram_tensor(name, shape, dtype, kind="ExternalInput").ap()

    xh = din("xh", [AHALF, D])
    xe = din("xe", [ETOK, D])
    cid = din("cid", [1, 1], I32)
    wqT = din("wqT", [D, D])
    wkT = din("wkT", [D, D])
    wvT = din("wvT", [D, D])
    woT = din("woT", [D, D])
    w1T = din("w1T", [D, DFF])
    w2T = din("w2T", [DFF, D])
    projTd = din("projTd", [DH, MF])
    rhs_d = din("rhs", [D, D])
    rhsT_d = din("rhsT", [D, D])
    bq128 = din("bq128", [128, 8])
    bk128 = din("bk128", [128, 8])
    b1fm_d = din("b1fm", [128, 32])
    bv_row = din("bv_row", [1, D])
    bo_row = din("bo_row", [1, D])
    b2_row = din("b2_row", [1, D])
    g2_row = din("g2_row", [1, D])
    be2_row = din("be2_row", [1, D])
    g3_row = din("g3_row", [1, D])
    be3_row = din("be3_row", [1, D])
    eye_d = din("eye", [128, 128])
    dsel_d = din("dsel", [128, 128])

    y_out = nc.dram_tensor("y", [ETOK, D], F32, kind="ExternalOutput").ap()
    dbg = {}
    if DEBUG:
        for nm, sh in [("d_y1", [128, D]), ("d_kfm", [128, 512]),
                       ("d_ek", [128, 512]), ("d_wdg", [128, 16]),
                       ("d_exch", [128, EXW]), ("d_comb", [128, EXW]),
                       ("d_ctxa", [128, 2 * H * 65]), ("d_stil", [128, H * 65]),
                       ("d_eq", [128, 512]), ("d_beta", [128, 16]),
                       ("d_nums", [128, H * 65]), ("d_tres", [128, D]),
                       ("d_u", [128, D]), ("d_uff", [128, D]),
                       ("d_x2", [128, D]), ("d_ql", [128, D]),
                       ("d_xin", [128, D]), ("d_stat", [128, 16]),
                       ("d_dmpA", [128, D]), ("d_dmpB", [128, D]),
                       ("d_qfm", [128, 512]), ("d_y1fe", [128, 512]),
                       ("d_xh3", [128, D]), ("d_xin2", [128, D]),
                       ("d_y1e", [128, D]),
                       ("d_g3b", [128, D]), ("d_be3b", [128, D]),
                       ("d_g3p0", [128, D]), ("d_g3p1", [128, D]),
                       ("d_g3p2", [128, D]), ("d_g3p3", [128, D]),
                       ("d_g3p4", [128, D]), ("d_g3p5", [128, D]),
                       ("d_lsc", [128, 8])]:
            dbg[nm] = nc.dram_tensor(nm, sh, F32, kind="ExternalOutput").ap()

    def ddump(nm, src_ap):
        if not DEBUG:
            return
        P.wait(ge, P.tok("PEs"), P.tok("ACTs"), P.tok("DVEs"))
        tok = P.inc(ge.dma_start(out=dbg[nm], in_=src_ap), "OUT", 16)
        for eng in (te, ve, se, sy):
            P.wait(eng, tok)

    def ddump2(nm, src_ap, col0):
        if not DEBUG:
            return
        ncol = src_ap.shape[-1]
        P.wait(ge, P.tok("PEs"), P.tok("ACTs"), P.tok("DVEs"))
        P.inc(ge.dma_start(out=dbg[nm][:, col0:col0 + ncol], in_=src_ap),
              "OUT", 16)
    shx = nc.dram_tensor("shx", [2, 128, EXW], F32, kind="Internal",
                         addr_space="Shared").ap()
    shf = nc.dram_tensor("shf", [2, 16], I32, kind="Internal",
                         addr_space="Shared").ap()
    rq_dram = nc.dram_tensor("rq_dram", [128, 8 * D], BF16,
                             kind="Internal").ap()
    u_dram = nc.dram_tensor("u_dram", [8 * 128, D], BF16,
                            kind="Internal").ap()

    stack = contextlib.ExitStack()
    P = Prog(nc)
    _dma_sems = ["Scid", "Seye", "Seyef", "Sdsel", "Sproj", "Sbq", "Sbk",
                 "Sb1", "Sbo", "Sb2", "Swk", "Swv", "Swq", "Swo", "Sw1a",
                 "Sw1b", "Sw20", "Sw21", "Sw22", "Sw23", "Srq2", "Sstage",
                 "Srow", "Sx0", "Sx1", "Sxe2", "Sexw", "Sexf", "Sprt",
                 "Srqd", "Srow1", "Sus0", "Sus1", "Su0", "Su1", "Su2", "Su3", "Su4",
                 "Su5", "Su6", "Su7"]
    for s in _dma_sems + ["PEs", "ACTs", "DVEs", "PL", "OUT"]:
        P.mksem(stack, s)

    al = Alloc()

    def sbat(name, shape, dtype, lo, hi):
        nby = shape[1] * (2 if dtype == BF16 else 4)
        off = al.place(name, nby, lo, hi)
        return nc.alloc_sbuf_tensor_at(name, shape, dtype, offset=off).ap()

    _decls = [
        ("ones_rb", [1, 128], BF16, PS, PE2),
        ("ones_rf", [1, 128], F32, PS, PX2),
        ("ones_cb", [128, 1], BF16, PS, PX2),
        ("n8192b", [1, 1], BF16, PS, PX1),
        ("eye_bf", [128, 128], BF16, PS, PE2),
        ("eye_f", [128, 128], F32, PS, PE1),
        ("dsel_bf", [128, 128], BF16, PS, PE1),
        ("projbd", [128, 512], BF16, PS, PE1),
        ("proj_f", [64, 256], F32, PS, PS),
        ("bqc", [128, 8], F32, PS, PE1),
        ("bkc", [128, 8], F32, PS, PA),
        ("b1c", [128, 32], F32, PS, PE2),
        ("bo_rb", [1, D], BF16, PS, PE1),
        ("b2_rb", [1, D], BF16, PS, PE2),
        ("srow", [1, 2176], F32, PS, PX2),
        ("stg", [128, 2 * D], F32, PS, PS),
        ("srow_bf", [1, 1088], BF16, PS, PX1),
        ("cids", [1, 1], I32, PS, PX1),
        ("mgk", [1, 16], I32, PS, PX1),
        ("flg", [1, 32], I32, PS, PX1),
        ("statc", [128, 16], F32, PS, PE2),
        ("cc4", [128, 4], F32, PS, PE2),
        ("ndg", [128, 64], F32, PS, PE1),
        ("scr64", [64, 128], F32, PS, PE1),
        ("wdg", [128, 16], F32, PS, PE1),
        ("wdg_bf", [128, 16], BF16, PS, PA),
        ("ycsb", [128, 8], BF16, PX1, PX1),
        ("alfb", [128, 16], F32, PX1, PE1),
        ("beta", [128, 16], F32, PE1, PE1),
        ("eqmax", [128, 16], F32, PE1, PE1),
        ("rd16", [128, 16], F32, PE1, PE1),
        ("lsc", [128, 8], F32, PE2, PE2),
        ("g3b", [128, D], F32, PS, PE2),
        ("be3b", [128, D], F32, PS, PE2),
        ("rsb", [128, D], BF16, PS, PE2),
        ("g2b", [128, D], F32, PS, PE1),
        ("be2b", [128, D], F32, PS, PE1),
        ("bvb", [128, D], F32, PS, PX1),
        ("rlsb", [128, D], F32, PS, PS),
        ("wk_bf", [128, 8 * D], BF16, PS, PA),
        ("wv_bf", [128, 8 * D], BF16, PS, PX1),
        ("wq_bf", [128, 8 * D], BF16, PX2, PE1),
        ("wo_bf", [128, 8 * D], BF16, PX2, PE1),
        ("rq_bf", [128, 8 * D], BF16, PS, PS),
        ("exch", [128, EXW], F32, PA, PX2),
        ("prt", [128, EXW], F32, PX1, PX1),
        ("maxac", [128, H * MF], BF16, PA, PA),
        ("xin", [128, 2 * D], F32, PS, PE1),
        ("ek", [128, H * MF], BF16, PA, PE1),
        ("kfm", [128, 8 * 512], BF16, PA, PE1),
        ("y1fm", [128, 8 * 512], BF16, PA, PE1),
        ("y1fm2", [128, 8 * 512], BF16, PA, PA),
        ("y1t_bf", [128, 2 * D], BF16, PA, PE1),
        ("vtmp", [128, D], F32, PA, PA),
        ("vpr", [128, D], BF16, PA, PA),
        ("ksq", [128, D], BF16, PA, PE1),
        ("dumpA", [128, D], BF16, PS, PE2),
        ("dumpB", [128, D], BF16, PS, PE2),
        ("ctxa0", [128, H * 65], BF16, PX2, PE1),
        ("ctxa1", [128, H * 65], BF16, PX2, PE1),
        ("stilb", [128, H * 65], F32, PX2, PE1),
        ("sepsb", [128, D], F32, PX1, PX2),
        ("y1e", [128, D], F32, PX1, PE1),
        ("nums", [128, H * 65], F32, PE1, PE1),
        ("attn_t", [128, D], BF16, PE1, PE1),
        ("attn_f", [128, D], BF16, PE1, PE1),
        ("tres", [128, D], F32, PE1, PE1),
        ("u_tm", [128, 8 * D], BF16, PE1, PE2),
        ("u_st", [128, 2 * D], BF16, PE1, PE1),
        ("ufm", [128, 8 * 256], BF16, PE2, PE2),
        ("hfm", [128, 12 * 256], BF16, PE2, PE2),
        ("uyo", [128, D], F32, PE2, PE2),
        ("x2b", [128, D], F32, PE2, PE2),
        ("qlt", [128, D], BF16, PE2, PE2),
        ("xin2", [128, D], F32, PE2, PE2),
        ("rq2", [128, 8 * D], BF16, PX1, PE2),
        ("scrB", [128, D], F32, PS, PE2),
        ("w1a", [128, 4 * DFF], BF16, PA, PE2),
        ("w1b", [128, 4 * DFF], BF16, PE2, PE2),
    ] + [(f"w2p{i}", [128, 8 * D], BF16,
          PE2, PE2)
         for i in range(4)]
    _nby = {n: sh[1] * (2 if dt == BF16 else 4)
            for (n, sh, dt, lo, hi) in _decls}
    import random as _rnd
    _offs = None
    _keys = [lambda d: (-_nby[d[0]], -(d[4] - d[3])),
             lambda d: (-(d[4] - d[3]), -_nby[d[0]]),
             lambda d: -_nby[d[0]] * (d[4] - d[3] + 1)]
    _orders = [sorted(_decls, key=k) for k in _keys]
    _r = _rnd.Random(0)
    for _try in range(3000):
        if _try < len(_orders):
            o = _orders[_try]
        elif _try % 3 == 0:
            o = list(_decls)
            _r.shuffle(o)
            o.sort(key=lambda d: -_nby[d[0]])
        elif _try % 3 == 1:
            o = list(_decls)
            _r.shuffle(o)
            o.sort(key=lambda d: -(d[4] - d[3]))
        else:
            o = list(_decls)
            _r.shuffle(o)
        a2 = Alloc()
        try:
            cand = {}
            for (n, sh, dt, lo, hi) in o:
                cand[n] = a2.place(n, _nby[n], lo, hi)
            _offs = cand
            break
        except AssertionError:
            continue
    assert _offs is not None, "no SBUF packing found"
    import os as _os
    if _os.environ.get("ALLOC_DEBUG"):
        for (n, sh, dt, lo, hi) in sorted(_decls, key=lambda d: _offs[d[0]]):
            print(f"{_offs[n]:7d} +{_nby[n]:6d} -> {_offs[n]+_nby[n]:7d}  "
                  f"[{lo},{hi}] {n}")
    _T = {}
    for (n, sh, dt, lo, hi) in _decls:
        _T[n] = nc.alloc_sbuf_tensor_at(n, sh, dt, offset=_offs[n]).ap()
    ones_rb = _T["ones_rb"]
    ones_rf = _T["ones_rf"]
    ones_cb = _T["ones_cb"]
    n8192b = _T["n8192b"]
    eye_bf = _T["eye_bf"]
    eye_f = _T["eye_f"]
    dsel_bf = _T["dsel_bf"]
    projbd = _T["projbd"]
    proj_f = _T["proj_f"]
    bqc = _T["bqc"]
    bkc = _T["bkc"]
    b1c = _T["b1c"]
    bo_rb = _T["bo_rb"]
    b2_rb = _T["b2_rb"]
    srow = _T["srow"]
    stg = _T["stg"]
    srow_bf = _T["srow_bf"]
    cids = _T["cids"]
    mgk = _T["mgk"]
    flg = _T["flg"]
    statc = _T["statc"]
    cc4 = _T["cc4"]
    ndg = _T["ndg"]
    scr64 = _T["scr64"]
    wdg = _T["wdg"]
    wdg_bf = _T["wdg_bf"]
    ycsb = _T["ycsb"]
    alfb = _T["alfb"]
    beta = _T["beta"]
    eqmax = _T["eqmax"]
    rd16 = _T["rd16"]
    lsc = _T["lsc"]
    g3b = _T["g3b"]
    be3b = _T["be3b"]
    rsb = _T["rsb"]
    g2b = _T["g2b"]
    be2b = _T["be2b"]
    bvb = _T["bvb"]
    rlsb = _T["rlsb"]
    wk_bf = _T["wk_bf"]
    wv_bf = _T["wv_bf"]
    wq_bf = _T["wq_bf"]
    wo_bf = _T["wo_bf"]
    rq_bf = _T["rq_bf"]
    exch = _T["exch"]
    prt = _T["prt"]
    maxac = _T["maxac"]
    xin = _T["xin"]
    ek = _T["ek"]
    kfm = _T["kfm"]
    y1fm = _T["y1fm"]
    y1fm2 = _T["y1fm2"]
    y1t_bf = _T["y1t_bf"]
    vtmp = _T["vtmp"]
    vpr = _T["vpr"]
    ksq = _T["ksq"]
    dumpA = _T["dumpA"]
    dumpB = _T["dumpB"]
    eqm = _T["kfm"]
    ctxa0 = _T["ctxa0"]
    ctxa1 = _T["ctxa1"]
    stilb = _T["stilb"]
    sepsb = _T["sepsb"]
    qfm = _T["y1fm"][:, 0:2048]
    y1fe = _T["y1fm"][:, 2048:4096]
    y1e = _T["y1e"]
    nums = _T["nums"]
    attn_t = _T["attn_t"]
    attn_f = _T["attn_f"]
    tres = _T["tres"]
    u_tm = _T["u_tm"]
    u_st = _T["u_st"]
    ufm = _T["ufm"]
    hfm = _T["hfm"]
    uyo = _T["uyo"]
    x2b = _T["x2b"]
    qlt = _T["qlt"]
    qlf = _T["qlt"]
    xin2 = _T["xin2"]
    rq2 = _T["rq2"]
    scrB = _T["scrB"]
    w1a = _T["w1a"]
    uffb = uyo
    y2q = uyo
    outb = uyo
    scrA = _T["scrB"]
    w1b = _T["w1b"]
    w2p = [_T[f"w2p{i}"] for i in range(4)]



    pctx = nc.alloc_psum_tensor("pctx", [128, 2048], F32).ap()
    pmisc = nc.alloc_psum_tensor("pmisc", [128, 512], F32).ap()
    ptr = nc.alloc_psum_tensor("ptr", [128, 1024], BF16).ap()
    ptA = nc.alloc_psum_tensor("ptA", [128, 512], F32).ap()
    ptB = nc.alloc_psum_tensor("ptB", [128, 512], F32).ap()

    g_ctx, g_misc, g_tr, g_A, g_B = PG(P), PG(P), PG(P), PG(P), PG(P)
    g_ctxb = PG(P)
    g_ctx_all = PG2(g_ctx, g_ctxb)

    te, ve, se, ge, sy = nc.tensor, nc.vector, nc.scalar, nc.gpsimd, nc.sync

    def vsync(ins):
        t = P.inc(ins, "DVEs", 1)
        P.wait(ve, t)
        return t

    def asynca(ins):
        t = P.inc(ins, "ACTs", 1)
        P.wait(se, t)
        return t


    t_cid = P.inc(sy.dma_start(out=cids[:], in_=cid), "Scid", 16)

    t_eyeb = P.inc(ge.dma_start(out=eye_bf[:], in_=eye_d), "Seye", 16)
    t_eyef = P.inc(sy.dma_start(out=eye_f[:], in_=eye_d), "Seyef", 16)
    t_dsel = P.inc(ge.dma_start(out=dsel_bf[:], in_=dsel_d), "Sdsel", 16)
    t_proj = P.inc(sy.dma_start(out=proj_f[:], in_=projTd), "Sproj", 16)
    t_bq = P.inc(sy.dma_start(out=bqc[:], in_=bq128), "Sbq", 16)
    t_bk = P.inc(sy.dma_start(out=bkc[:], in_=bk128), "Sbk", 16)
    t_b1 = P.inc(sy.dma_start(out=b1c[:], in_=b1fm_d), "Sb1", 16)
    t_bo = P.inc(ge.dma_start(out=bo_rb[:], in_=bo_row), "Sbo", 16)
    t_b2 = P.inc(ge.dma_start(out=b2_rb[:], in_=b2_row), "Sb2", 16)

    def wload(dst, src, nch, ncol, sem):
        tt = []
        for c in range(nch):
            tt.append(P.inc(ge.dma_start(
                out=dst[:, c * ncol:(c + 1) * ncol],
                in_=src[c * 128:(c + 1) * 128, :]), sem, 16))
        return tt

    tw_k = wload(wk_bf, wkT, 8, D, "Swk")
    tw_v = wload(wv_bf, wvT, 8, D, "Swv")

    P.wait(ve, t_proj)
    ve.memset(cc4[:, 0:1], LN_EPS)
    ve.memset(cc4[:, 1:2], LNRATIO)
    ve.memset(cc4[:, 2:3], LNEPS)
    ve.memset(cc4[:, 3:4], -1.0)
    i = ve.memset(ones_rb[:], 1.0)
    t_cc4 = P.inc(i, "DVEs", 1)
    ve.memset(ones_rf[:], 1.0)
    ve.memset(ones_cb[:], 1.0)
    ve.memset(n8192b[:], 8192.0)
    ve.memset(mgk[:], FLAG_MAGIC)
    ve.memset(scr64[:], 0.0)
    ve.memset(ndg[:], 0.0)
    ve.memset(projbd[:], 0.0)
    ve.tensor_scalar(projbd[0:64, 0:256], proj_f[:], NORM, None, ALU.mult)
    i = ve.tensor_scalar(projbd[64:128, 256:512], proj_f[:], NORM, None,
                         ALU.mult)
    t_pbd = P.inc(i, "DVEs", 1)

    def row_bcast_f32(row_ap, dst, tok_row):
        g_A.pe(te)
        g_B.pe(te)
        P.wait(te, tok_row)
        te.matmul(ptA[:, 0:512], ones_rf[:], row_ap[0:1, 0:512],
                  start=True, stop=True)
        m = te.matmul(ptB[:, 0:512], ones_rf[:], row_ap[0:1, 512:1024],
                      start=True, stop=True)
        tm = P.inc(m, "PEs", 1)
        g_A.set_pe(tm)
        g_B.set_pe(tm)
        g_A.rd(ve)
        g_B.rd(ve)
        ve.tensor_copy(dst[:, 0:512], ptA[:, 0:512])
        c = ve.tensor_copy(dst[:, 512:1024], ptB[:, 0:512])
        tc = P.inc(c, "DVEs", 1)
        g_A.set_rd(tc)
        g_B.set_rd(tc)
        return tc

    _rb_state = {"n": 0, "pe": [None, None]}

    def load_row_bcast(src_dram, dst):
        sl = _rb_state["n"] % 2
        _rb_state["n"] += 1
        P.wait(sy, _rb_state["pe"][sl])
        tr = P.inc(sy.dma_start(out=srow[0:1, sl * 1088:sl * 1088 + D],
                                in_=src_dram),
                   "Srow" if sl == 0 else "Srow1", 16)
        tc = row_bcast_f32(srow[0:1, sl * 1088:sl * 1088 + D], dst, tr)
        _rb_state["pe"][sl] = g_A.last_pe
        return tc

    t_bvb = load_row_bcast(bv_row, bvb)
    t_g2b = load_row_bcast(g2_row, g2b)
    t_be2b = load_row_bcast(be2_row, be2b)
    t_g3b = load_row_bcast(g3_row, g3b)
    t_be3b = load_row_bcast(be3_row, be3b)
    if DEBUG:
        ddump("d_g3p0", g3b[:, 0:D])

    stg_rd = [None, None]
    for c in range(8):
        sl = c % 2
        P.wait(sy, stg_rd[sl])
        tr = P.inc(sy.dma_start(out=stg[:, sl * D:(sl + 1) * D],
                                in_=rhsT_d[c * 128:(c + 1) * 128, :]),
                   "Sstage" if sl == 0 else "Sxe2", 16)
        P.wait(ve, tr)
        r = ve.reduce_max(statc[:, c:c + 1], stg[:, sl * D:(sl + 1) * D],
                          axis=AXX, apply_absolute_value=True)
        stg_rd[sl] = P.inc(r, "DVEs", 1)
    P.wait(ve, stg_rd[0], stg_rd[1])
    vsync(ve.tensor_scalar(wdg[:, 0:8], statc[:, 0:8], 1e-12, 1.0 / 127.0,
                           ALU.max, ALU.mult))
    vsync(ve.tensor_scalar(scrB[:, 0:8], statc[:, 0:8], 1e-12, None, ALU.max))
    vsync(ve.reciprocal(scrB[:, 8:16], scrB[:, 0:8]))
    i = ve.tensor_scalar(wdg[:, 8:16], scrB[:, 8:16], 127.0, None, ALU.mult)
    t_rls = P.inc(i, "DVEs", 1)

    g_A.pe(te)
    g_B.pe(te)
    P.wait(te, t_rls, t_eyef)
    for c in range(8):
        pt = ptA if c < 4 else ptB
        m = te.matmul(pt[0:1, (c % 4) * 128:(c % 4 + 1) * 128],
                      wdg[:, c:c + 1], eye_f[:], start=True, stop=True)
    t_rows = P.inc(m, "PEs", 1)
    g_A.set_pe(t_rows)
    g_B.set_pe(t_rows)
    g_A.rd(ve)
    g_B.rd(ve)
    ve.tensor_copy(srow[0:1, 0:512], ptA[0:1, 0:512])
    c = ve.tensor_copy(srow[0:1, 512:1024], ptB[0:1, 0:512])
    t_rows2 = P.inc(c, "DVEs", 1)
    g_A.set_rd(t_rows2)
    g_B.set_rd(t_rows2)
    t_rsb = row_bcast_f32(srow, rsb, t_rows2)
    g_A.pe(te)
    g_B.pe(te)
    for c in range(8):
        pt = ptA if c < 4 else ptB
        m = te.matmul(pt[0:1, (c % 4) * 128:(c % 4 + 1) * 128],
                      wdg[:, 8 + c:9 + c], eye_f[:], start=True, stop=True)
    t_rows = P.inc(m, "PEs", 1)
    g_A.set_pe(t_rows)
    g_B.set_pe(t_rows)
    g_A.rd(ve)
    g_B.rd(ve)
    ve.tensor_copy(scrB[0:1, 0:512], ptA[0:1, 0:512])
    c = ve.tensor_copy(scrB[0:1, 512:1024], ptB[0:1, 0:512])
    t_rows2 = P.inc(c, "DVEs", 1)
    g_A.set_rd(t_rows2)
    g_B.set_rd(t_rows2)
    t_rlsb = row_bcast_f32(scrB, rlsb, t_rows2)

    for c in range(8):
        sl = c % 2
        P.wait(sy, stg_rd[sl])
        tr = P.inc(sy.dma_start(out=stg[:, sl * D:(sl + 1) * D],
                                in_=rhs_d[c * 128:(c + 1) * 128, :]),
                   "Sstage" if sl == 0 else "Sxe2", 16)
        P.wait(ve, tr, t_rlsb if c == 0 else None)
        ve.tensor_mul(scrB[:, 0:D], stg[:, sl * D:(sl + 1) * D],
                      rlsb[:, 0:D])
        ve.tensor_scalar(scrB[:, 0:D], scrB[:, 0:D], RMAGIC, None, ALU.add)
        i = ve.tensor_scalar(rq_bf[:, c * D:(c + 1) * D], scrB[:, 0:D],
                             -RMAGIC, None, ALU.add)
        stg_rd[sl] = P.inc(i, "DVEs", 1)
    P.wait(sy, stg_rd[0], stg_rd[1])
    t_rqd = P.inc(sy.dma_start(out=rq_dram, in_=rq_bf[:]), "Srqd", 16)

    def stats_ln(x_ap, out_ap, out_is_bf, prev_act_tok, act_guards=(),
                 dve_guards=()):
        P.wait(ve, *dve_guards)
        i = ve.tensor_scalar(dumpA[:, 0:D], x_ap, 0.0, None, ALU.add,
                             ALU.add, accum_out=statc[:, 0:1])
        tsum = vsync(i)
        P.wait(se, prev_act_tok, *act_guards)
        i = se.activation(dumpB[:, 0:D], x_ap, AF.Square,
                          accum_out=statc[:, 1:2])
        tssq = P.inc(i, "ACTs", 1)
        P.wait(ve, tssq)
        vsync(ve.tensor_scalar(statc[:, 2:3], statc[:, 0:1], 1.0 / D, None,
                               ALU.mult))
        vsync(ve.tensor_scalar(statc[:, 3:4], statc[:, 1:2], 1.0 / D, None,
                               ALU.mult))
        vsync(ve.tensor_mul(statc[:, 4:5], statc[:, 2:3], statc[:, 2:3]))
        i = ve.tensor_sub(statc[:, 5:6], statc[:, 3:4], statc[:, 4:5])
        tvar = vsync(i)
        P.wait(se, tvar)
        i = se.activation(statc[:, 6:7], statc[:, 5:6], AF.Sqrt,
                          bias=cc4[:, 0:1])
        tsd = P.inc(i, "ACTs", 1)
        P.wait(ve, tsd)
        vsync(ve.reciprocal(statc[:, 7:8], statc[:, 6:7]))
        i = ve.tensor_scalar(statc[:, 8:9], statc[:, 2:3], statc[:, 7:8],
                             -1.0, ALU.mult, ALU.mult)
        tnm = P.inc(i, "DVEs", 1)
        P.wait(se, tnm)
        i = se.activation(out_ap, x_ap, AF.Identity, bias=statc[:, 8:9],
                          scale=statc[:, 7:8])
        return P.inc(i, "ACTs", 1)

    xin_rd = [None, None]
    y1t_rd = [None, None]
    apply_tok = None
    ycs_guard = None
    g_dg = None
    g_s64 = None
    g_vpr = None
    g_wdg = None
    g_vprD = None
    g_mx = None
    g_kfm = None
    g_ndg = None

    P.wait(ge, t_rqd, t_pbd)
    tw1a = wload(w1a, w1T, 4, DFF, "Sw1a")

    tev_q = [None] * 8

    def pass1_tile(t, ybuf):
        nonlocal apply_tok
        qd, qs, bf = t // 4, t % 4, t % 2
        qd, qs, bf = t // 4, t % 4, t % 2
        P.wait(sy, xin_rd[bf])
        tx = P.inc(sy.dma_start(out=xin[:, bf * D:(bf + 1) * D],
                                in_=xh[t * 128:(t + 1) * 128, :]),
                   "Sx0" if bf == 0 else "Sx1", 16)
        P.wait(ve, tx)
        P.wait(se, tx)
        apply_tok = stats_ln(xin[:, bf * D:(bf + 1) * D],
                             y1t_bf[:, bf * D:(bf + 1) * D], True, apply_tok,
                             act_guards=(y1t_rd[bf],),
                             dve_guards=(apply_tok,))
        xin_rd[bf] = apply_tok
        g_tr.pe(te)
        P.wait(te, apply_tok)
        for c in range(8):
            mtr = te.transpose(ptr[:, c * 128:(c + 1) * 128],
                               y1t_bf[:, bf * D + c * 128:bf * D + (c + 1) * 128],
                               eye_bf[:])
        ttr = P.inc(mtr, "PEs", 1)
        g_tr.set_pe(ttr)
        g_tr.rd(ve)
        dst = strided(ybuf, 128, 512, 8, 128, qs * 128)
        i = ve.tensor_copy(dst, ptr[:].rearrange("p (c f) -> p c f", f=128))
        tev = P.inc(i, "DVEs", 1)
        g_tr.set_rd(tev)
        tev_q[qd] = tev
        g_misc.pe(te)
        P.wait(te, apply_tok, t_bk if t == 0 else None)
        for c in range(8):
            mm = te.matmul(pmisc[:, 32 + c:33 + c],
                           y1t_bf[:, bf * D + c * 128:bf * D + (c + 1) * 128],
                           ones_cb[:], start=(t == 0 and c == 0),
                           stop=False)
        t_ycs = P.inc(mm, "PEs", 1)
        g_misc.set_pe(t_ycs)
        y1t_rd[bf] = t_ycs

        if DEBUG and t == 0:
            ddump("d_xin", xin[:, 0:D])
            ddump("d_stat", statc[:])
            ddump("d_dmpA", dumpA[:, 0:D])
            ddump("d_dmpB", dumpB[:, 0:D])
            ddump("d_y1", y1t_bf[:, 0:D])
    for t in range(4):
        pass1_tile(t, y1fm)
    for qd in range(8):
        ybuf = y1fm if qd % 2 == 0 else y1fm2
        P.wait(te, tev_q[qd], *([tw_k[-1]] if qd == 0 else []))
        for oc in range(8):
            pt, g = (ptA, g_A) if oc % 2 == 0 else (ptB, g_B)
            g.pe(te)
            for kc in range(8):
                mm = te.matmul(
                    pt[:, 0:512],
                    wk_bf[:, kc * D + oc * 128:kc * D + (oc + 1) * 128],
                    ybuf[:, kc * 512:(kc + 1) * 512],
                    start=(kc == 0), stop=(kc == 7))
            tk = P.inc(mm, "PEs", 1)
            g.set_pe(tk)
            g.rd(se)
            P.wait(se, g_kfm)
            i = se.activation(kfm[:, oc * 512:(oc + 1) * 512], pt[:, 0:512],
                              AF.Identity, bias=bkc[:, oc:oc + 1])
            tke = P.inc(i, "ACTs", 1)
            g.set_rd(tke)
        t_kfm = tke

        if DEBUG and qd == 0:
            ddump("d_kfm", kfm[:, 0:512])
        if qd < 7:
            for t in range(4 * qd + 4, 4 * qd + 8):
                pass1_tile(t, y1fm2 if qd % 2 == 0 else y1fm)
        for qs3 in range(4):
            t3 = qd * 4 + qs3
            src = strided(kfm, 128, 512, 8, 128, qs3 * 128)
            P.wait(ve, t_kfm, g_dg)
            i = ve.tensor_tensor(ksq[:, 0:D], src, src, ALU.mult)
            t_ksq = P.inc(i, "DVEs", 1)
            g_kfm = t_ksq
            g_A.pe(te)
            P.wait(te, t_ksq)
            for c in range(8):
                mm = te.matmul(ptA[:, 0:16],
                               ksq[:, c * 128:(c + 1) * 128],
                               dsel_bf[:, c * 16:(c + 1) * 16],
                               start=(c == 0), stop=(c == 7))
            t_dg = P.inc(mm, "PEs", 1)
            g_dg = t_dg
            g_A.set_pe(t_dg)
            g_A.rd(se)
            P.wait(se, g_vprD)
            i = se.activation(wdg[:, 0:16], ptA[:, 0:16], AF.Exp)
            t_wdg = P.inc(i, "ACTs", 1)
            g_A.set_rd(t_wdg)
            P.wait(ve, t_wdg, g_wdg)
            i = ve.tensor_copy(wdg_bf[:], wdg[:])
            t_wdgb = P.inc(i, "DVEs", 1)

            for og in range(2):
                pt, g = (ptA, g_A) if og == 0 else (ptB, g_B)
                g.pe(te)
                if t3 == 0:
                    P.wait(te, tw_v[-1], t_bvb)
                for kc in range(8):
                    mm = te.matmul(
                        pt[:, 0:512],
                        ybuf[:, kc * 512 + qs3 * 128:kc * 512 + qs3 * 128 + 128],
                        wv_bf[:, kc * D + og * 512:kc * D + og * 512 + 512],
                        start=(kc == 0), stop=(kc == 7))
                tv = P.inc(mm, "PEs", 1)
                g.set_pe(tv)
                g.rd(ve)
                i = ve.scalar_tensor_tensor(
                    vtmp[:, og * 512:(og + 1) * 512], pt[:, 0:512], 0.0,
                    bvb[:, og * 512:(og + 1) * 512], ALU.add, ALU.add)
                tve = P.inc(i, "DVEs", 1)
                g.set_rd(tve)
            P.wait(ve, g_vpr)
            i = ve.tensor_tensor(vpr[:, 0:D], vtmp[:, 0:D],
                                 bcast(wdg, 16, 64), ALU.mult)
            t_vpr = P.inc(i, "DVEs", 1)
            g_vprD = t_vpr

            t_ek = None
            for p in range(8):
                pt, g = (ptA, g_A) if p % 2 == 0 else (ptB, g_B)
                g.pe(te)
                P.wait(te, t_pbd if t3 == 0 else None)
                mm = te.matmul(
                    pt[:, 0:512],
                    kfm[:, p * 512 + qs3 * 128:p * 512 + qs3 * 128 + 128],
                    projbd[:], start=True, stop=True)
                txp = P.inc(mm, "PEs", 1)
                g.set_pe(txp)
                g.rd(se)
                P.wait(se, g_mx)
                i = se.activation(ek[:, p * 512:(p + 1) * 512],
                                  pt[:, 0:512], AF.Exp)
                t_ek = P.inc(i, "ACTs", 1)
                g.set_rd(t_ek)
                P.wait(ve, t_ek)
                if t3 == 0:
                    i = ve.tensor_copy(maxac[:, p * 512:(p + 1) * 512],
                                       ek[:, p * 512:(p + 1) * 512])
                else:
                    i = ve.tensor_max(maxac[:, p * 512:(p + 1) * 512],
                                      maxac[:, p * 512:(p + 1) * 512],
                                      ek[:, p * 512:(p + 1) * 512])
                t_mx = P.inc(i, "DVEs", 1)

            if DEBUG and t3 == 0:
                ddump("d_ek", ek[:, 0:512])
                ddump("d_wdg", wdg[:, 0:16])
            g_mx = t_mx

            g_ctx_all.pe(te)
            P.wait(te, t_ek, t_vpr)
            for h in range(H):
                for j in range(2):
                    mm = te.matmul(
                        pctx[:, h * 128 + j * 64:h * 128 + j * 64 + 64],
                        ek[:, h * 256 + j * 128:h * 256 + j * 128 + 128],
                        vpr[:, h * 64:(h + 1) * 64],
                        start=(qs3 == 0 and h % 4 == 0 and j == 0),
                        stop=(qs3 == 3 and h % 4 == 3 and j == 1))
            t_ctx = P.inc(mm, "PEs", 1)
            g_vpr = t_ctx
            g_ctx_all.set_pe(t_ctx)

            g_misc.pe(te)
            P.wait(te, t_wdgb)
            for h in range(H):
                for j in range(2):
                    mm = te.matmul(
                        pmisc[:, h * 2 + j:h * 2 + j + 1],
                        ek[:, h * 256 + j * 128:h * 256 + j * 128 + 128],
                        wdg_bf[:, h:h + 1],
                        start=False,
                        stop=(t3 == ATILES - 1 and h == H - 1 and j == 1))
            t_ks = P.inc(mm, "PEs", 1)
            g_wdg = t_ks
            g_misc.set_pe(t_ks)

        g_ctx_all.rd(ve)
        if qd == 0:
            i = ve.tensor_copy(exch[:, 0:2048], pctx[:])
        else:
            i = ve.tensor_add(exch[:, 0:2048], exch[:, 0:2048], pctx[:])
        t_ctxev = P.inc(i, "DVEs", 1)
        g_ctx_all.set_rd(t_ctxev)

    g_misc.rd(ve)
    ve.tensor_copy(exch[:, 2048:2080], pmisc[:, 0:32])
    i = ve.tensor_copy(exch[:, 2080:2088], pmisc[:, 32:40])
    t_m1 = P.inc(i, "DVEs", 1)
    g_misc.set_rd(t_m1)
    i = ve.reduce_max(exch[:, 2088:2104],
                      bass.AP(maxac.tensor, maxac.offset,
                              [list(maxac.ap[0]), [MF, H], [1, MF]]),
                      axis=AXX)
    t_aend = P.inc(i, "DVEs", 1)

    pe_endA = P.tok("PEs")
    act_endA = P.tok("ACTs")
    if DEBUG:
        ddump("d_exch", exch[:])
        ddump("d_g3p1", g3b[:, 0:D])
    apply_tok2 = apply_tok
    g_y1e = None
    prev_tu = None
    e1_tev = [None] * 8

    def e1_pass1(pp):
        nonlocal apply_tok2, g_y1e, prev_tu
        for j2 in range(2):
            t1 = pp * 2 + j2
            bf = t1 % 2
            P.wait(sy, xin_rd[bf])
            tx = P.inc(sy.dma_start(out=xin[:, bf * D:(bf + 1) * D],
                                    in_=xe[t1 * 128:(t1 + 1) * 128, :]),
                       "Sx0" if bf == 0 else "Sx1", 16)
            P.wait(ve, tx)
            P.wait(se, tx)
            apply_tok2 = stats_ln(xin[:, bf * D:(bf + 1) * D],
                                  y1t_bf[:, bf * D:(bf + 1) * D], True,
                                  apply_tok2,
                                  act_guards=(y1t_rd[bf], prev_tu),
                                  dve_guards=(apply_tok2,))
            P.wait(se, g_y1e)
            i = se.activation(y1e[:, 0:D] if j2 == 0 else scrB[:, 0:D],
                              xin[:, bf * D:(bf + 1) * D], AF.Identity,
                              bias=statc[:, 8:9], scale=statc[:, 7:8])
            apply_tok2 = P.inc(i, "ACTs", 1)
            xin_rd[bf] = apply_tok2
            g_tr.pe(te)
            P.wait(te, apply_tok2)
            for c in range(8):
                mtr = te.transpose(
                    ptr[:, c * 128:(c + 1) * 128],
                    y1t_bf[:, bf * D + c * 128:bf * D + (c + 1) * 128],
                    eye_bf[:])
            ttr = P.inc(mtr, "PEs", 1)
            g_tr.set_pe(ttr)
            g_tr.rd(ve)
            dst = strided(y1fe, 128, 256, 8, 128, j2 * 128)
            i = ve.tensor_copy(dst, ptr[:].rearrange("p (c f) -> p c f",
                                                     f=128))
            tev = P.inc(i, "DVEs", 1)
            g_tr.set_rd(tev)
            y1t_rd[bf] = tev
            e1_tev[pp] = tev


    e1_pass1(0)

    P.wait(sy, t_aend, t_ctxev, t_cid)
    me_reg = sy.to_reg(sy.value_load(cids[0:1, 0:1]))
    with sy.If_eq(me_reg, 0):
        sy.dma_start(out=shx[0, :, :], in_=exch[:]).then_inc(P.sem("Sexw"), 16)
    with sy.Else():
        sy.dma_start(out=shx[1, :, :], in_=exch[:]).then_inc(P.sem("Sexw"), 16)
    P.cnt["Sexw"][1] += 16
    P.wait(sy, P.tok("Sexw"))
    with sy.If_eq(me_reg, 0):
        sy.dma_start(out=shf[0:1, :], in_=mgk[:]).then_inc(P.sem("Sexf"), 16)
    with sy.Else():
        sy.dma_start(out=shf[1:2, :], in_=mgk[:]).then_inc(P.sem("Sexf"), 16)
    P.cnt["Sexf"][1] += 16
    P.wait(sy, P.tok("Sexf"))

    pcnt = nc.alloc_registers("pcnt", engines=[sy.engine])
    nc.regs_mov(pcnt, 0)
    nc.br("spin_loop", engines=[sy.engine])
    with nc.body("spin_loop", valid_engines=[sy.engine]):
        sy.dma_start(out=flg[0:1, 0:32],
                     in_=shf.rearrange("a b -> (a b)").rearrange(
                         "(o c) -> o c", o=1)).then_inc(P.sem("PL"), 16)
        nc.regs_alu(pcnt, pcnt, 16, op=ALU.add)
        iv = nc.snap(pcnt, engines=[sy.engine], min_val=16)
        sy.wait_ge(P.sem("PL"), iv)
        v0 = sy.value_load(flg[0:1, 0:1])
        v1 = sy.value_load(flg[0:1, 16:17])
        vm = nc.alloc_registers("vm", engines=[sy.engine])
        nc.regs_mov(vm, v0)
        nc.regs_alu(vm, vm, v1, op=ALU.min)
        vmv = nc.snap(vm, engines=[sy.engine])
        nc.br_eq(vmv, FLAG_MAGIC, on_true="spin_end", on_false="spin_loop",
                 engines=[sy.engine])
    nc.switch_bb("spin_end")

    sy.dma_start(out=exch[:], in_=shx[0, :, :]).then_inc(P.sem("Sprt"), 16)
    tp = P.inc(sy.dma_start(out=prt[:], in_=shx[1, :, :]), "Sprt", 16)
    P.cnt["Sprt"][1] += 16
    tp = P.tok("Sprt")
    P.wait(ve, tp)
    ve.tensor_add(exch[:, 0:2088], exch[:, 0:2088], prt[:, 0:2088])
    i = ve.tensor_max(exch[:, 2088:2104], exch[:, 2088:2104],
                      prt[:, 2088:2104])
    t_comb = P.inc(i, "DVEs", 1)


    if DEBUG:
        ddump("d_comb", exch[:])
        ddump("d_g3p2", g3b[:, 0:D])
    g_A.pe(te)
    P.wait(te, t_comb)
    mm = te.transpose(ptA[0:16, 0:128], exch[:, 2088:2104], eye_f[:])
    t_kt = P.inc(mm, "PEs", 1)
    g_A.set_pe(t_kt)
    g_A.rd(ve)
    i = ve.reduce_max(statc[0:16, 9:10], ptA[0:16, 0:128], axis=AXX)
    t_km = vsync(i)
    g_A.set_rd(t_km)
    P.wait(se, t_km)
    asynca(se.activation(statc[0:16, 10:11], statc[0:16, 9:10], AF.Ln))
    i = se.activation(statc[0:16, 11:12], statc[0:16, 10:11], AF.Exp,
                      bias=cc4[0:16, 1:2], scale=cc4[0:16, 3:4])
    t_alf = P.inc(i, "ACTs", 1)
    g_B.pe(te)
    P.wait(te, t_alf)
    mm = te.matmul(ptB[0:1, 0:16], statc[0:16, 11:12], eye_f[0:16, 0:16],
                   start=True, stop=True)
    t_ar = P.inc(mm, "PEs", 1)
    g_B.set_pe(t_ar)
    g_B.rd(ve)
    i = ve.tensor_copy(srow[0:1, 1024:1040], ptB[0:1, 0:16])
    t_arw = P.inc(i, "DVEs", 1)
    g_B.set_rd(t_arw)
    g_B.pe(te)
    P.wait(te, t_arw)
    mm = te.matmul(ptB[:, 0:16], ones_rf[:], srow[0:1, 1024:1040],
                   start=True, stop=True)
    t_ab = P.inc(mm, "PEs", 1)
    g_B.set_pe(t_ab)
    g_B.rd(ve)
    i = ve.tensor_copy(alfb[:], ptB[:, 0:16])
    t_alfb = P.inc(i, "DVEs", 1)
    g_B.set_rd(t_alfb)

    P.wait(ve, t_comb)
    ve.tensor_copy(ycsb[:], exch[:, 2080:2088])
    i = ve.tensor_copy(srow_bf[0:1, 0:D], bvb[0:1, 0:D])
    t_ycsb = P.inc(i, "DVEs", 1)
    g_A.pe(te)
    g_B.pe(te)
    P.wait(te, t_ycsb)
    for c in range(8):
        for og in range(2):
            pt = ptA if og == 0 else ptB
            mm = te.matmul(pt[0:1, 0:512], ycsb[:, c:c + 1],
                           wv_bf[:, c * D + og * 512:c * D + og * 512 + 512],
                           start=(c == 0), stop=False)
    for og in range(2):
        pt = ptA if og == 0 else ptB
        mm = te.matmul(pt[0:1, 0:512], n8192b[:],
                       srow_bf[0:1, og * 512:og * 512 + 512],
                       start=False, stop=True)
    t_s = P.inc(mm, "PEs", 1)
    g_A.set_pe(t_s)
    g_B.set_pe(t_s)
    g_A.rd(ve)
    g_B.rd(ve)
    ve.tensor_scalar(srow[0:1, 0:512], ptA[0:1, 0:512], REPS, None, ALU.mult)
    i = ve.tensor_scalar(srow[0:1, 512:1024], ptB[0:1, 0:512], REPS, None,
                         ALU.mult)
    t_sr = P.inc(i, "DVEs", 1)
    g_A.set_rd(t_sr)
    g_B.set_rd(t_sr)
    t_seps = row_bcast_f32(srow, sepsb, t_sr)

    ctxa = [ctxa0, ctxa1]
    t_ctxa = None
    P.wait(ve, t_comb, t_alfb, t_seps)
    for h in range(H):
        for j in range(2):
            i = ve.scalar_tensor_tensor(
                ctxa[j][:, h * 65:h * 65 + 64],
                exch[:, h * 128 + j * 64:h * 128 + j * 64 + 64],
                alfb[:, h:h + 1],
                sepsb[:, h * 64:(h + 1) * 64], ALU.mult, ALU.add)
    t_ctxa = P.inc(i, "DVEs", 1)
    for h in range(H):
        for j in range(2):
            i = ve.tensor_scalar(ctxa[j][:, h * 65 + 64:h * 65 + 65],
                                 exch[:, 2048 + h * 2 + j:2049 + h * 2 + j],
                                 alfb[:, h:h + 1], REPS * 8192.0,
                                 ALU.mult, ALU.add)
    t_ctxa = P.inc(i, "DVEs", 1)

    if DEBUG:
        ddump2("d_ctxa", ctxa0[:], 0)
        ddump2("d_ctxa", ctxa1[:], H * 65)
    P.wait(ge, t_ctxa, P.tok("PEs"), pe_endA, act_endA, t_aend, t_comb,
           t_rqd)
    tw_q = wload(wq_bf, wqT, 8, D, "Swq")
    tw_o = wload(wo_bf, woT, 8, D, "Swo")
    t_rq2 = P.inc(ge.dma_start(out=rq2[:], in_=rq_dram), "Srq2", 16)
    tw2 = []

    g_A.pe(te)
    g_B.pe(te)
    g_misc.pe(te)
    P.wait(te, t_ctxa)
    for j in range(2):
        te.matmul(ptA[0:1, 0:512], ones_cb[:], ctxa[j][:, 0:512],
                  start=(j == 0), stop=(j == 1))
        te.matmul(ptB[0:1, 0:512], ones_cb[:], ctxa[j][:, 512:1024],
                  start=(j == 0), stop=(j == 1))
        mm = te.matmul(pmisc[0:1, 0:16], ones_cb[:], ctxa[j][:, 1024:1040],
                       start=(j == 0), stop=(j == 1))
    t_st = P.inc(mm, "PEs", 1)
    g_A.set_pe(t_st)
    g_B.set_pe(t_st)
    g_misc.set_pe(t_st)
    g_A.rd(ve)
    g_B.rd(ve)
    g_misc.rd(ve)
    ve.tensor_copy(srow[0:1, 0:512], ptA[0:1, 0:512])
    ve.tensor_copy(srow[0:1, 512:1024], ptB[0:1, 0:512])
    i = ve.tensor_copy(srow[0:1, 1024:1040], pmisc[0:1, 0:16])
    t_str = P.inc(i, "DVEs", 1)
    g_A.set_rd(t_str)
    g_B.set_rd(t_str)
    g_misc.set_rd(t_str)
    g_A.pe(te)
    g_B.pe(te)
    g_misc.pe(te)
    P.wait(te, t_str)
    te.matmul(ptA[:, 0:512], ones_rf[:], srow[0:1, 0:512], start=True,
              stop=True)
    te.matmul(ptB[:, 0:512], ones_rf[:], srow[0:1, 512:1024], start=True,
              stop=True)
    mm = te.matmul(pmisc[:, 0:16], ones_rf[:], srow[0:1, 1024:1040],
                   start=True, stop=True)
    t_stb = P.inc(mm, "PEs", 1)
    g_A.set_pe(t_stb)
    g_B.set_pe(t_stb)
    g_misc.set_pe(t_stb)
    g_A.rd(ve)
    g_B.rd(ve)
    g_misc.rd(ve)
    ve.tensor_copy(stilb[:, 0:512], ptA[:, 0:512])
    ve.tensor_copy(stilb[:, 512:1024], ptB[:, 0:512])
    i = ve.tensor_copy(stilb[:, 1024:1040], pmisc[:, 0:16])
    t_stilb = P.inc(i, "DVEs", 1)
    g_A.set_rd(t_stilb)
    g_B.set_rd(t_stilb)
    g_misc.set_rd(t_stilb)

    if DEBUG:
        ddump("d_stil", stilb[:])
    u_done = []
    ust_rd = [None, None]
    g_qfm = None
    g_eq = None
    g_att = None
    if DEBUG:
        ddump("d_g3p3", g3b[:, 0:D])
    for pp in range(ETILES // 2):
        if DEBUG and pp == 1:
            ddump("d_g3p4", g3b[:, 0:D])
        if pp > 0:
            e1_pass1(pp)
        for oc in range(8):
            pt, g = (ptA, g_A) if oc % 2 == 0 else (ptB, g_B)
            g.pe(te)
            P.wait(te, e1_tev[pp],
                   *([tw_q[-1]] if pp == 0 else []))
            for kc in range(8):
                mm = te.matmul(
                    pt[:, 0:256],
                    wq_bf[:, kc * D + oc * 128:kc * D + (oc + 1) * 128],
                    y1fe[:, kc * 256:(kc + 1) * 256],
                    start=(kc == 0), stop=(kc == 7))
            tq = P.inc(mm, "PEs", 1)
            g.set_pe(tq)
            g.rd(se)
            P.wait(se, g_qfm)
            i = se.activation(qfm[:, oc * 256:(oc + 1) * 256], pt[:, 0:256],
                              AF.Identity, bias=bqc[:, oc:oc + 1])
            t_qfm = P.inc(i, "ACTs", 1)
            g.set_rd(t_qfm)

        for j2 in range(2):
            t1 = pp * 2 + j2
            y1res = y1e if j2 == 0 else scrB
            src = strided(qfm, 128, 256, 8, 128, j2 * 128)
            P.wait(ve, t_qfm, g_dg)
            i = ve.tensor_tensor(ksq[:, 0:D], src, src, ALU.mult)
            t_qsq = P.inc(i, "DVEs", 1)
            g_qfm = t_qsq
            g_A.pe(te)
            P.wait(te, t_qsq)
            for c in range(8):
                mm = te.matmul(ptA[:, 0:16],
                               ksq[:, c * 128:(c + 1) * 128],
                               dsel_bf[:, c * 16:(c + 1) * 16],
                               start=(c == 0), stop=(c == 7))
            t_dg = P.inc(mm, "PEs", 1)
            g_dg = t_dg
            g_A.set_pe(t_dg)
            g_A.rd(se)
            P.wait(se, g_eq)
            i = se.activation(wdg[:, 0:16], ptA[:, 0:16], AF.Exp,
                              scale=cc4[:, 3:4], bias=cc4[:, 2:3])
            t_edq = P.inc(i, "ACTs", 1)
            g_A.set_rd(t_edq)

            t_eq = None
            for p in range(8):
                pt, g = (ptA, g_A) if p % 2 == 0 else (ptB, g_B)
                g.pe(te)
                mm = te.matmul(
                    pt[:, 0:512],
                    qfm[:, p * 256 + j2 * 128:p * 256 + j2 * 128 + 128],
                    projbd[:], start=True, stop=True)
                txp = P.inc(mm, "PEs", 1)
                g.set_pe(txp)
                g.rd(se)
                P.wait(se, g_eq)
                i = se.activation(ek[:, p * 512:(p + 1) * 512], pt[:, 0:512],
                                  AF.Exp)
                t_eq = P.inc(i, "ACTs", 1)
                g.set_rd(t_eq)
                P.wait(ve, t_eq)
                i = ve.reduce_max(
                    eqmax[:, 2 * p:2 * p + 2],
                    bass.AP(ek.tensor, ek.offset + p * 512,
                            [list(ek.ap[0]), [256, 2], [1, 256]]),
                    axis=AXX)
                i = P.inc(i, "DVEs", 1)
                if p == 7:
                    P.wait(ve, i)

            if DEBUG and t1 == 0:
                ddump("d_eq", ek[:, 0:512])
                ddump("d_qfm", qfm[:, 0:512])
                ddump("d_y1fe", y1fe[:, 0:512])
                ddump("d_y1e", y1e[:, 0:D])
            P.wait(ve, t_edq)
            i2 = ve.tensor_mul(beta[:], eqmax[:], wdg[:, 0:16])
            t_beta = vsync(i2)
            g_eq = t_beta

            if DEBUG and t1 == 0:
                ddump("d_beta", beta[:])
            for r in range(4):
                g_tr.pe(te)
                P.wait(te, t_eq)
                for k in range(8):
                    hh, jj = (r * 8 + k) // 2, (r * 8 + k) % 2
                    mtr = te.transpose(
                        ptr[:, k * 128:(k + 1) * 128],
                        ek[:, hh * 256 + jj * 128:hh * 256 + jj * 128 + 128],
                        eye_bf[:])
                ttr = P.inc(mtr, "PEs", 1)
                g_tr.set_pe(ttr)
                g_tr.rd(ve)
                i = ve.tensor_copy(eqm[:, r * 1024:(r + 1) * 1024], ptr[:])
                t_eqm = P.inc(i, "DVEs", 1)
                g_tr.set_rd(t_eqm)

            g_ctx_all.pe(te)
            P.wait(te, t_eqm, t_stilb)
            for h in range(H):
                for j in range(2):
                    mm = te.matmul(
                        pctx[:, h * 128:h * 128 + 65],
                        eqm[:, (h * 2 + j) * 128:(h * 2 + j + 1) * 128],
                        ctxa[j][:, h * 65:(h + 1) * 65],
                        start=(h % 4 == 0 and j == 0),
                        stop=(h % 4 == 3 and j == 1))
            t_oa = P.inc(mm, "PEs", 1)
            g_ctx_all.set_pe(t_oa)
            g_ctx_all.rd(ve)
            P.wait(ve, t_beta)
            for h in range(H):
                i = ve.scalar_tensor_tensor(
                    nums[:, h * 65:(h + 1) * 65], stilb[:, h * 65:(h + 1) * 65],
                    beta[:, h:h + 1], pctx[:, h * 128:h * 128 + 65],
                    ALU.mult, ALU.add)
            t_num = P.inc(i, "DVEs", 1)
            g_ctx_all.set_rd(t_num)

            if DEBUG and t1 == 0:
                ddump("d_nums", nums[:])
            den_ap = bass.AP(nums.tensor, nums.offset + 64,
                             [list(nums.ap[0]), [65, 16]])
            vsync(ve.reciprocal(rd16[:], den_ap))
            P.wait(ve, g_att)
            for h in range(H):
                i = ve.tensor_scalar(attn_t[:, h * 64:(h + 1) * 64],
                                     nums[:, h * 65:h * 65 + 64],
                                     rd16[:, h:h + 1], None, ALU.mult)
            t_att = P.inc(i, "DVEs", 1)

            g_tr.pe(te)
            P.wait(te, t_att)
            for c in range(8):
                mtr = te.transpose(ptr[:, c * 128:(c + 1) * 128],
                                   attn_t[:, c * 128:(c + 1) * 128], eye_bf[:])
            ttr = P.inc(mtr, "PEs", 1)
            g_att = ttr
            g_tr.set_pe(ttr)
            g_tr.rd(ve)
            i = ve.tensor_copy(attn_f[:], ptr[:])
            t_atf = P.inc(i, "DVEs", 1)
            g_tr.set_rd(t_atf)

            for og in range(2):
                pt, g = (ptA, g_A) if og == 0 else (ptB, g_B)
                g.pe(te)
                P.wait(te, t_atf, *([tw_o[-1], t_bo] if t1 == 0 else []))
                te.matmul(pt[:, 0:512], ones_rb[:],
                          bo_rb[0:1, og * 512:og * 512 + 512],
                          start=True, stop=False)
                for c in range(8):
                    mm = te.matmul(
                        pt[:, 0:512], attn_f[:, c * 128:(c + 1) * 128],
                        wo_bf[:, c * D + og * 512:c * D + og * 512 + 512],
                        start=False, stop=(c == 7))
                t_wo = P.inc(mm, "PEs", 1)
                g.set_pe(t_wo)
                g.rd(ve)
                i = ve.scalar_tensor_tensor(
                    tres[:, og * 512:og * 512 + 512], pt[:, 0:512], 0.0,
                    y1res[:, og * 512:og * 512 + 512], ALU.add, ALU.add)
                t_tr2 = P.inc(i, "DVEs", 1)
                g.set_rd(t_tr2)
            g_y1e = t_tr2

            if DEBUG and t1 == 0:
                ddump("d_tres", tres[:, 0:D])
            apply_tok2 = stats_ln(tres[:, 0:D], dumpB[:, 0:D], True,
                                  apply_tok2,
                                  act_guards=(t_tr2, prev_tu))
            P.wait(ve, apply_tok2, t_g2b if t1 == 0 else None,
                   t_be2b if t1 == 0 else None,
                   ust_rd[t1 % 2] if t1 >= 8 else None)
            ve.scalar_tensor_tensor(dumpA[:, 0:D], dumpB[:, 0:D], 1.0,
                                    g2b[:, 0:D], ALU.mult, ALU.mult)
            udst = (u_tm[:, t1 * D:(t1 + 1) * D] if t1 < 8 else
                    u_st[:, (t1 % 2) * D:(t1 % 2 + 1) * D])
            i = ve.tensor_add(udst, dumpA[:, 0:D], be2b[:, 0:D])
            tu = P.inc(i, "DVEs", 1)
            prev_tu = tu
            if t1 >= 8:
                P.wait(ge, tu)
                ust_rd[t1 % 2] = P.inc(ge.dma_start(
                    out=u_dram[(t1 - 8) * 128:(t1 - 7) * 128, :],
                    in_=u_st[:, (t1 % 2) * D:(t1 % 2 + 1) * D]),
                    "Sus0" if t1 % 2 == 0 else "Sus1", 16)
            u_done.append(tu)
            if DEBUG and t1 == 0:
                ddump("d_u", u_tm[:, 0:D])

    pe_endE1 = P.tok("PEs")
    act_endE1 = P.tok("ACTs")
    dve_endE1 = P.tok("DVEs")
    P.wait(te, ust_rd[0], ust_rd[1])
    P.wait(ve, ust_rd[0], ust_rd[1])
    P.wait(se, ust_rd[0], ust_rd[1])
    P.wait(ge, ust_rd[0], ust_rd[1])
    P.wait(ge, pe_endE1, act_endE1, dve_endE1)
    tw1b = []
    for c in range(4):
        tw1b.append(P.inc(ge.dma_start(
            out=w1b[:, c * DFF:(c + 1) * DFF],
            in_=w1T[(4 + c) * 128:(5 + c) * 128, :]), "Sw1b", 16))
    P.wait(sy, pe_endE1, act_endE1, dve_endE1, ust_rd[0], ust_rd[1])
    for pc in range(4):
        for c in range(8):
            ch = pc * 8 + c
            tw2.append(P.inc(ge.dma_start(
                out=w2p[pc][:, c * D:(c + 1) * D],
                in_=w2T[ch * 128:(ch + 1) * 128, :]), f"Sw2{pc}", 16))

    apply_tok3 = apply_tok2
    xin_rd = [None, None]
    g_out = None
    g_ql = None
    g_x2 = None
    g_uffA = None
    g_y2 = None
    g_qlfr = None
    u_tr_tok = [None] * 16
    uld = [None] * 8
    NPRE = 12
    HS = 12
    t_h = [None] * HS
    t_ufm_q = [None]
    t_relu_a = [None] * 32

    def w1_chunk(oc, pp, t_ufm):
        pt, g = (ptA, g_A) if oc % 2 == 0 else (ptB, g_B)
        slot = oc % HS
        g.pe(te)
        P.wait(te, t_h[slot], t_ufm,
               *([tw1a[-1], tw1b[-1]] if pp == 0 and oc == 0 else []))
        for kc in range(8):
            wsrc = w1a if kc < 4 else w1b
            mm = te.matmul(
                pt[:, 0:256],
                wsrc[:, (kc % 4) * DFF + oc * 128:
                     (kc % 4) * DFF + (oc + 1) * 128],
                ufm[:, kc * 256:(kc + 1) * 256],
                start=(kc == 0), stop=(kc == 7))
        th = P.inc(mm, "PEs", 1)
        g.set_pe(th)
        g.rd(se)
        i = se.activation(hfm[:, slot * 256:(slot + 1) * 256],
                          pt[:, 0:256], AF.Relu, bias=b1c[:, oc:oc + 1])
        t_relu_a[oc] = P.inc(i, "ACTs", 1)
        g.set_rd(t_relu_a[oc])

    def ff_half(oc, pp, j2):
        slot = oc % HS
        gcx = g_ctx if j2 == 0 else g_ctxb
        gcx.pe(te)
        P.wait(te, t_relu_a[oc], tw2[(oc // 8) * 8 + 7] if pp == 0 else None)
        for og in range(2):
            mm = te.matmul(
                pctx[:, j2 * 1024 + og * 512:j2 * 1024 + og * 512 + 512],
                hfm[:, slot * 256 + j2 * 128:slot * 256 + j2 * 128 + 128],
                w2p[oc // 8][:, (oc % 8) * D + og * 512:
                             (oc % 8) * D + og * 512 + 512],
                start=False, stop=(oc == 31))
        t_ff = P.inc(mm, "PEs", 1)
        gcx.set_pe(t_ff)
        t_h[slot] = t_ff

    def b2_half(pp, j2):
        gcx = g_ctx if j2 == 0 else g_ctxb
        gcx.pe(te)
        P.wait(te, t_ufm_q[0], *([t_b2] if pp == 0 and j2 == 0 else []))
        for og in range(2):
            mm = te.matmul(pctx[:, j2 * 1024 + og * 512:
                                j2 * 1024 + og * 512 + 512],
                           ones_rb[:], b2_rb[0:1, og * 512:og * 512 + 512],
                           start=True, stop=False)
        P.inc(mm, "PEs", 1)

    def chain_tile(pp, j2):
        nonlocal apply_tok3, g_out, g_ql, g_x2, g_uffA, g_y2, g_qlfr
        t1 = pp * 2 + j2
        bf = t1 % 2
        P.wait(sy, xin_rd[0])
        tx = P.inc(sy.dma_start(out=xin2[:, 0:D],
                                in_=xe[t1 * 128:(t1 + 1) * 128, :]),
                   "Sxe2", 16)
        gcx = g_ctx if j2 == 0 else g_ctxb
        gcx.rd(ve)
        P.wait(ve, g_uffA, g_out,
               uld[t1 % 8] if t1 >= 8 else None)
        i = ve.scalar_tensor_tensor(uffb[:, 0:D],
                                    pctx[:, j2 * 1024:(j2 + 1) * 1024],
                                    0.0, u_tm[:, (t1 % 8) * D:
                                              (t1 % 8 + 1) * D],
                                    ALU.add, ALU.add)
        t_uff = P.inc(i, "DVEs", 1)
        gcx.set_rd(t_uff)
        if t1 < 8:
            P.wait(sy, u_tr_tok[t1], t_uff,
                   P.tok("Sus0") if t1 == 0 else None,
                   P.tok("Sus1") if t1 == 0 else None)
            uld[t1] = P.inc(sy.dma_start(
                out=u_tm[:, t1 * D:(t1 + 1) * D],
                in_=u_dram[t1 * 128:(t1 + 1) * 128, :]),
                "Su%d" % t1, 16)
        if DEBUG and t1 == 0:
            ddump("d_uff", uffb[:, 0:D])
        apply_tok3 = stats_ln(uffb[:, 0:D], scrA[:, 0:D], False,
                              apply_tok3, act_guards=(t_uff, g_x2),
                              dve_guards=(t_uff,))
        g_uffA = apply_tok3
        P.wait(ve, tx, xin_rd[0],
               t_be3b if t1 == 0 else None)
        i = ve.tensor_add(xin2[:, 0:D], xin2[:, 0:D], be3b[:, 0:D])
        P.inc(i, "DVEs", 1)
        P.wait(ve, apply_tok3, g_x2,
               t_g3b if t1 == 0 else None)
        ve.scalar_tensor_tensor(x2b[:, 0:D], scrA[:, 0:D], 1.0,
                                g3b[:, 0:D], ALU.mult, ALU.mult)
        i = ve.tensor_add(x2b[:, 0:D], x2b[:, 0:D], xin2[:, 0:D])
        t_x2 = P.inc(i, "DVEs", 1)
        xin_rd[0] = t_x2
        if DEBUG and t1 == 0:
            ddump("d_x2", x2b[:, 0:D])
            ddump("d_xh3", scrA[:, 0:D])
            ddump("d_xin2", xin2[:, 0:D])
            ddump("d_g3b", g3b[:, 0:D])
            ddump("d_be3b", be3b[:, 0:D])
        apply_tok3 = stats_ln(x2b[:, 0:D], y2q[:, 0:D], False, apply_tok3,
                              act_guards=(t_x2, g_y2),
                              dve_guards=(t_x2,))
        g_x2 = apply_tok3
        P.wait(ve, apply_tok3)
        i = ve.reduce_max(lsc[:, 0:1], y2q[:, 0:D], axis=AXX,
                          apply_absolute_value=True)
        vsync(i)
        vsync(ve.tensor_scalar(lsc[:, 1:2], lsc[:, 0:1], 1e-12,
                               1.0 / 127.0, ALU.max, ALU.mult))
        vsync(ve.reciprocal(lsc[:, 4:5], lsc[:, 1:2]))
        P.wait(ve, g_ql)
        ve.tensor_scalar(scrB[:, 0:D], y2q[:, 0:D], lsc[:, 4:5], RMAGIC,
                         ALU.mult, ALU.add)
        i = ve.tensor_scalar(qlt[:, 0:D], scrB[:, 0:D], -RMAGIC, None,
                             ALU.add)
        t_ql = P.inc(i, "DVEs", 1)
        g_y2 = t_ql
        if DEBUG and t1 == 0:
            ddump("d_ql", qlt[:, 0:D])
            ddump("d_lsc", lsc[:, 0:8])
        g_tr.pe(te)
        P.wait(te, t_ql)
        for c in range(8):
            mtr = te.transpose(ptr[:, c * 128:(c + 1) * 128],
                               qlt[:, c * 128:(c + 1) * 128], eye_bf[:])
        ttr = P.inc(mtr, "PEs", 1)
        g_ql = ttr
        g_tr.set_pe(ttr)
        g_tr.rd(se)
        P.wait(se, g_qlfr)
        i = se.activation(qlf[:], ptr[:], AF.Identity)
        t_qlf = P.inc(i, "ACTs", 1)
        g_tr.set_rd(t_qlf)
        gcx.pe(te)
        P.wait(te, t_qlf, t_rq2 if pp == 0 else None)
        for og in range(2):
            for c in range(8):
                mm = te.matmul(
                    pctx[:, j2 * 1024 + og * 512:j2 * 1024 + og * 512 + 512],
                    qlf[:, c * 128:(c + 1) * 128],
                    rq2[:, c * D + og * 512:c * D + og * 512 + 512],
                    start=(c == 0), stop=(c == 7))
        t_acc = P.inc(mm, "PEs", 1)
        g_qlfr = t_acc
        gcx.set_pe(t_acc)
        gcx.rd(ve)
        P.wait(ve, g_out)
        i = ve.scalar_tensor_tensor(outb[:, 0:D],
                                    pctx[:, j2 * 1024:(j2 + 1) * 1024],
                                    lsc[:, 1:2], rsb[:, 0:D],
                                    ALU.mult, ALU.mult)
        t_dq = P.inc(i, "DVEs", 1)
        gcx.set_rd(t_dq)
        i = ve.tensor_add(outb[:, 0:D], outb[:, 0:D], x2b[:, 0:D])
        t_out = P.inc(i, "DVEs", 1)
        P.wait(ge, t_out)
        g_out = P.inc(ge.dma_start(out=y_out[t1 * 128:(t1 + 1) * 128, :],
                                   in_=outb[:, 0:D]), "OUT", 16)


    for pp in range(ETILES // 2):
        if DEBUG and pp == 0:
            ddump("d_g3p5", g3b[:, 0:D])
        for j2 in range(2):
            t1 = pp * 2 + j2
            sl = t1 % 8
            g_tr.pe(te)
            P.wait(te, u_done[t1] if t1 < 8 else uld[sl])
            for c in range(8):
                mtr = te.transpose(
                    ptr[:, c * 128:(c + 1) * 128],
                    u_tm[:, sl * D + c * 128:sl * D + (c + 1) * 128],
                    eye_bf[:])
            ttr = P.inc(mtr, "PEs", 1)
            u_tr_tok[t1] = ttr
            g_tr.set_pe(ttr)
            g_tr.rd(ve)
            if t1 == 0:
                P.wait(ve, ust_rd[0], ust_rd[1])
            dst = strided(ufm, 128, 256, 8, 128, j2 * 128)
            i = ve.tensor_copy(dst, ptr[:].rearrange("p (c f) -> p c f",
                                                     f=128))
            t_ufm = P.inc(i, "DVEs", 1)
            g_tr.set_rd(t_ufm)

        t_ufm_q[0] = t_ufm
        for oc in range(5):
            w1_chunk(oc, pp, t_ufm)
        if pp > 0:
            chain_tile(pp - 1, 0)
        for oc in range(5, NPRE):
            w1_chunk(oc, pp, t_ufm)
        b2_half(pp, 0)
        for oc in range(NPRE):
            ff_half(oc, pp, 0)
        if pp > 0:
            chain_tile(pp - 1, 1)
        b2_half(pp, 1)
        for oc in range(NPRE):
            ff_half(oc, pp, 1)
        for oc in range(NPRE, 32):
            w1_chunk(oc, pp, t_ufm)
            ff_half(oc, pp, 0)
            ff_half(oc, pp, 1)
    chain_tile(ETILES // 2 - 1, 0)
    chain_tile(ETILES // 2 - 1, 1)

    ge.wait_ge(P.sem("OUT"), P.cnt["OUT"][1])
    stack.close()
    return nc


def _host_inputs():
    plans = []
    for c in range(8):
        b = c // 4
        j = c % 4
        half = j % 2
        eoff = (j // 2) * 2048
        a0 = half * AHALF
        e0 = a0 + eoff
        plans.append((b, a0, e0))
    return plans


def kernel(L0_attn_video, L0_attn_Wq, L0_attn_bq, L0_attn_Wk, L0_attn_bk,
           L0_attn_Wv, L0_attn_bv, L0_attn_Wo, L0_attn_bo, L0_attn_proj,
           L0_attn_W1, L0_attn_b1, L0_attn_W2, L0_attn_b2, L0_attn_g2,
           L0_attn_be2, L0_attn_g3, L0_attn_be3, L0_ffn_rhs):
    x = np.ascontiguousarray(np.asarray(L0_attn_video, np.float32)
                             .reshape(2, 8192, D))
    f32c = lambda a: np.ascontiguousarray(np.asarray(a, np.float32))
    wq = f32c(L0_attn_Wq); wk = f32c(L0_attn_Wk); wv = f32c(L0_attn_Wv)
    wo = f32c(L0_attn_Wo); w1 = f32c(L0_attn_W1); w2 = f32c(L0_attn_W2)
    proj = f32c(L0_attn_proj); rhs = f32c(L0_ffn_rhs)

    def col128(v, nch):
        return np.ascontiguousarray(
            np.asarray(v, np.float32).reshape(nch, 128).T)

    def row(v):
        return np.ascontiguousarray(np.asarray(v, np.float32)[None, :])

    dsel = np.zeros((128, 128), np.float32)
    for c in range(8):
        dsel[0:64, c * 16 + 2 * c] = -0.0625
        dsel[64:128, c * 16 + 2 * c + 1] = -0.0625

    common = {
        "wqT": np.ascontiguousarray(wq.T), "wkT": np.ascontiguousarray(wk.T),
        "wvT": np.ascontiguousarray(wv.T), "woT": np.ascontiguousarray(wo.T),
        "w1T": np.ascontiguousarray(w1.T), "w2T": np.ascontiguousarray(w2.T),
        "projTd": np.ascontiguousarray(proj.T),
        "rhs": rhs, "rhsT": np.ascontiguousarray(rhs.T),
        "bq128": col128(L0_attn_bq, 8), "bk128": col128(L0_attn_bk, 8),
        "b1fm": col128(L0_attn_b1, 32),
        "bv_row": row(L0_attn_bv), "bo_row": row(L0_attn_bo),
        "b2_row": row(L0_attn_b2), "g2_row": row(L0_attn_g2),
        "be2_row": row(L0_attn_be2), "g3_row": row(L0_attn_g3),
        "be3_row": row(L0_attn_be3),
        "eye": np.eye(128, dtype=np.float32),
        "dsel": dsel,
    }

    key = "v1"
    if key not in _NC_CACHE:
        _NC_CACHE[key] = build_nc()
    nc = _NC_CACHE[key]

    in_maps = []
    plans = _host_inputs()
    for c in range(8):
        b, a0, e0 = plans[c]
        m = dict(common)
        m["xh"] = np.ascontiguousarray(x[b, a0:a0 + AHALF])
        m["xe"] = np.ascontiguousarray(x[b, e0:e0 + ETOK])
        m["cid"] = np.full((1, 1), c % 2, np.int32)
        in_maps.append(m)

    res = run_bass_kernel_spmd(nc, in_maps, core_ids=list(range(8)))
    out = np.zeros((2, 8192, D), np.float32)
    for c in range(8):
        b, a0, e0 = plans[c]
        out[b, e0:e0 + ETOK] = res.results[c]["y"]
    return out.reshape(2, 8192, D)


# version 27
# speedup vs baseline: 1.0111x; 1.0111x over previous
import contextlib
import numpy as np
import ml_dtypes
import concourse.bass as bass
import concourse.mybir as mybir
from concourse.bass_utils import run_bass_kernel_spmd

F32 = mybir.dt.float32
BF16 = mybir.dt.bfloat16
I32 = mybir.dt.int32
AF = mybir.ActivationFunctionType
ALU = mybir.AluOpType
AXX = mybir.AxisListType.X

D = 1024
H = 16
DH = 64
MF = 256
DFF = 4096
AHALF = 4096
ETOK = 2048
EPS_K = 1e-4
LN_EPS = 1e-5
NORM = DH ** -0.25
RATIO = MF ** -0.5
LNRATIO = float(np.log(RATIO))
LNEPS = float(np.log(EPS_K))
RMAGIC = 12582912.0
FLAG_MAGIC = 1597463007
REPS = RATIO * EPS_K

ATILES = AHALF // 128
ETILES = ETOK // 128
EXW = 2048 + 32 + 8 + 16
SBUF_BYTES = 224 * 1024
SBUF_BASE = 17408

PS, PA, PX1, PX2, PE1, PE2 = 0, 1, 2, 3, 4, 5
DEBUG = False

_NC_CACHE = {}


class Alloc:

    def __init__(self):
        self.items = []

    def place(self, name, nbytes, lo, hi, align=64):
        nbytes = (nbytes + align - 1) // align * align
        spans = sorted((off, off + sz) for (n, off, sz, l, h) in self.items
                       if not (h < lo or l > hi))
        gaps = []
        cur = SBUF_BASE
        for st, en in spans:
            if st > cur:
                gaps.append((cur, st))
            cur = max(cur, en)
        if cur < SBUF_BYTES:
            gaps.append((cur, SBUF_BYTES))
        off = None
        best = None
        for gs, ge_ in gaps:
            if ge_ - gs >= nbytes and (best is None or ge_ - gs < best):
                best = ge_ - gs
                off = gs
        assert off is not None, \
            f"SBUF overflow placing {name} ({nbytes}B, [{lo},{hi}])"
        self.items.append((name, off, nbytes, lo, hi))
        return off


def strided(ap2d, p_cnt, d1_step, d1_cnt, d2_cnt, off):
    return bass.AP(ap2d.tensor, ap2d.offset + off,
                   [list(ap2d.ap[0])[:1] + [p_cnt], [d1_step, d1_cnt],
                    [1, d2_cnt]])


def bcast(ap2d, cnt, rep):
    return bass.AP(ap2d.tensor, ap2d.offset,
                   [list(ap2d.ap[0]), [1, cnt], [0, rep]])


class Prog:
    def __init__(self, nc):
        self.nc = nc
        self.cnt = {}
        self.hw = {}

    def mksem(self, stack, name):
        h = stack.enter_context(self.nc.semaphore(name))
        self.cnt[name] = [h, 0]

    def sem(self, name):
        return self.cnt[name][0]

    def inc(self, ins, name, amt):
        e = self.cnt[name]
        ins.then_inc(e[0], amt)
        e[1] += amt
        return (name, e[1])

    def tok(self, name):
        return (name, self.cnt[name][1])

    def wait(self, eng, *tokens):
        ek = id(eng)
        for tk in tokens:
            if tk is None:
                continue
            name, val = tk
            if val <= 0:
                continue
            k = (ek, name)
            if self.hw.get(k, -1) >= val:
                continue
            eng.wait_ge(self.cnt[name][0], val)
            self.hw[k] = val


class PG:

    def __init__(self, P):
        self.P = P
        self.last_pe = None
        self.last_rd = None

    def pe(self, eng):
        self.P.wait(eng, self.last_rd)

    def rd(self, eng):
        self.P.wait(eng, self.last_pe)

    def set_pe(self, tok):
        self.last_pe = tok

    def set_rd(self, tok):
        self.last_rd = tok


class PG2:
    def __init__(self, a, b):
        self.g = (a, b)

    def pe(self, eng):
        self.g[0].pe(eng)
        self.g[1].pe(eng)

    def rd(self, eng):
        self.g[0].rd(eng)
        self.g[1].rd(eng)

    def set_pe(self, tok):
        self.g[0].set_pe(tok)
        self.g[1].set_pe(tok)

    def set_rd(self, tok):
        self.g[0].set_rd(tok)
        self.g[1].set_rd(tok)


def build_nc():
    nc = bass.Bass("TRN2", target_bir_lowering=False, num_devices=1,
                   dynamic_dma_scratch_size=16384,
                   detect_race_conditions=False)

    def din(name, shape, dtype=F32):
        return nc.dram_tensor(name, shape, dtype, kind="ExternalInput").ap()

    xh = din("xh", [AHALF, D])
    xe = din("xe", [ETOK, D])
    cid = din("cid", [1, 1], I32)
    wqT = din("wqT", [D, D])
    wkT = din("wkT", [D, D])
    wvT = din("wvT", [D, D])
    woT = din("woT", [D, D])
    w1T = din("w1T", [D, DFF])
    w2T = din("w2T", [DFF, D])
    projTd = din("projTd", [DH, MF])
    rhs_d = din("rhs", [D, D])
    rhsT_d = din("rhsT", [D, D])
    bq128 = din("bq128", [128, 8])
    bk128 = din("bk128", [128, 8])
    b1fm_d = din("b1fm", [128, 32])
    bv_row = din("bv_row", [1, D])
    bo_row = din("bo_row", [1, D])
    b2_row = din("b2_row", [1, D])
    g2_row = din("g2_row", [1, D])
    be2_row = din("be2_row", [1, D])
    g3_row = din("g3_row", [1, D])
    be3_row = din("be3_row", [1, D])
    eye_d = din("eye", [128, 128])
    dsel_d = din("dsel", [128, 128])

    y_out = nc.dram_tensor("y", [ETOK, D], F32, kind="ExternalOutput").ap()
    dbg = {}
    if DEBUG:
        for nm, sh in [("d_y1", [128, D]), ("d_kfm", [128, 512]),
                       ("d_ek", [128, 512]), ("d_wdg", [128, 16]),
                       ("d_exch", [128, EXW]), ("d_comb", [128, EXW]),
                       ("d_ctxa", [128, 2 * H * 65]), ("d_stil", [128, H * 65]),
                       ("d_eq", [128, 512]), ("d_beta", [128, 16]),
                       ("d_nums", [128, H * 65]), ("d_tres", [128, D]),
                       ("d_u", [128, D]), ("d_uff", [128, D]),
                       ("d_x2", [128, D]), ("d_ql", [128, D]),
                       ("d_xin", [128, D]), ("d_stat", [128, 16]),
                       ("d_dmpA", [128, D]), ("d_dmpB", [128, D]),
                       ("d_qfm", [128, 512]), ("d_y1fe", [128, 512]),
                       ("d_xh3", [128, D]), ("d_xin2", [128, D]),
                       ("d_y1e", [128, D]),
                       ("d_g3b", [128, D]), ("d_be3b", [128, D]),
                       ("d_g3p0", [128, D]), ("d_g3p1", [128, D]),
                       ("d_g3p2", [128, D]), ("d_g3p3", [128, D]),
                       ("d_g3p4", [128, D]), ("d_g3p5", [128, D]),
                       ("d_lsc", [128, 8])]:
            dbg[nm] = nc.dram_tensor(nm, sh, F32, kind="ExternalOutput").ap()

    def ddump(nm, src_ap):
        if not DEBUG:
            return
        P.wait(ge, P.tok("PEs"), P.tok("ACTs"), P.tok("DVEs"))
        tok = P.inc(ge.dma_start(out=dbg[nm], in_=src_ap), "OUT", 16)
        for eng in (te, ve, se, sy):
            P.wait(eng, tok)

    def ddump2(nm, src_ap, col0):
        if not DEBUG:
            return
        ncol = src_ap.shape[-1]
        P.wait(ge, P.tok("PEs"), P.tok("ACTs"), P.tok("DVEs"))
        P.inc(ge.dma_start(out=dbg[nm][:, col0:col0 + ncol], in_=src_ap),
              "OUT", 16)
    shx = nc.dram_tensor("shx", [2, 128, EXW], F32, kind="Internal",
                         addr_space="Shared").ap()
    shf = nc.dram_tensor("shf", [2, 16], I32, kind="Internal",
                         addr_space="Shared").ap()
    rq_dram = nc.dram_tensor("rq_dram", [128, 8 * D], BF16,
                             kind="Internal").ap()
    u_dram = nc.dram_tensor("u_dram", [8 * 128, D], BF16,
                            kind="Internal").ap()

    stack = contextlib.ExitStack()
    P = Prog(nc)
    _dma_sems = ["Scid", "Seye", "Seyef", "Sdsel", "Sproj", "Sbq", "Sbk",
                 "Sb1", "Sbo", "Sb2", "Swk", "Swv", "Swq", "Swo", "Sw1a",
                 "Sw1b", "Sw20", "Sw21", "Sw22", "Sw23", "Srq2", "Sstage",
                 "Srow", "Sx0", "Sx1", "Sxe2", "Sexw", "Sexf", "Sprt",
                 "Srqd", "Srow1", "Sus0", "Sus1", "Su0", "Su1", "Su2", "Su3", "Su4",
                 "Su5", "Su6", "Su7"]
    for s in _dma_sems + ["PEs", "ACTs", "DVEs", "PL", "OUT"]:
        P.mksem(stack, s)

    al = Alloc()

    def sbat(name, shape, dtype, lo, hi):
        nby = shape[1] * (2 if dtype == BF16 else 4)
        off = al.place(name, nby, lo, hi)
        return nc.alloc_sbuf_tensor_at(name, shape, dtype, offset=off).ap()

    _decls = [
        ("ones_rb", [1, 128], BF16, PS, PE2),
        ("ones_rf", [1, 128], F32, PS, PX2),
        ("ones_cb", [128, 1], BF16, PS, PX2),
        ("n8192b", [1, 1], BF16, PS, PX1),
        ("eye_bf", [128, 128], BF16, PS, PE2),
        ("eye_f", [128, 128], F32, PS, PE1),
        ("dsel_bf", [128, 128], BF16, PS, PE1),
        ("projbd", [128, 512], BF16, PS, PE1),
        ("proj_f", [64, 256], F32, PS, PS),
        ("bqc", [128, 8], F32, PS, PE1),
        ("bkc", [128, 8], F32, PS, PA),
        ("b1c", [128, 32], F32, PS, PE2),
        ("bo_rb", [1, D], BF16, PS, PE1),
        ("b2_rb", [1, D], BF16, PS, PE2),
        ("srow", [1, 2176], F32, PS, PX2),
        ("stg", [128, 2 * D], F32, PS, PS),
        ("srow_bf", [1, 1088], BF16, PS, PX1),
        ("cids", [1, 1], I32, PS, PX1),
        ("mgk", [1, 16], I32, PS, PX1),
        ("flg", [1, 32], I32, PS, PX1),
        ("statc", [128, 16], F32, PS, PE2),
        ("cc4", [128, 4], F32, PS, PE2),
        ("ndg", [128, 64], F32, PS, PE1),
        ("scr64", [64, 128], F32, PS, PE1),
        ("wdg", [128, 16], F32, PS, PE1),
        ("wdg_bf", [128, 16], BF16, PS, PA),
        ("ycsb", [128, 8], BF16, PX1, PX1),
        ("alfb", [128, 16], F32, PX1, PE1),
        ("beta", [128, 16], F32, PE1, PE1),
        ("eqmax", [128, 16], F32, PE1, PE1),
        ("rd16", [128, 16], F32, PE1, PE1),
        ("lsc", [128, 8], F32, PE2, PE2),
        ("g3b", [128, D], F32, PS, PE2),
        ("be3b", [128, D], F32, PS, PE2),
        ("rsb", [128, D], BF16, PS, PE2),
        ("g2b", [128, D], F32, PS, PE1),
        ("be2b", [128, D], F32, PS, PE1),
        ("bvb", [128, D], F32, PS, PX1),
        ("rlsb", [128, D], F32, PS, PS),
        ("wk_bf", [128, 8 * D], BF16, PS, PA),
        ("wv_bf", [128, 8 * D], BF16, PS, PX1),
        ("wq_bf", [128, 8 * D], BF16, PX2, PE1),
        ("wo_bf", [128, 8 * D], BF16, PX2, PE1),
        ("rq_bf", [128, 8 * D], BF16, PS, PS),
        ("exch", [128, EXW], F32, PA, PX2),
        ("prt", [128, EXW], F32, PX1, PX1),
        ("maxac", [128, H * MF], BF16, PA, PA),
        ("xin", [128, 2 * D], F32, PS, PE1),
        ("ek", [128, H * MF], BF16, PA, PE1),
        ("kfm", [128, 8 * 512], BF16, PA, PE1),
        ("y1fm", [128, 8 * 512], BF16, PA, PE1),
        ("y1fm2", [128, 8 * 512], BF16, PA, PA),
        ("y1t_bf", [128, 2 * D], BF16, PA, PE1),
        ("vtmp", [128, D], F32, PA, PA),
        ("vpr", [128, D], BF16, PA, PA),
        ("ksq", [128, D], BF16, PA, PE1),
        ("dumpA", [128, D], BF16, PS, PE2),
        ("dumpB", [128, D], BF16, PS, PE2),
        ("ctxa0", [128, H * 65], BF16, PX2, PE1),
        ("ctxa1", [128, H * 65], BF16, PX2, PE1),
        ("stilb", [128, H * 65], F32, PX2, PE1),
        ("sepsb", [128, D], F32, PX1, PX2),
        ("y1e", [128, D], F32, PX1, PE1),
        ("nums", [128, H * 65], F32, PE1, PE1),
        ("attn_t", [128, D], BF16, PE1, PE1),
        ("attn_f", [128, D], BF16, PE1, PE1),
        ("tres", [128, D], F32, PE1, PE1),
        ("u_tm", [128, 8 * D], BF16, PE1, PE2),
        ("u_st", [128, 2 * D], BF16, PE1, PE1),
        ("ufm", [128, 8 * 256], BF16, PE2, PE2),
        ("hfm", [128, 12 * 256], BF16, PE2, PE2),
        ("uyo", [128, D], F32, PE2, PE2),
        ("x2b", [128, D], F32, PE2, PE2),
        ("qlt", [128, D], BF16, PE2, PE2),
        ("xin2", [128, D], F32, PE2, PE2),
        ("rq2", [128, 8 * D], BF16, PX1, PE2),
        ("scrB", [128, D], F32, PS, PE2),
        ("w1a", [128, 4 * DFF], BF16, PA, PE2),
        ("w1b", [128, 4 * DFF], BF16, PE2, PE2),
    ] + [(f"w2p{i}", [128, 8 * D], BF16,
          PE2, PE2)
         for i in range(4)]
    _nby = {n: sh[1] * (2 if dt == BF16 else 4)
            for (n, sh, dt, lo, hi) in _decls}
    import random as _rnd
    _offs = None
    _keys = [lambda d: (-_nby[d[0]], -(d[4] - d[3])),
             lambda d: (-(d[4] - d[3]), -_nby[d[0]]),
             lambda d: -_nby[d[0]] * (d[4] - d[3] + 1)]
    _orders = [sorted(_decls, key=k) for k in _keys]
    _r = _rnd.Random(0)
    for _try in range(3000):
        if _try < len(_orders):
            o = _orders[_try]
        elif _try % 3 == 0:
            o = list(_decls)
            _r.shuffle(o)
            o.sort(key=lambda d: -_nby[d[0]])
        elif _try % 3 == 1:
            o = list(_decls)
            _r.shuffle(o)
            o.sort(key=lambda d: -(d[4] - d[3]))
        else:
            o = list(_decls)
            _r.shuffle(o)
        a2 = Alloc()
        try:
            cand = {}
            for (n, sh, dt, lo, hi) in o:
                cand[n] = a2.place(n, _nby[n], lo, hi)
            _offs = cand
            break
        except AssertionError:
            continue
    assert _offs is not None, "no SBUF packing found"
    import os as _os
    if _os.environ.get("ALLOC_DEBUG"):
        for (n, sh, dt, lo, hi) in sorted(_decls, key=lambda d: _offs[d[0]]):
            print(f"{_offs[n]:7d} +{_nby[n]:6d} -> {_offs[n]+_nby[n]:7d}  "
                  f"[{lo},{hi}] {n}")
    _T = {}
    for (n, sh, dt, lo, hi) in _decls:
        _T[n] = nc.alloc_sbuf_tensor_at(n, sh, dt, offset=_offs[n]).ap()
    ones_rb = _T["ones_rb"]
    ones_rf = _T["ones_rf"]
    ones_cb = _T["ones_cb"]
    n8192b = _T["n8192b"]
    eye_bf = _T["eye_bf"]
    eye_f = _T["eye_f"]
    dsel_bf = _T["dsel_bf"]
    projbd = _T["projbd"]
    proj_f = _T["proj_f"]
    bqc = _T["bqc"]
    bkc = _T["bkc"]
    b1c = _T["b1c"]
    bo_rb = _T["bo_rb"]
    b2_rb = _T["b2_rb"]
    srow = _T["srow"]
    stg = _T["stg"]
    srow_bf = _T["srow_bf"]
    cids = _T["cids"]
    mgk = _T["mgk"]
    flg = _T["flg"]
    statc = _T["statc"]
    cc4 = _T["cc4"]
    ndg = _T["ndg"]
    scr64 = _T["scr64"]
    wdg = _T["wdg"]
    wdg_bf = _T["wdg_bf"]
    ycsb = _T["ycsb"]
    alfb = _T["alfb"]
    beta = _T["beta"]
    eqmax = _T["eqmax"]
    rd16 = _T["rd16"]
    lsc = _T["lsc"]
    g3b = _T["g3b"]
    be3b = _T["be3b"]
    rsb = _T["rsb"]
    g2b = _T["g2b"]
    be2b = _T["be2b"]
    bvb = _T["bvb"]
    rlsb = _T["rlsb"]
    wk_bf = _T["wk_bf"]
    wv_bf = _T["wv_bf"]
    wq_bf = _T["wq_bf"]
    wo_bf = _T["wo_bf"]
    rq_bf = _T["rq_bf"]
    exch = _T["exch"]
    prt = _T["prt"]
    maxac = _T["maxac"]
    xin = _T["xin"]
    ek = _T["ek"]
    kfm = _T["kfm"]
    y1fm = _T["y1fm"]
    y1fm2 = _T["y1fm2"]
    y1t_bf = _T["y1t_bf"]
    vtmp = _T["vtmp"]
    vpr = _T["vpr"]
    ksq = _T["ksq"]
    dumpA = _T["dumpA"]
    dumpB = _T["dumpB"]
    eqm = _T["kfm"]
    ctxa0 = _T["ctxa0"]
    ctxa1 = _T["ctxa1"]
    stilb = _T["stilb"]
    sepsb = _T["sepsb"]
    qfm = _T["y1fm"][:, 0:2048]
    y1fe = _T["y1fm"][:, 2048:4096]
    y1e = _T["y1e"]
    nums = _T["nums"]
    attn_t = _T["attn_t"]
    attn_f = _T["attn_f"]
    tres = _T["tres"]
    u_tm = _T["u_tm"]
    u_st = _T["u_st"]
    ufm = _T["ufm"]
    hfm = _T["hfm"]
    uyo = _T["uyo"]
    x2b = _T["x2b"]
    qlt = _T["qlt"]
    qlf = _T["qlt"]
    xin2 = _T["xin2"]
    rq2 = _T["rq2"]
    scrB = _T["scrB"]
    w1a = _T["w1a"]
    uffb = uyo
    y2q = uyo
    outb = uyo
    scrA = _T["scrB"]
    w1b = _T["w1b"]
    w2p = [_T[f"w2p{i}"] for i in range(4)]



    pctx = nc.alloc_psum_tensor("pctx", [128, 2048], F32).ap()
    pmisc = nc.alloc_psum_tensor("pmisc", [128, 512], F32).ap()
    ptr = nc.alloc_psum_tensor("ptr", [128, 1024], BF16).ap()
    ptA = nc.alloc_psum_tensor("ptA", [128, 512], F32).ap()
    ptB = nc.alloc_psum_tensor("ptB", [128, 512], F32).ap()

    g_ctx, g_misc, g_tr, g_A, g_B = PG(P), PG(P), PG(P), PG(P), PG(P)
    g_ctxb = PG(P)
    g_ctx_all = PG2(g_ctx, g_ctxb)

    te, ve, se, ge, sy = nc.tensor, nc.vector, nc.scalar, nc.gpsimd, nc.sync

    def vsync(ins):
        t = P.inc(ins, "DVEs", 1)
        P.wait(ve, t)
        return t

    def asynca(ins):
        t = P.inc(ins, "ACTs", 1)
        P.wait(se, t)
        return t


    t_cid = P.inc(sy.dma_start(out=cids[:], in_=cid), "Scid", 16)

    t_eyeb = P.inc(ge.dma_start(out=eye_bf[:], in_=eye_d), "Seye", 16)
    t_eyef = P.inc(sy.dma_start(out=eye_f[:], in_=eye_d), "Seyef", 16)
    t_dsel = P.inc(ge.dma_start(out=dsel_bf[:], in_=dsel_d), "Sdsel", 16)
    t_proj = P.inc(sy.dma_start(out=proj_f[:], in_=projTd), "Sproj", 16)
    t_bq = P.inc(sy.dma_start(out=bqc[:], in_=bq128), "Sbq", 16)
    t_bk = P.inc(sy.dma_start(out=bkc[:], in_=bk128), "Sbk", 16)
    t_b1 = P.inc(sy.dma_start(out=b1c[:], in_=b1fm_d), "Sb1", 16)
    t_bo = P.inc(ge.dma_start(out=bo_rb[:], in_=bo_row), "Sbo", 16)
    t_b2 = P.inc(ge.dma_start(out=b2_rb[:], in_=b2_row), "Sb2", 16)

    def wload(dst, src, nch, ncol, sem):
        tt = []
        for c in range(nch):
            tt.append(P.inc(ge.dma_start(
                out=dst[:, c * ncol:(c + 1) * ncol],
                in_=src[c * 128:(c + 1) * 128, :]), sem, 16))
        return tt

    tw_k = wload(wk_bf, wkT, 8, D, "Swk")
    tw_v = wload(wv_bf, wvT, 8, D, "Swv")

    P.wait(ve, t_proj)
    ve.memset(cc4[:, 0:1], LN_EPS)
    ve.memset(cc4[:, 1:2], LNRATIO)
    ve.memset(cc4[:, 2:3], LNEPS)
    ve.memset(cc4[:, 3:4], -1.0)
    i = ve.memset(ones_rb[:], 1.0)
    t_cc4 = P.inc(i, "DVEs", 1)
    ve.memset(ones_rf[:], 1.0)
    ve.memset(ones_cb[:], 1.0)
    ve.memset(n8192b[:], 8192.0)
    ve.memset(mgk[:], FLAG_MAGIC)
    ve.memset(scr64[:], 0.0)
    ve.memset(ndg[:], 0.0)
    ve.memset(projbd[:], 0.0)
    ve.tensor_scalar(projbd[0:64, 0:256], proj_f[:], NORM, None, ALU.mult)
    i = ve.tensor_scalar(projbd[64:128, 256:512], proj_f[:], NORM, None,
                         ALU.mult)
    t_pbd = P.inc(i, "DVEs", 1)

    def row_bcast_f32(row_ap, dst, tok_row):
        g_A.pe(te)
        g_B.pe(te)
        P.wait(te, tok_row)
        te.matmul(ptA[:, 0:512], ones_rf[:], row_ap[0:1, 0:512],
                  start=True, stop=True)
        m = te.matmul(ptB[:, 0:512], ones_rf[:], row_ap[0:1, 512:1024],
                      start=True, stop=True)
        tm = P.inc(m, "PEs", 1)
        g_A.set_pe(tm)
        g_B.set_pe(tm)
        g_A.rd(ve)
        g_B.rd(ve)
        ve.tensor_copy(dst[:, 0:512], ptA[:, 0:512])
        c = ve.tensor_copy(dst[:, 512:1024], ptB[:, 0:512])
        tc = P.inc(c, "DVEs", 1)
        g_A.set_rd(tc)
        g_B.set_rd(tc)
        return tc

    _rb_state = {"n": 0, "pe": [None, None]}

    def load_row_bcast(src_dram, dst):
        sl = _rb_state["n"] % 2
        _rb_state["n"] += 1
        P.wait(sy, _rb_state["pe"][sl])
        tr = P.inc(sy.dma_start(out=srow[0:1, sl * 1088:sl * 1088 + D],
                                in_=src_dram),
                   "Srow" if sl == 0 else "Srow1", 16)
        tc = row_bcast_f32(srow[0:1, sl * 1088:sl * 1088 + D], dst, tr)
        _rb_state["pe"][sl] = g_A.last_pe
        return tc

    t_bvb = load_row_bcast(bv_row, bvb)
    t_g2b = load_row_bcast(g2_row, g2b)
    t_be2b = load_row_bcast(be2_row, be2b)
    t_g3b = load_row_bcast(g3_row, g3b)
    t_be3b = load_row_bcast(be3_row, be3b)
    if DEBUG:
        ddump("d_g3p0", g3b[:, 0:D])

    stg_rd = [None, None]
    for c in range(8):
        sl = c % 2
        P.wait(sy, stg_rd[sl])
        tr = P.inc(sy.dma_start(out=stg[:, sl * D:(sl + 1) * D],
                                in_=rhsT_d[c * 128:(c + 1) * 128, :]),
                   "Sstage" if sl == 0 else "Sxe2", 16)
        P.wait(ve, tr)
        r = ve.reduce_max(statc[:, c:c + 1], stg[:, sl * D:(sl + 1) * D],
                          axis=AXX, apply_absolute_value=True)
        stg_rd[sl] = P.inc(r, "DVEs", 1)
    P.wait(ve, stg_rd[0], stg_rd[1])
    vsync(ve.tensor_scalar(wdg[:, 0:8], statc[:, 0:8], 1e-12, 1.0 / 127.0,
                           ALU.max, ALU.mult))
    vsync(ve.tensor_scalar(scrB[:, 0:8], statc[:, 0:8], 1e-12, None, ALU.max))
    vsync(ve.reciprocal(scrB[:, 8:16], scrB[:, 0:8]))
    i = ve.tensor_scalar(wdg[:, 8:16], scrB[:, 8:16], 127.0, None, ALU.mult)
    t_rls = P.inc(i, "DVEs", 1)

    g_A.pe(te)
    g_B.pe(te)
    P.wait(te, t_rls, t_eyef)
    for c in range(8):
        pt = ptA if c < 4 else ptB
        m = te.matmul(pt[0:1, (c % 4) * 128:(c % 4 + 1) * 128],
                      wdg[:, c:c + 1], eye_f[:], start=True, stop=True)
    t_rows = P.inc(m, "PEs", 1)
    g_A.set_pe(t_rows)
    g_B.set_pe(t_rows)
    g_A.rd(ve)
    g_B.rd(ve)
    ve.tensor_copy(srow[0:1, 0:512], ptA[0:1, 0:512])
    c = ve.tensor_copy(srow[0:1, 512:1024], ptB[0:1, 0:512])
    t_rows2 = P.inc(c, "DVEs", 1)
    g_A.set_rd(t_rows2)
    g_B.set_rd(t_rows2)
    t_rsb = row_bcast_f32(srow, rsb, t_rows2)
    g_A.pe(te)
    g_B.pe(te)
    for c in range(8):
        pt = ptA if c < 4 else ptB
        m = te.matmul(pt[0:1, (c % 4) * 128:(c % 4 + 1) * 128],
                      wdg[:, 8 + c:9 + c], eye_f[:], start=True, stop=True)
    t_rows = P.inc(m, "PEs", 1)
    g_A.set_pe(t_rows)
    g_B.set_pe(t_rows)
    g_A.rd(ve)
    g_B.rd(ve)
    ve.tensor_copy(scrB[0:1, 0:512], ptA[0:1, 0:512])
    c = ve.tensor_copy(scrB[0:1, 512:1024], ptB[0:1, 0:512])
    t_rows2 = P.inc(c, "DVEs", 1)
    g_A.set_rd(t_rows2)
    g_B.set_rd(t_rows2)
    t_rlsb = row_bcast_f32(scrB, rlsb, t_rows2)

    for c in range(8):
        sl = c % 2
        P.wait(sy, stg_rd[sl])
        tr = P.inc(sy.dma_start(out=stg[:, sl * D:(sl + 1) * D],
                                in_=rhs_d[c * 128:(c + 1) * 128, :]),
                   "Sstage" if sl == 0 else "Sxe2", 16)
        P.wait(ve, tr, t_rlsb if c == 0 else None)
        ve.tensor_mul(scrB[:, 0:D], stg[:, sl * D:(sl + 1) * D],
                      rlsb[:, 0:D])
        ve.tensor_scalar(scrB[:, 0:D], scrB[:, 0:D], RMAGIC, None, ALU.add)
        i = ve.tensor_scalar(rq_bf[:, c * D:(c + 1) * D], scrB[:, 0:D],
                             -RMAGIC, None, ALU.add)
        stg_rd[sl] = P.inc(i, "DVEs", 1)
    P.wait(sy, stg_rd[0], stg_rd[1])
    t_rqd = P.inc(sy.dma_start(out=rq_dram, in_=rq_bf[:]), "Srqd", 16)

    def stats_ln(x_ap, out_ap, out_is_bf, prev_act_tok, act_guards=(),
                 dve_guards=()):
        P.wait(ve, *dve_guards)
        i = ve.tensor_scalar(dumpA[:, 0:D], x_ap, 0.0, None, ALU.add,
                             ALU.add, accum_out=statc[:, 0:1])
        tsum = vsync(i)
        P.wait(se, prev_act_tok, *act_guards)
        i = se.activation(dumpB[:, 0:D], x_ap, AF.Square,
                          accum_out=statc[:, 1:2])
        tssq = P.inc(i, "ACTs", 1)
        P.wait(ve, tssq)
        vsync(ve.tensor_scalar(statc[:, 2:3], statc[:, 0:1], 1.0 / D, None,
                               ALU.mult))
        vsync(ve.tensor_scalar(statc[:, 3:4], statc[:, 1:2], 1.0 / D, None,
                               ALU.mult))
        vsync(ve.tensor_mul(statc[:, 4:5], statc[:, 2:3], statc[:, 2:3]))
        i = ve.tensor_sub(statc[:, 5:6], statc[:, 3:4], statc[:, 4:5])
        tvar = vsync(i)
        P.wait(se, tvar)
        i = se.activation(statc[:, 6:7], statc[:, 5:6], AF.Sqrt,
                          bias=cc4[:, 0:1])
        tsd = P.inc(i, "ACTs", 1)
        P.wait(ve, tsd)
        vsync(ve.reciprocal(statc[:, 7:8], statc[:, 6:7]))
        i = ve.tensor_scalar(statc[:, 8:9], statc[:, 2:3], statc[:, 7:8],
                             -1.0, ALU.mult, ALU.mult)
        tnm = P.inc(i, "DVEs", 1)
        P.wait(se, tnm)
        i = se.activation(out_ap, x_ap, AF.Identity, bias=statc[:, 8:9],
                          scale=statc[:, 7:8])
        return P.inc(i, "ACTs", 1)

    xin_rd = [None, None]
    y1t_rd = [None, None]
    apply_tok = None
    ycs_guard = None
    g_dg = None
    g_s64 = None
    g_vpr = None
    g_wdg = None
    g_vprD = None
    g_mx = None
    g_kfm = None
    g_ndg = None

    P.wait(ge, t_rqd, t_pbd)
    tw1a = wload(w1a, w1T, 4, DFF, "Sw1a")

    tev_q = [None] * 8

    def pass1_tile(t, ybuf):
        nonlocal apply_tok
        qd, qs, bf = t // 4, t % 4, t % 2
        qd, qs, bf = t // 4, t % 4, t % 2
        P.wait(sy, xin_rd[bf])
        tx = P.inc(sy.dma_start(out=xin[:, bf * D:(bf + 1) * D],
                                in_=xh[t * 128:(t + 1) * 128, :]),
                   "Sx0" if bf == 0 else "Sx1", 16)
        P.wait(ve, tx)
        P.wait(se, tx)
        apply_tok = stats_ln(xin[:, bf * D:(bf + 1) * D],
                             y1t_bf[:, bf * D:(bf + 1) * D], True, apply_tok,
                             act_guards=(y1t_rd[bf],),
                             dve_guards=(apply_tok,))
        xin_rd[bf] = apply_tok
        g_tr.pe(te)
        P.wait(te, apply_tok)
        for c in range(8):
            mtr = te.transpose(ptr[:, c * 128:(c + 1) * 128],
                               y1t_bf[:, bf * D + c * 128:bf * D + (c + 1) * 128],
                               eye_bf[:])
        ttr = P.inc(mtr, "PEs", 1)
        g_tr.set_pe(ttr)
        g_tr.rd(ve)
        dst = strided(ybuf, 128, 512, 8, 128, qs * 128)
        i = ve.tensor_copy(dst, ptr[:].rearrange("p (c f) -> p c f", f=128))
        tev = P.inc(i, "DVEs", 1)
        g_tr.set_rd(tev)
        tev_q[qd] = tev
        g_misc.pe(te)
        P.wait(te, apply_tok, t_bk if t == 0 else None)
        for c in range(8):
            mm = te.matmul(pmisc[:, 32 + c:33 + c],
                           y1t_bf[:, bf * D + c * 128:bf * D + (c + 1) * 128],
                           ones_cb[:], start=(t == 0 and c == 0),
                           stop=False)
        t_ycs = P.inc(mm, "PEs", 1)
        g_misc.set_pe(t_ycs)
        y1t_rd[bf] = t_ycs

        if DEBUG and t == 0:
            ddump("d_xin", xin[:, 0:D])
            ddump("d_stat", statc[:])
            ddump("d_dmpA", dumpA[:, 0:D])
            ddump("d_dmpB", dumpB[:, 0:D])
            ddump("d_y1", y1t_bf[:, 0:D])
    for t in range(4):
        pass1_tile(t, y1fm)
    for qd in range(8):
        ybuf = y1fm if qd % 2 == 0 else y1fm2
        P.wait(te, tev_q[qd], *([tw_k[-1]] if qd == 0 else []))
        for oc in range(8):
            pt, g = (ptA, g_A) if oc % 2 == 0 else (ptB, g_B)
            g.pe(te)
            for kc in range(8):
                mm = te.matmul(
                    pt[:, 0:512],
                    wk_bf[:, kc * D + oc * 128:kc * D + (oc + 1) * 128],
                    ybuf[:, kc * 512:(kc + 1) * 512],
                    start=(kc == 0), stop=(kc == 7))
            tk = P.inc(mm, "PEs", 1)
            g.set_pe(tk)
            g.rd(se)
            P.wait(se, g_kfm)
            i = se.activation(kfm[:, oc * 512:(oc + 1) * 512], pt[:, 0:512],
                              AF.Identity, bias=bkc[:, oc:oc + 1])
            tke = P.inc(i, "ACTs", 1)
            g.set_rd(tke)
        t_kfm = tke

        if DEBUG and qd == 0:
            ddump("d_kfm", kfm[:, 0:512])
        if qd < 7:
            for t in range(4 * qd + 4, 4 * qd + 8):
                pass1_tile(t, y1fm2 if qd % 2 == 0 else y1fm)
        for qs3 in range(4):
            t3 = qd * 4 + qs3
            src = strided(kfm, 128, 512, 8, 128, qs3 * 128)
            P.wait(ve, t_kfm, g_dg)
            i = ve.tensor_tensor(ksq[:, 0:D], src, src, ALU.mult)
            t_ksq = P.inc(i, "DVEs", 1)
            g_kfm = t_ksq
            g_A.pe(te)
            P.wait(te, t_ksq)
            for c in range(8):
                mm = te.matmul(ptA[:, 0:16],
                               ksq[:, c * 128:(c + 1) * 128],
                               dsel_bf[:, c * 16:(c + 1) * 16],
                               start=(c == 0), stop=(c == 7))
            t_dg = P.inc(mm, "PEs", 1)
            g_dg = t_dg
            g_A.set_pe(t_dg)
            g_A.rd(se)
            P.wait(se, g_vprD)
            i = se.activation(wdg[:, 0:16], ptA[:, 0:16], AF.Exp)
            t_wdg = P.inc(i, "ACTs", 1)
            g_A.set_rd(t_wdg)
            P.wait(ve, t_wdg, g_wdg)
            i = ve.tensor_copy(wdg_bf[:], wdg[:])
            t_wdgb = P.inc(i, "DVEs", 1)

            for og in range(2):
                pt, g = (ptA, g_A) if og == 0 else (ptB, g_B)
                g.pe(te)
                if t3 == 0:
                    P.wait(te, tw_v[-1], t_bvb)
                for kc in range(8):
                    mm = te.matmul(
                        pt[:, 0:512],
                        ybuf[:, kc * 512 + qs3 * 128:kc * 512 + qs3 * 128 + 128],
                        wv_bf[:, kc * D + og * 512:kc * D + og * 512 + 512],
                        start=(kc == 0), stop=(kc == 7))
                tv = P.inc(mm, "PEs", 1)
                g.set_pe(tv)
                g.rd(ve)
                i = ve.scalar_tensor_tensor(
                    vtmp[:, og * 512:(og + 1) * 512], pt[:, 0:512], 0.0,
                    bvb[:, og * 512:(og + 1) * 512], ALU.add, ALU.add)
                tve = P.inc(i, "DVEs", 1)
                g.set_rd(tve)
            P.wait(ve, g_vpr)
            i = ve.tensor_tensor(vpr[:, 0:D], vtmp[:, 0:D],
                                 bcast(wdg, 16, 64), ALU.mult)
            t_vpr = P.inc(i, "DVEs", 1)
            g_vprD = t_vpr

            t_ek = None
            for p in range(8):
                pt, g = (ptA, g_A) if p % 2 == 0 else (ptB, g_B)
                g.pe(te)
                P.wait(te, t_pbd if t3 == 0 else None)
                mm = te.matmul(
                    pt[:, 0:512],
                    kfm[:, p * 512 + qs3 * 128:p * 512 + qs3 * 128 + 128],
                    projbd[:], start=True, stop=True)
                txp = P.inc(mm, "PEs", 1)
                g.set_pe(txp)
                g.rd(se)
                P.wait(se, g_mx)
                i = se.activation(ek[:, p * 512:(p + 1) * 512],
                                  pt[:, 0:512], AF.Exp)
                t_ek = P.inc(i, "ACTs", 1)
                g.set_rd(t_ek)
                P.wait(ve, t_ek)
                if t3 == 0:
                    i = ve.tensor_copy(maxac[:, p * 512:(p + 1) * 512],
                                       ek[:, p * 512:(p + 1) * 512])
                else:
                    i = ve.tensor_max(maxac[:, p * 512:(p + 1) * 512],
                                      maxac[:, p * 512:(p + 1) * 512],
                                      ek[:, p * 512:(p + 1) * 512])
                t_mx = P.inc(i, "DVEs", 1)

            if DEBUG and t3 == 0:
                ddump("d_ek", ek[:, 0:512])
                ddump("d_wdg", wdg[:, 0:16])
            g_mx = t_mx

            g_ctx_all.pe(te)
            P.wait(te, t_ek, t_vpr)
            for h in range(H):
                for j in range(2):
                    mm = te.matmul(
                        pctx[:, h * 128 + j * 64:h * 128 + j * 64 + 64],
                        ek[:, h * 256 + j * 128:h * 256 + j * 128 + 128],
                        vpr[:, h * 64:(h + 1) * 64],
                        start=(qs3 == 0 and h % 4 == 0 and j == 0),
                        stop=(qs3 == 3 and h % 4 == 3 and j == 1))
            t_ctx = P.inc(mm, "PEs", 1)
            g_vpr = t_ctx
            g_ctx_all.set_pe(t_ctx)

            g_misc.pe(te)
            P.wait(te, t_wdgb)
            for h in range(H):
                for j in range(2):
                    mm = te.matmul(
                        pmisc[:, h * 2 + j:h * 2 + j + 1],
                        ek[:, h * 256 + j * 128:h * 256 + j * 128 + 128],
                        wdg_bf[:, h:h + 1],
                        start=False,
                        stop=(t3 == ATILES - 1 and h == H - 1 and j == 1))
            t_ks = P.inc(mm, "PEs", 1)
            g_wdg = t_ks
            g_misc.set_pe(t_ks)

        g_ctx_all.rd(ve)
        if qd == 0:
            i = ve.tensor_copy(exch[:, 0:2048], pctx[:])
        else:
            i = ve.tensor_add(exch[:, 0:2048], exch[:, 0:2048], pctx[:])
        t_ctxev = P.inc(i, "DVEs", 1)
        g_ctx_all.set_rd(t_ctxev)

    g_misc.rd(ve)
    ve.tensor_copy(exch[:, 2048:2080], pmisc[:, 0:32])
    i = ve.tensor_copy(exch[:, 2080:2088], pmisc[:, 32:40])
    t_m1 = P.inc(i, "DVEs", 1)
    g_misc.set_rd(t_m1)
    i = ve.reduce_max(exch[:, 2088:2104],
                      bass.AP(maxac.tensor, maxac.offset,
                              [list(maxac.ap[0]), [MF, H], [1, MF]]),
                      axis=AXX)
    t_aend = P.inc(i, "DVEs", 1)

    pe_endA = P.tok("PEs")
    act_endA = P.tok("ACTs")
    if DEBUG:
        ddump("d_exch", exch[:])
        ddump("d_g3p1", g3b[:, 0:D])
    apply_tok2 = apply_tok
    g_y1e = None
    prev_tu = None
    e1_tev = [None] * 8

    def e1_pass1(pp):
        nonlocal apply_tok2, g_y1e, prev_tu
        for j2 in range(2):
            t1 = pp * 2 + j2
            bf = t1 % 2
            P.wait(sy, xin_rd[bf])
            tx = P.inc(sy.dma_start(out=xin[:, bf * D:(bf + 1) * D],
                                    in_=xe[t1 * 128:(t1 + 1) * 128, :]),
                       "Sx0" if bf == 0 else "Sx1", 16)
            P.wait(ve, tx)
            P.wait(se, tx)
            apply_tok2 = stats_ln(xin[:, bf * D:(bf + 1) * D],
                                  y1t_bf[:, bf * D:(bf + 1) * D], True,
                                  apply_tok2,
                                  act_guards=(y1t_rd[bf], prev_tu),
                                  dve_guards=(apply_tok2,))
            P.wait(se, g_y1e)
            i = se.activation(y1e[:, 0:D] if j2 == 0 else scrB[:, 0:D],
                              xin[:, bf * D:(bf + 1) * D], AF.Identity,
                              bias=statc[:, 8:9], scale=statc[:, 7:8])
            apply_tok2 = P.inc(i, "ACTs", 1)
            xin_rd[bf] = apply_tok2
            g_tr.pe(te)
            P.wait(te, apply_tok2)
            for c in range(8):
                mtr = te.transpose(
                    ptr[:, c * 128:(c + 1) * 128],
                    y1t_bf[:, bf * D + c * 128:bf * D + (c + 1) * 128],
                    eye_bf[:])
            ttr = P.inc(mtr, "PEs", 1)
            g_tr.set_pe(ttr)
            g_tr.rd(ve)
            dst = strided(y1fe, 128, 256, 8, 128, j2 * 128)
            i = ve.tensor_copy(dst, ptr[:].rearrange("p (c f) -> p c f",
                                                     f=128))
            tev = P.inc(i, "DVEs", 1)
            g_tr.set_rd(tev)
            y1t_rd[bf] = tev
            e1_tev[pp] = tev


    e1_pass1(0)

    P.wait(sy, t_aend, t_ctxev, t_cid)
    me_reg = sy.to_reg(sy.value_load(cids[0:1, 0:1]))
    with sy.If_eq(me_reg, 0):
        sy.dma_start(out=shx[0, :, :], in_=exch[:]).then_inc(P.sem("Sexw"), 16)
    with sy.Else():
        sy.dma_start(out=shx[1, :, :], in_=exch[:]).then_inc(P.sem("Sexw"), 16)
    P.cnt["Sexw"][1] += 16
    P.wait(sy, P.tok("Sexw"))
    with sy.If_eq(me_reg, 0):
        sy.dma_start(out=shf[0:1, :], in_=mgk[:]).then_inc(P.sem("Sexf"), 16)
    with sy.Else():
        sy.dma_start(out=shf[1:2, :], in_=mgk[:]).then_inc(P.sem("Sexf"), 16)
    P.cnt["Sexf"][1] += 16
    P.wait(sy, P.tok("Sexf"))

    pcnt = nc.alloc_registers("pcnt", engines=[sy.engine])
    nc.regs_mov(pcnt, 0)
    nc.br("spin_loop", engines=[sy.engine])
    with nc.body("spin_loop", valid_engines=[sy.engine]):
        sy.dma_start(out=flg[0:1, 0:32],
                     in_=shf.rearrange("a b -> (a b)").rearrange(
                         "(o c) -> o c", o=1)).then_inc(P.sem("PL"), 16)
        nc.regs_alu(pcnt, pcnt, 16, op=ALU.add)
        iv = nc.snap(pcnt, engines=[sy.engine], min_val=16)
        sy.wait_ge(P.sem("PL"), iv)
        v0 = sy.value_load(flg[0:1, 0:1])
        v1 = sy.value_load(flg[0:1, 16:17])
        vm = nc.alloc_registers("vm", engines=[sy.engine])
        nc.regs_mov(vm, v0)
        nc.regs_alu(vm, vm, v1, op=ALU.min)
        vmv = nc.snap(vm, engines=[sy.engine])
        nc.br_eq(vmv, FLAG_MAGIC, on_true="spin_end", on_false="spin_loop",
                 engines=[sy.engine])
    nc.switch_bb("spin_end")

    sy.dma_start(out=exch[:], in_=shx[0, :, :]).then_inc(P.sem("Sprt"), 16)
    tp = P.inc(sy.dma_start(out=prt[:], in_=shx[1, :, :]), "Sprt", 16)
    P.cnt["Sprt"][1] += 16
    tp = P.tok("Sprt")
    P.wait(ve, tp)
    ve.tensor_add(exch[:, 0:2088], exch[:, 0:2088], prt[:, 0:2088])
    i = ve.tensor_max(exch[:, 2088:2104], exch[:, 2088:2104],
                      prt[:, 2088:2104])
    t_comb = P.inc(i, "DVEs", 1)


    if DEBUG:
        ddump("d_comb", exch[:])
        ddump("d_g3p2", g3b[:, 0:D])
    g_A.pe(te)
    P.wait(te, t_comb)
    mm = te.transpose(ptA[0:16, 0:128], exch[:, 2088:2104], eye_f[:])
    t_kt = P.inc(mm, "PEs", 1)
    g_A.set_pe(t_kt)
    g_A.rd(ve)
    i = ve.reduce_max(statc[0:16, 9:10], ptA[0:16, 0:128], axis=AXX)
    t_km = vsync(i)
    g_A.set_rd(t_km)
    P.wait(se, t_km)
    asynca(se.activation(statc[0:16, 10:11], statc[0:16, 9:10], AF.Ln))
    i = se.activation(statc[0:16, 11:12], statc[0:16, 10:11], AF.Exp,
                      bias=cc4[0:16, 1:2], scale=cc4[0:16, 3:4])
    t_alf = P.inc(i, "ACTs", 1)
    g_B.pe(te)
    P.wait(te, t_alf)
    mm = te.matmul(ptB[0:1, 0:16], statc[0:16, 11:12], eye_f[0:16, 0:16],
                   start=True, stop=True)
    t_ar = P.inc(mm, "PEs", 1)
    g_B.set_pe(t_ar)
    g_B.rd(ve)
    i = ve.tensor_copy(srow[0:1, 1024:1040], ptB[0:1, 0:16])
    t_arw = P.inc(i, "DVEs", 1)
    g_B.set_rd(t_arw)
    g_B.pe(te)
    P.wait(te, t_arw)
    mm = te.matmul(ptB[:, 0:16], ones_rf[:], srow[0:1, 1024:1040],
                   start=True, stop=True)
    t_ab = P.inc(mm, "PEs", 1)
    g_B.set_pe(t_ab)
    g_B.rd(ve)
    i = ve.tensor_copy(alfb[:], ptB[:, 0:16])
    t_alfb = P.inc(i, "DVEs", 1)
    g_B.set_rd(t_alfb)

    P.wait(ve, t_comb)
    ve.tensor_copy(ycsb[:], exch[:, 2080:2088])
    i = ve.tensor_copy(srow_bf[0:1, 0:D], bvb[0:1, 0:D])
    t_ycsb = P.inc(i, "DVEs", 1)
    g_A.pe(te)
    g_B.pe(te)
    P.wait(te, t_ycsb)
    for c in range(8):
        for og in range(2):
            pt = ptA if og == 0 else ptB
            mm = te.matmul(pt[0:1, 0:512], ycsb[:, c:c + 1],
                           wv_bf[:, c * D + og * 512:c * D + og * 512 + 512],
                           start=(c == 0), stop=False)
    for og in range(2):
        pt = ptA if og == 0 else ptB
        mm = te.matmul(pt[0:1, 0:512], n8192b[:],
                       srow_bf[0:1, og * 512:og * 512 + 512],
                       start=False, stop=True)
    t_s = P.inc(mm, "PEs", 1)
    g_A.set_pe(t_s)
    g_B.set_pe(t_s)
    g_A.rd(ve)
    g_B.rd(ve)
    ve.tensor_scalar(srow[0:1, 0:512], ptA[0:1, 0:512], REPS, None, ALU.mult)
    i = ve.tensor_scalar(srow[0:1, 512:1024], ptB[0:1, 0:512], REPS, None,
                         ALU.mult)
    t_sr = P.inc(i, "DVEs", 1)
    g_A.set_rd(t_sr)
    g_B.set_rd(t_sr)
    t_seps = row_bcast_f32(srow, sepsb, t_sr)

    ctxa = [ctxa0, ctxa1]
    t_ctxa = None
    P.wait(ve, t_comb, t_alfb, t_seps)
    for h in range(H):
        for j in range(2):
            i = ve.scalar_tensor_tensor(
                ctxa[j][:, h * 65:h * 65 + 64],
                exch[:, h * 128 + j * 64:h * 128 + j * 64 + 64],
                alfb[:, h:h + 1],
                sepsb[:, h * 64:(h + 1) * 64], ALU.mult, ALU.add)
    t_ctxa = P.inc(i, "DVEs", 1)
    for h in range(H):
        for j in range(2):
            i = ve.tensor_scalar(ctxa[j][:, h * 65 + 64:h * 65 + 65],
                                 exch[:, 2048 + h * 2 + j:2049 + h * 2 + j],
                                 alfb[:, h:h + 1], REPS * 8192.0,
                                 ALU.mult, ALU.add)
    t_ctxa = P.inc(i, "DVEs", 1)

    if DEBUG:
        ddump2("d_ctxa", ctxa0[:], 0)
        ddump2("d_ctxa", ctxa1[:], H * 65)
    P.wait(ge, t_ctxa, P.tok("PEs"), pe_endA, act_endA, t_aend, t_comb,
           t_rqd)
    tw_q = wload(wq_bf, wqT, 8, D, "Swq")
    tw_o = wload(wo_bf, woT, 8, D, "Swo")
    t_rq2 = P.inc(ge.dma_start(out=rq2[:], in_=rq_dram), "Srq2", 16)
    tw2 = []

    g_A.pe(te)
    g_B.pe(te)
    g_misc.pe(te)
    P.wait(te, t_ctxa)
    for j in range(2):
        te.matmul(ptA[0:1, 0:512], ones_cb[:], ctxa[j][:, 0:512],
                  start=(j == 0), stop=(j == 1))
        te.matmul(ptB[0:1, 0:512], ones_cb[:], ctxa[j][:, 512:1024],
                  start=(j == 0), stop=(j == 1))
        mm = te.matmul(pmisc[0:1, 0:16], ones_cb[:], ctxa[j][:, 1024:1040],
                       start=(j == 0), stop=(j == 1))
    t_st = P.inc(mm, "PEs", 1)
    g_A.set_pe(t_st)
    g_B.set_pe(t_st)
    g_misc.set_pe(t_st)
    g_A.rd(ve)
    g_B.rd(ve)
    g_misc.rd(ve)
    ve.tensor_copy(srow[0:1, 0:512], ptA[0:1, 0:512])
    ve.tensor_copy(srow[0:1, 512:1024], ptB[0:1, 0:512])
    i = ve.tensor_copy(srow[0:1, 1024:1040], pmisc[0:1, 0:16])
    t_str = P.inc(i, "DVEs", 1)
    g_A.set_rd(t_str)
    g_B.set_rd(t_str)
    g_misc.set_rd(t_str)
    g_A.pe(te)
    g_B.pe(te)
    g_misc.pe(te)
    P.wait(te, t_str)
    te.matmul(ptA[:, 0:512], ones_rf[:], srow[0:1, 0:512], start=True,
              stop=True)
    te.matmul(ptB[:, 0:512], ones_rf[:], srow[0:1, 512:1024], start=True,
              stop=True)
    mm = te.matmul(pmisc[:, 0:16], ones_rf[:], srow[0:1, 1024:1040],
                   start=True, stop=True)
    t_stb = P.inc(mm, "PEs", 1)
    g_A.set_pe(t_stb)
    g_B.set_pe(t_stb)
    g_misc.set_pe(t_stb)
    g_A.rd(ve)
    g_B.rd(ve)
    g_misc.rd(ve)
    ve.tensor_copy(stilb[:, 0:512], ptA[:, 0:512])
    ve.tensor_copy(stilb[:, 512:1024], ptB[:, 0:512])
    i = ve.tensor_copy(stilb[:, 1024:1040], pmisc[:, 0:16])
    t_stilb = P.inc(i, "DVEs", 1)
    g_A.set_rd(t_stilb)
    g_B.set_rd(t_stilb)
    g_misc.set_rd(t_stilb)

    if DEBUG:
        ddump("d_stil", stilb[:])
    u_done = []
    ust_rd = [None, None]
    g_qfm = None
    g_eq = None
    g_att = None
    if DEBUG:
        ddump("d_g3p3", g3b[:, 0:D])
    for pp in range(ETILES // 2):
        if DEBUG and pp == 1:
            ddump("d_g3p4", g3b[:, 0:D])
        if pp > 0:
            e1_pass1(pp)
        for oc in range(8):
            pt, g = (ptA, g_A) if oc % 2 == 0 else (ptB, g_B)
            g.pe(te)
            P.wait(te, e1_tev[pp],
                   *([tw_q[-1]] if pp == 0 else []))
            for kc in range(8):
                mm = te.matmul(
                    pt[:, 0:256],
                    wq_bf[:, kc * D + oc * 128:kc * D + (oc + 1) * 128],
                    y1fe[:, kc * 256:(kc + 1) * 256],
                    start=(kc == 0), stop=(kc == 7))
            tq = P.inc(mm, "PEs", 1)
            g.set_pe(tq)
            g.rd(se)
            P.wait(se, g_qfm)
            i = se.activation(qfm[:, oc * 256:(oc + 1) * 256], pt[:, 0:256],
                              AF.Identity, bias=bqc[:, oc:oc + 1])
            t_qfm = P.inc(i, "ACTs", 1)
            g.set_rd(t_qfm)

        for j2 in range(2):
            t1 = pp * 2 + j2
            y1res = y1e if j2 == 0 else scrB
            src = strided(qfm, 128, 256, 8, 128, j2 * 128)
            P.wait(ve, t_qfm, g_dg)
            i = ve.tensor_tensor(ksq[:, 0:D], src, src, ALU.mult)
            t_qsq = P.inc(i, "DVEs", 1)
            g_qfm = t_qsq
            g_A.pe(te)
            P.wait(te, t_qsq)
            for c in range(8):
                mm = te.matmul(ptA[:, 0:16],
                               ksq[:, c * 128:(c + 1) * 128],
                               dsel_bf[:, c * 16:(c + 1) * 16],
                               start=(c == 0), stop=(c == 7))
            t_dg = P.inc(mm, "PEs", 1)
            g_dg = t_dg
            g_A.set_pe(t_dg)
            g_A.rd(se)
            P.wait(se, g_eq)
            i = se.activation(wdg[:, 0:16], ptA[:, 0:16], AF.Exp,
                              scale=cc4[:, 3:4], bias=cc4[:, 2:3])
            t_edq = P.inc(i, "ACTs", 1)
            g_A.set_rd(t_edq)

            t_eq = None
            for p in range(8):
                pt, g = (ptA, g_A) if p % 2 == 0 else (ptB, g_B)
                g.pe(te)
                mm = te.matmul(
                    pt[:, 0:512],
                    qfm[:, p * 256 + j2 * 128:p * 256 + j2 * 128 + 128],
                    projbd[:], start=True, stop=True)
                txp = P.inc(mm, "PEs", 1)
                g.set_pe(txp)
                g.rd(se)
                P.wait(se, g_eq)
                i = se.activation(ek[:, p * 512:(p + 1) * 512], pt[:, 0:512],
                                  AF.Exp)
                t_eq = P.inc(i, "ACTs", 1)
                g.set_rd(t_eq)
                P.wait(ve, t_eq)
                i = ve.reduce_max(
                    eqmax[:, 2 * p:2 * p + 2],
                    bass.AP(ek.tensor, ek.offset + p * 512,
                            [list(ek.ap[0]), [256, 2], [1, 256]]),
                    axis=AXX)
                i = P.inc(i, "DVEs", 1)
                if p == 7:
                    P.wait(ve, i)

            if DEBUG and t1 == 0:
                ddump("d_eq", ek[:, 0:512])
                ddump("d_qfm", qfm[:, 0:512])
                ddump("d_y1fe", y1fe[:, 0:512])
                ddump("d_y1e", y1e[:, 0:D])
            P.wait(ve, t_edq)
            i2 = ve.tensor_mul(beta[:], eqmax[:], wdg[:, 0:16])
            t_beta = vsync(i2)
            g_eq = t_beta

            if DEBUG and t1 == 0:
                ddump("d_beta", beta[:])
            for r in range(4):
                g_tr.pe(te)
                P.wait(te, t_eq)
                for k in range(8):
                    hh, jj = (r * 8 + k) // 2, (r * 8 + k) % 2
                    mtr = te.transpose(
                        ptr[:, k * 128:(k + 1) * 128],
                        ek[:, hh * 256 + jj * 128:hh * 256 + jj * 128 + 128],
                        eye_bf[:])
                ttr = P.inc(mtr, "PEs", 1)
                g_tr.set_pe(ttr)
                g_tr.rd(ve)
                i = ve.tensor_copy(eqm[:, r * 1024:(r + 1) * 1024], ptr[:])
                t_eqm = P.inc(i, "DVEs", 1)
                g_tr.set_rd(t_eqm)

            g_ctx_all.pe(te)
            P.wait(te, t_eqm, t_stilb)
            for h in range(H):
                for j in range(2):
                    mm = te.matmul(
                        pctx[:, h * 128:h * 128 + 65],
                        eqm[:, (h * 2 + j) * 128:(h * 2 + j + 1) * 128],
                        ctxa[j][:, h * 65:(h + 1) * 65],
                        start=(h % 4 == 0 and j == 0),
                        stop=(h % 4 == 3 and j == 1))
            t_oa = P.inc(mm, "PEs", 1)
            g_ctx_all.set_pe(t_oa)
            g_ctx_all.rd(ve)
            P.wait(ve, t_beta)
            for h in range(H):
                i = ve.scalar_tensor_tensor(
                    nums[:, h * 65:(h + 1) * 65], stilb[:, h * 65:(h + 1) * 65],
                    beta[:, h:h + 1], pctx[:, h * 128:h * 128 + 65],
                    ALU.mult, ALU.add)
            t_num = P.inc(i, "DVEs", 1)
            g_ctx_all.set_rd(t_num)

            if DEBUG and t1 == 0:
                ddump("d_nums", nums[:])
            den_ap = bass.AP(nums.tensor, nums.offset + 64,
                             [list(nums.ap[0]), [65, 16]])
            vsync(ve.reciprocal(rd16[:], den_ap))
            P.wait(ve, g_att)
            for h in range(H):
                i = ve.tensor_scalar(attn_t[:, h * 64:(h + 1) * 64],
                                     nums[:, h * 65:h * 65 + 64],
                                     rd16[:, h:h + 1], None, ALU.mult)
            t_att = P.inc(i, "DVEs", 1)

            g_tr.pe(te)
            P.wait(te, t_att)
            for c in range(8):
                mtr = te.transpose(ptr[:, c * 128:(c + 1) * 128],
                                   attn_t[:, c * 128:(c + 1) * 128], eye_bf[:])
            ttr = P.inc(mtr, "PEs", 1)
            g_att = ttr
            g_tr.set_pe(ttr)
            g_tr.rd(ve)
            i = ve.tensor_copy(attn_f[:], ptr[:])
            t_atf = P.inc(i, "DVEs", 1)
            g_tr.set_rd(t_atf)

            for og in range(2):
                pt, g = (ptA, g_A) if og == 0 else (ptB, g_B)
                g.pe(te)
                P.wait(te, t_atf, *([tw_o[-1], t_bo] if t1 == 0 else []))
                te.matmul(pt[:, 0:512], ones_rb[:],
                          bo_rb[0:1, og * 512:og * 512 + 512],
                          start=True, stop=False)
                for c in range(8):
                    mm = te.matmul(
                        pt[:, 0:512], attn_f[:, c * 128:(c + 1) * 128],
                        wo_bf[:, c * D + og * 512:c * D + og * 512 + 512],
                        start=False, stop=(c == 7))
                t_wo = P.inc(mm, "PEs", 1)
                g.set_pe(t_wo)
                g.rd(ve)
                i = ve.scalar_tensor_tensor(
                    tres[:, og * 512:og * 512 + 512], pt[:, 0:512], 0.0,
                    y1res[:, og * 512:og * 512 + 512], ALU.add, ALU.add)
                t_tr2 = P.inc(i, "DVEs", 1)
                g.set_rd(t_tr2)
            g_y1e = t_tr2

            if DEBUG and t1 == 0:
                ddump("d_tres", tres[:, 0:D])
            apply_tok2 = stats_ln(tres[:, 0:D], dumpB[:, 0:D], True,
                                  apply_tok2,
                                  act_guards=(t_tr2, prev_tu))
            P.wait(ve, apply_tok2, t_g2b if t1 == 0 else None,
                   t_be2b if t1 == 0 else None,
                   ust_rd[t1 % 2] if t1 >= 8 else None)
            ve.scalar_tensor_tensor(dumpA[:, 0:D], dumpB[:, 0:D], 1.0,
                                    g2b[:, 0:D], ALU.mult, ALU.mult)
            udst = (u_tm[:, t1 * D:(t1 + 1) * D] if t1 < 8 else
                    u_st[:, (t1 % 2) * D:(t1 % 2 + 1) * D])
            i = ve.tensor_add(udst, dumpA[:, 0:D], be2b[:, 0:D])
            tu = P.inc(i, "DVEs", 1)
            prev_tu = tu
            if t1 >= 8:
                P.wait(ge, tu)
                ust_rd[t1 % 2] = P.inc(ge.dma_start(
                    out=u_dram[(t1 - 8) * 128:(t1 - 7) * 128, :],
                    in_=u_st[:, (t1 % 2) * D:(t1 % 2 + 1) * D]),
                    "Sus0" if t1 % 2 == 0 else "Sus1", 16)
            u_done.append(tu)
            if DEBUG and t1 == 0:
                ddump("d_u", u_tm[:, 0:D])

    pe_endE1 = P.tok("PEs")
    act_endE1 = P.tok("ACTs")
    dve_endE1 = P.tok("DVEs")
    P.wait(te, ust_rd[0], ust_rd[1])
    P.wait(ve, ust_rd[0], ust_rd[1])
    P.wait(se, ust_rd[0], ust_rd[1])
    P.wait(ge, ust_rd[0], ust_rd[1])
    P.wait(ge, pe_endE1, act_endE1, dve_endE1)
    tw1b = []
    for c in range(4):
        tw1b.append(P.inc(ge.dma_start(
            out=w1b[:, c * DFF:(c + 1) * DFF],
            in_=w1T[(4 + c) * 128:(5 + c) * 128, :]), "Sw1b", 16))
    P.wait(sy, pe_endE1, act_endE1, dve_endE1, ust_rd[0], ust_rd[1])
    for pc in range(4):
        tw2.append(P.inc(ge.dma_start(
            out=w2p[pc][:].rearrange("p (c n) -> p c n", n=D),
            in_=w2T[pc * 1024:(pc + 1) * 1024, :]
            .rearrange("(c p) n -> p c n", p=128)), f"Sw2{pc}", 16))

    apply_tok3 = apply_tok2
    xin_rd = [None, None]
    g_out = None
    g_ql = None
    g_x2 = None
    g_uffA = None
    g_y2 = None
    u_tr_tok = [None] * 16
    uld = [None] * 8
    NPRE = 12
    HS = 12
    t_h = [None] * HS
    t_ufm_q = [None]
    t_relu_a = [None] * 32

    def w1_chunk(oc, pp, t_ufm):
        pt, g = (ptA, g_A) if oc % 2 == 0 else (ptB, g_B)
        slot = oc % HS
        g.pe(te)
        P.wait(te, t_h[slot], t_ufm,
               *([tw1a[-1], tw1b[-1]] if pp == 0 and oc == 0 else []))
        for kc in range(8):
            wsrc = w1a if kc < 4 else w1b
            mm = te.matmul(
                pt[:, 0:256],
                wsrc[:, (kc % 4) * DFF + oc * 128:
                     (kc % 4) * DFF + (oc + 1) * 128],
                ufm[:, kc * 256:(kc + 1) * 256],
                start=(kc == 0), stop=(kc == 7))
        th = P.inc(mm, "PEs", 1)
        g.set_pe(th)
        g.rd(se)
        i = se.activation(hfm[:, slot * 256:(slot + 1) * 256],
                          pt[:, 0:256], AF.Relu, bias=b1c[:, oc:oc + 1])
        t_relu_a[oc] = P.inc(i, "ACTs", 1)
        g.set_rd(t_relu_a[oc])

    def ff_half(oc, pp, j2):
        slot = oc % HS
        gcx = g_ctx if j2 == 0 else g_ctxb
        gcx.pe(te)
        P.wait(te, t_relu_a[oc], tw2[oc // 8] if pp == 0 else None)
        for og in range(2):
            mm = te.matmul(
                pctx[:, j2 * 1024 + og * 512:j2 * 1024 + og * 512 + 512],
                hfm[:, slot * 256 + j2 * 128:slot * 256 + j2 * 128 + 128],
                w2p[oc // 8][:, (oc % 8) * D + og * 512:
                             (oc % 8) * D + og * 512 + 512],
                start=False, stop=(oc == 31))
        t_ff = P.inc(mm, "PEs", 1)
        gcx.set_pe(t_ff)
        t_h[slot] = t_ff

    def b2_half(pp, j2):
        gcx = g_ctx if j2 == 0 else g_ctxb
        gcx.pe(te)
        P.wait(te, t_ufm_q[0], *([t_b2] if pp == 0 and j2 == 0 else []))
        for og in range(2):
            mm = te.matmul(pctx[:, j2 * 1024 + og * 512:
                                j2 * 1024 + og * 512 + 512],
                           ones_rb[:], b2_rb[0:1, og * 512:og * 512 + 512],
                           start=True, stop=False)
        P.inc(mm, "PEs", 1)

    def chain_tile(pp, j2):
        nonlocal apply_tok3, g_out, g_ql, g_x2, g_uffA, g_y2
        t1 = pp * 2 + j2
        bf = t1 % 2
        P.wait(sy, xin_rd[0])
        tx = P.inc(sy.dma_start(out=xin2[:, 0:D],
                                in_=xe[t1 * 128:(t1 + 1) * 128, :]),
                   "Sxe2", 16)
        gcx = g_ctx if j2 == 0 else g_ctxb
        gcx.rd(ve)
        P.wait(ve, g_uffA, g_out,
               uld[t1 % 8] if t1 >= 8 else None)
        i = ve.scalar_tensor_tensor(uffb[:, 0:D],
                                    pctx[:, j2 * 1024:(j2 + 1) * 1024],
                                    0.0, u_tm[:, (t1 % 8) * D:
                                              (t1 % 8 + 1) * D],
                                    ALU.add, ALU.add)
        t_uff = P.inc(i, "DVEs", 1)
        gcx.set_rd(t_uff)
        if t1 < 8:
            P.wait(sy, u_tr_tok[t1], t_uff,
                   P.tok("Sus0") if t1 == 0 else None,
                   P.tok("Sus1") if t1 == 0 else None)
            uld[t1] = P.inc(sy.dma_start(
                out=u_tm[:, t1 * D:(t1 + 1) * D],
                in_=u_dram[t1 * 128:(t1 + 1) * 128, :]),
                "Su%d" % t1, 16)
        if DEBUG and t1 == 0:
            ddump("d_uff", uffb[:, 0:D])
        apply_tok3 = stats_ln(uffb[:, 0:D], scrA[:, 0:D], False,
                              apply_tok3, act_guards=(t_uff, g_x2),
                              dve_guards=(t_uff,))
        g_uffA = apply_tok3
        P.wait(ve, tx, xin_rd[0],
               t_be3b if t1 == 0 else None)
        i = ve.tensor_add(xin2[:, 0:D], xin2[:, 0:D], be3b[:, 0:D])
        P.inc(i, "DVEs", 1)
        P.wait(ve, apply_tok3, g_x2,
               t_g3b if t1 == 0 else None)
        ve.scalar_tensor_tensor(x2b[:, 0:D], scrA[:, 0:D], 1.0,
                                g3b[:, 0:D], ALU.mult, ALU.mult)
        i = ve.tensor_add(x2b[:, 0:D], x2b[:, 0:D], xin2[:, 0:D])
        t_x2 = P.inc(i, "DVEs", 1)
        xin_rd[0] = t_x2
        if DEBUG and t1 == 0:
            ddump("d_x2", x2b[:, 0:D])
            ddump("d_xh3", scrA[:, 0:D])
            ddump("d_xin2", xin2[:, 0:D])
            ddump("d_g3b", g3b[:, 0:D])
            ddump("d_be3b", be3b[:, 0:D])
        apply_tok3 = stats_ln(x2b[:, 0:D], y2q[:, 0:D], False, apply_tok3,
                              act_guards=(t_x2, g_y2),
                              dve_guards=(t_x2,))
        g_x2 = apply_tok3
        P.wait(ve, apply_tok3)
        i = ve.reduce_max(lsc[:, 0:1], y2q[:, 0:D], axis=AXX,
                          apply_absolute_value=True)
        vsync(i)
        vsync(ve.tensor_scalar(lsc[:, 1:2], lsc[:, 0:1], 1e-12,
                               1.0 / 127.0, ALU.max, ALU.mult))
        vsync(ve.reciprocal(lsc[:, 4:5], lsc[:, 1:2]))
        P.wait(ve, g_ql)
        ve.tensor_scalar(scrB[:, 0:D], y2q[:, 0:D], lsc[:, 4:5], RMAGIC,
                         ALU.mult, ALU.add)
        i = ve.tensor_scalar(qlt[:, 0:D], scrB[:, 0:D], -RMAGIC, None,
                             ALU.add)
        t_ql = P.inc(i, "DVEs", 1)
        g_y2 = t_ql
        if DEBUG and t1 == 0:
            ddump("d_ql", qlt[:, 0:D])
            ddump("d_lsc", lsc[:, 0:8])
        g_tr.pe(te)
        P.wait(te, t_ql)
        for c in range(8):
            mtr = te.transpose(ptr[:, c * 128:(c + 1) * 128],
                               qlt[:, c * 128:(c + 1) * 128], eye_bf[:])
        ttr = P.inc(mtr, "PEs", 1)
        g_ql = ttr
        g_tr.set_pe(ttr)
        g_tr.rd(ve)
        i = ve.tensor_copy(qlf[:], ptr[:])
        t_qlf = P.inc(i, "DVEs", 1)
        g_tr.set_rd(t_qlf)
        gcx.pe(te)
        P.wait(te, t_qlf, t_rq2 if pp == 0 else None)
        for og in range(2):
            for c in range(8):
                mm = te.matmul(
                    pctx[:, j2 * 1024 + og * 512:j2 * 1024 + og * 512 + 512],
                    qlf[:, c * 128:(c + 1) * 128],
                    rq2[:, c * D + og * 512:c * D + og * 512 + 512],
                    start=(c == 0), stop=(c == 7))
        t_acc = P.inc(mm, "PEs", 1)
        gcx.set_pe(t_acc)
        gcx.rd(ve)
        P.wait(ve, g_out)
        i = ve.scalar_tensor_tensor(outb[:, 0:D],
                                    pctx[:, j2 * 1024:(j2 + 1) * 1024],
                                    lsc[:, 1:2], rsb[:, 0:D],
                                    ALU.mult, ALU.mult)
        t_dq = P.inc(i, "DVEs", 1)
        gcx.set_rd(t_dq)
        i = ve.tensor_add(outb[:, 0:D], outb[:, 0:D], x2b[:, 0:D])
        t_out = P.inc(i, "DVEs", 1)
        P.wait(ge, t_out)
        g_out = P.inc(ge.dma_start(out=y_out[t1 * 128:(t1 + 1) * 128, :],
                                   in_=outb[:, 0:D]), "OUT", 16)


    for pp in range(ETILES // 2):
        if DEBUG and pp == 0:
            ddump("d_g3p5", g3b[:, 0:D])
        for j2 in range(2):
            t1 = pp * 2 + j2
            sl = t1 % 8
            g_tr.pe(te)
            P.wait(te, u_done[t1] if t1 < 8 else uld[sl])
            for c in range(8):
                mtr = te.transpose(
                    ptr[:, c * 128:(c + 1) * 128],
                    u_tm[:, sl * D + c * 128:sl * D + (c + 1) * 128],
                    eye_bf[:])
            ttr = P.inc(mtr, "PEs", 1)
            u_tr_tok[t1] = ttr
            g_tr.set_pe(ttr)
            g_tr.rd(ve)
            if t1 == 0:
                P.wait(ve, ust_rd[0], ust_rd[1])
            dst = strided(ufm, 128, 256, 8, 128, j2 * 128)
            i = ve.tensor_copy(dst, ptr[:].rearrange("p (c f) -> p c f",
                                                     f=128))
            t_ufm = P.inc(i, "DVEs", 1)
            g_tr.set_rd(t_ufm)

        t_ufm_q[0] = t_ufm
        for oc in range(5):
            w1_chunk(oc, pp, t_ufm)
        if pp > 0:
            chain_tile(pp - 1, 0)
        for oc in range(5, NPRE):
            w1_chunk(oc, pp, t_ufm)
        b2_half(pp, 0)
        for oc in range(NPRE):
            ff_half(oc, pp, 0)
        if pp > 0:
            chain_tile(pp - 1, 1)
        b2_half(pp, 1)
        for oc in range(NPRE):
            ff_half(oc, pp, 1)
        for oc in range(NPRE, 32):
            w1_chunk(oc, pp, t_ufm)
            ff_half(oc, pp, 0)
            ff_half(oc, pp, 1)
    chain_tile(ETILES // 2 - 1, 0)
    chain_tile(ETILES // 2 - 1, 1)

    ge.wait_ge(P.sem("OUT"), P.cnt["OUT"][1])
    stack.close()
    return nc


def _host_inputs():
    plans = []
    for c in range(8):
        b = c // 4
        j = c % 4
        half = j % 2
        eoff = (j // 2) * 2048
        a0 = half * AHALF
        e0 = a0 + eoff
        plans.append((b, a0, e0))
    return plans


def kernel(L0_attn_video, L0_attn_Wq, L0_attn_bq, L0_attn_Wk, L0_attn_bk,
           L0_attn_Wv, L0_attn_bv, L0_attn_Wo, L0_attn_bo, L0_attn_proj,
           L0_attn_W1, L0_attn_b1, L0_attn_W2, L0_attn_b2, L0_attn_g2,
           L0_attn_be2, L0_attn_g3, L0_attn_be3, L0_ffn_rhs):
    x = np.ascontiguousarray(np.asarray(L0_attn_video, np.float32)
                             .reshape(2, 8192, D))
    f32c = lambda a: np.ascontiguousarray(np.asarray(a, np.float32))
    wq = f32c(L0_attn_Wq); wk = f32c(L0_attn_Wk); wv = f32c(L0_attn_Wv)
    wo = f32c(L0_attn_Wo); w1 = f32c(L0_attn_W1); w2 = f32c(L0_attn_W2)
    proj = f32c(L0_attn_proj); rhs = f32c(L0_ffn_rhs)

    def col128(v, nch):
        return np.ascontiguousarray(
            np.asarray(v, np.float32).reshape(nch, 128).T)

    def row(v):
        return np.ascontiguousarray(np.asarray(v, np.float32)[None, :])

    dsel = np.zeros((128, 128), np.float32)
    for c in range(8):
        dsel[0:64, c * 16 + 2 * c] = -0.0625
        dsel[64:128, c * 16 + 2 * c + 1] = -0.0625

    common = {
        "wqT": np.ascontiguousarray(wq.T), "wkT": np.ascontiguousarray(wk.T),
        "wvT": np.ascontiguousarray(wv.T), "woT": np.ascontiguousarray(wo.T),
        "w1T": np.ascontiguousarray(w1.T), "w2T": np.ascontiguousarray(w2.T),
        "projTd": np.ascontiguousarray(proj.T),
        "rhs": rhs, "rhsT": np.ascontiguousarray(rhs.T),
        "bq128": col128(L0_attn_bq, 8), "bk128": col128(L0_attn_bk, 8),
        "b1fm": col128(L0_attn_b1, 32),
        "bv_row": row(L0_attn_bv), "bo_row": row(L0_attn_bo),
        "b2_row": row(L0_attn_b2), "g2_row": row(L0_attn_g2),
        "be2_row": row(L0_attn_be2), "g3_row": row(L0_attn_g3),
        "be3_row": row(L0_attn_be3),
        "eye": np.eye(128, dtype=np.float32),
        "dsel": dsel,
    }

    key = "v1"
    if key not in _NC_CACHE:
        _NC_CACHE[key] = build_nc()
    nc = _NC_CACHE[key]

    in_maps = []
    plans = _host_inputs()
    for c in range(8):
        b, a0, e0 = plans[c]
        m = dict(common)
        m["xh"] = np.ascontiguousarray(x[b, a0:a0 + AHALF])
        m["xe"] = np.ascontiguousarray(x[b, e0:e0 + ETOK])
        m["cid"] = np.full((1, 1), c % 2, np.int32)
        in_maps.append(m)

    res = run_bass_kernel_spmd(nc, in_maps, core_ids=list(range(8)))
    out = np.zeros((2, 8192, D), np.float32)
    for c in range(8):
        b, a0, e0 = plans[c]
        out[b, e0:e0 + ETOK] = res.results[c]["y"]
    return out.reshape(2, 8192, D)


# version 28
# speedup vs baseline: 1.0115x; 1.0004x over previous
import contextlib
import numpy as np
import ml_dtypes
import concourse.bass as bass
import concourse.mybir as mybir
from concourse.bass_utils import run_bass_kernel_spmd

F32 = mybir.dt.float32
BF16 = mybir.dt.bfloat16
I32 = mybir.dt.int32
AF = mybir.ActivationFunctionType
ALU = mybir.AluOpType
AXX = mybir.AxisListType.X

D = 1024
H = 16
DH = 64
MF = 256
DFF = 4096
AHALF = 4096
ETOK = 2048
EPS_K = 1e-4
LN_EPS = 1e-5
NORM = DH ** -0.25
RATIO = MF ** -0.5
LNRATIO = float(np.log(RATIO))
LNEPS = float(np.log(EPS_K))
RMAGIC = 12582912.0
FLAG_MAGIC = 1597463007
REPS = RATIO * EPS_K

ATILES = AHALF // 128
ETILES = ETOK // 128
EXW = 2048 + 32 + 8 + 16
SBUF_BYTES = 224 * 1024
SBUF_BASE = 17408

PS, PA, PX1, PX2, PE1, PE2 = 0, 1, 2, 3, 4, 5
DEBUG = False

_NC_CACHE = {}


class Alloc:

    def __init__(self):
        self.items = []

    def place(self, name, nbytes, lo, hi, align=64):
        nbytes = (nbytes + align - 1) // align * align
        spans = sorted((off, off + sz) for (n, off, sz, l, h) in self.items
                       if not (h < lo or l > hi))
        gaps = []
        cur = SBUF_BASE
        for st, en in spans:
            if st > cur:
                gaps.append((cur, st))
            cur = max(cur, en)
        if cur < SBUF_BYTES:
            gaps.append((cur, SBUF_BYTES))
        off = None
        best = None
        for gs, ge_ in gaps:
            if ge_ - gs >= nbytes and (best is None or ge_ - gs < best):
                best = ge_ - gs
                off = gs
        assert off is not None, \
            f"SBUF overflow placing {name} ({nbytes}B, [{lo},{hi}])"
        self.items.append((name, off, nbytes, lo, hi))
        return off


def strided(ap2d, p_cnt, d1_step, d1_cnt, d2_cnt, off):
    return bass.AP(ap2d.tensor, ap2d.offset + off,
                   [list(ap2d.ap[0])[:1] + [p_cnt], [d1_step, d1_cnt],
                    [1, d2_cnt]])


def bcast(ap2d, cnt, rep):
    return bass.AP(ap2d.tensor, ap2d.offset,
                   [list(ap2d.ap[0]), [1, cnt], [0, rep]])


class Prog:
    def __init__(self, nc):
        self.nc = nc
        self.cnt = {}
        self.hw = {}

    def mksem(self, stack, name):
        h = stack.enter_context(self.nc.semaphore(name))
        self.cnt[name] = [h, 0]

    def sem(self, name):
        return self.cnt[name][0]

    def inc(self, ins, name, amt):
        e = self.cnt[name]
        ins.then_inc(e[0], amt)
        e[1] += amt
        return (name, e[1])

    def tok(self, name):
        return (name, self.cnt[name][1])

    def wait(self, eng, *tokens):
        ek = id(eng)
        for tk in tokens:
            if tk is None:
                continue
            name, val = tk
            if val <= 0:
                continue
            k = (ek, name)
            if self.hw.get(k, -1) >= val:
                continue
            eng.wait_ge(self.cnt[name][0], val)
            self.hw[k] = val


class PG:

    def __init__(self, P):
        self.P = P
        self.last_pe = None
        self.last_rd = None

    def pe(self, eng):
        self.P.wait(eng, self.last_rd)

    def rd(self, eng):
        self.P.wait(eng, self.last_pe)

    def set_pe(self, tok):
        self.last_pe = tok

    def set_rd(self, tok):
        self.last_rd = tok


class PG2:
    def __init__(self, a, b):
        self.g = (a, b)

    def pe(self, eng):
        self.g[0].pe(eng)
        self.g[1].pe(eng)

    def rd(self, eng):
        self.g[0].rd(eng)
        self.g[1].rd(eng)

    def set_pe(self, tok):
        self.g[0].set_pe(tok)
        self.g[1].set_pe(tok)

    def set_rd(self, tok):
        self.g[0].set_rd(tok)
        self.g[1].set_rd(tok)


def build_nc():
    nc = bass.Bass("TRN2", target_bir_lowering=False, num_devices=1,
                   dynamic_dma_scratch_size=16384,
                   detect_race_conditions=False)

    def din(name, shape, dtype=F32):
        return nc.dram_tensor(name, shape, dtype, kind="ExternalInput").ap()

    xh = din("xh", [AHALF, D])
    xe = din("xe", [ETOK, D])
    cid = din("cid", [1, 1], I32)
    wqT = din("wqT", [D, D])
    wkT = din("wkT", [D, D])
    wvT = din("wvT", [D, D])
    woT = din("woT", [D, D])
    w1T = din("w1T", [D, DFF])
    w2T = din("w2T", [DFF, D])
    projTd = din("projTd", [DH, MF])
    rhs_d = din("rhs", [D, D])
    rhsT_d = din("rhsT", [D, D])
    bq128 = din("bq128", [128, 8])
    bk128 = din("bk128", [128, 8])
    b1fm_d = din("b1fm", [128, 32])
    bv_row = din("bv_row", [1, D])
    bo_row = din("bo_row", [1, D])
    b2_row = din("b2_row", [1, D])
    g2_row = din("g2_row", [1, D])
    be2_row = din("be2_row", [1, D])
    g3_row = din("g3_row", [1, D])
    be3_row = din("be3_row", [1, D])
    eye_d = din("eye", [128, 128])
    dsel_d = din("dsel", [128, 128])

    y_out = nc.dram_tensor("y", [ETOK, D], F32, kind="ExternalOutput").ap()
    dbg = {}
    if DEBUG:
        for nm, sh in [("d_y1", [128, D]), ("d_kfm", [128, 512]),
                       ("d_ek", [128, 512]), ("d_wdg", [128, 16]),
                       ("d_exch", [128, EXW]), ("d_comb", [128, EXW]),
                       ("d_ctxa", [128, 2 * H * 65]), ("d_stil", [128, H * 65]),
                       ("d_eq", [128, 512]), ("d_beta", [128, 16]),
                       ("d_nums", [128, H * 65]), ("d_tres", [128, D]),
                       ("d_u", [128, D]), ("d_uff", [128, D]),
                       ("d_x2", [128, D]), ("d_ql", [128, D]),
                       ("d_xin", [128, D]), ("d_stat", [128, 16]),
                       ("d_dmpA", [128, D]), ("d_dmpB", [128, D]),
                       ("d_qfm", [128, 512]), ("d_y1fe", [128, 512]),
                       ("d_xh3", [128, D]), ("d_xin2", [128, D]),
                       ("d_y1e", [128, D]),
                       ("d_g3b", [128, D]), ("d_be3b", [128, D]),
                       ("d_g3p0", [128, D]), ("d_g3p1", [128, D]),
                       ("d_g3p2", [128, D]), ("d_g3p3", [128, D]),
                       ("d_g3p4", [128, D]), ("d_g3p5", [128, D]),
                       ("d_lsc", [128, 8])]:
            dbg[nm] = nc.dram_tensor(nm, sh, F32, kind="ExternalOutput").ap()

    def ddump(nm, src_ap):
        if not DEBUG:
            return
        P.wait(ge, P.tok("PEs"), P.tok("ACTs"), P.tok("DVEs"))
        tok = P.inc(ge.dma_start(out=dbg[nm], in_=src_ap), "OUT", 16)
        for eng in (te, ve, se, sy):
            P.wait(eng, tok)

    def ddump2(nm, src_ap, col0):
        if not DEBUG:
            return
        ncol = src_ap.shape[-1]
        P.wait(ge, P.tok("PEs"), P.tok("ACTs"), P.tok("DVEs"))
        P.inc(ge.dma_start(out=dbg[nm][:, col0:col0 + ncol], in_=src_ap),
              "OUT", 16)
    shx = nc.dram_tensor("shx", [2, 128, EXW], F32, kind="Internal",
                         addr_space="Shared").ap()
    shf = nc.dram_tensor("shf", [2, 16], I32, kind="Internal",
                         addr_space="Shared").ap()
    rq_dram = nc.dram_tensor("rq_dram", [128, 8 * D], BF16,
                             kind="Internal").ap()
    u_dram = nc.dram_tensor("u_dram", [8 * 128, D], BF16,
                            kind="Internal").ap()

    stack = contextlib.ExitStack()
    P = Prog(nc)
    _dma_sems = ["Scid", "Seye", "Seyef", "Sdsel", "Sproj", "Sbq", "Sbk",
                 "Sb1", "Sbo", "Sb2", "Swk", "Swv", "Swq", "Swo", "Sw1a",
                 "Sw1b", "Sw20", "Sw21", "Sw22", "Sw23", "Srq2", "Sstage",
                 "Srow", "Sx0", "Sx1", "Sxe2", "Sexw", "Sexf", "Sprt",
                 "Srqd", "Srow1", "Sus0", "Sus1", "Su0", "Su1", "Su2", "Su3", "Su4",
                 "Su5", "Su6", "Su7"]
    for s in _dma_sems + ["PEs", "ACTs", "DVEs", "PL", "OUT"]:
        P.mksem(stack, s)

    al = Alloc()

    def sbat(name, shape, dtype, lo, hi):
        nby = shape[1] * (2 if dtype == BF16 else 4)
        off = al.place(name, nby, lo, hi)
        return nc.alloc_sbuf_tensor_at(name, shape, dtype, offset=off).ap()

    _decls = [
        ("ones_rb", [1, 128], BF16, PS, PE2),
        ("ones_rf", [1, 128], F32, PS, PX2),
        ("ones_cb", [128, 1], BF16, PS, PX2),
        ("n8192b", [1, 1], BF16, PS, PX1),
        ("eye_bf", [128, 128], BF16, PS, PE2),
        ("eye_f", [128, 128], F32, PS, PE1),
        ("dsel_bf", [128, 128], BF16, PS, PE1),
        ("projbd", [128, 512], BF16, PS, PE1),
        ("proj_f", [64, 256], F32, PS, PS),
        ("bqc", [128, 8], F32, PS, PE1),
        ("bkc", [128, 8], F32, PS, PA),
        ("b1c", [128, 32], F32, PS, PE2),
        ("bo_rb", [1, D], BF16, PS, PE1),
        ("b2_rb", [1, D], BF16, PS, PE2),
        ("srow", [1, 2176], F32, PS, PX2),
        ("stg", [128, 2 * D], F32, PS, PS),
        ("srow_bf", [1, 1088], BF16, PS, PX1),
        ("cids", [1, 1], I32, PS, PX1),
        ("mgk", [1, 16], I32, PS, PX1),
        ("flg", [1, 32], I32, PS, PX1),
        ("statc", [128, 16], F32, PS, PE2),
        ("cc4", [128, 4], F32, PS, PE2),
        ("ndg", [128, 64], F32, PS, PE1),
        ("scr64", [64, 128], F32, PS, PE1),
        ("wdg", [128, 16], F32, PS, PE1),
        ("wdg_bf", [128, 16], BF16, PS, PA),
        ("ycsb", [128, 8], BF16, PX1, PX1),
        ("alfb", [128, 16], F32, PX1, PE1),
        ("beta", [128, 16], F32, PE1, PE1),
        ("eqmax", [128, 16], F32, PE1, PE1),
        ("rd16", [128, 16], F32, PE1, PE1),
        ("lsc", [128, 8], F32, PE2, PE2),
        ("g3b", [128, D], F32, PS, PE2),
        ("be3b", [128, D], F32, PS, PE2),
        ("rsb", [128, D], BF16, PS, PE2),
        ("g2b", [128, D], F32, PS, PE1),
        ("be2b", [128, D], F32, PS, PE1),
        ("bvb", [128, D], F32, PS, PX1),
        ("rlsb", [128, D], F32, PS, PS),
        ("wk_bf", [128, 8 * D], BF16, PS, PA),
        ("wv_bf", [128, 8 * D], BF16, PS, PX1),
        ("wq_bf", [128, 8 * D], BF16, PX2, PE1),
        ("wo_bf", [128, 8 * D], BF16, PX2, PE1),
        ("rq_bf", [128, 8 * D], BF16, PS, PS),
        ("exch", [128, EXW], F32, PA, PX2),
        ("prt", [128, EXW], F32, PX1, PX1),
        ("maxac", [128, H * MF], BF16, PA, PA),
        ("xin", [128, 2 * D], F32, PS, PE1),
        ("ek", [128, H * MF], BF16, PA, PE1),
        ("kfm", [128, 8 * 512], BF16, PA, PE1),
        ("y1fm", [128, 8 * 512], BF16, PA, PE1),
        ("y1fm2", [128, 8 * 512], BF16, PA, PA),
        ("y1t_bf", [128, 2 * D], BF16, PA, PE1),
        ("vtmp", [128, D], F32, PA, PA),
        ("vpr", [128, D], BF16, PA, PA),
        ("ksq", [128, D], BF16, PA, PE1),
        ("dumpA", [128, D], BF16, PS, PE2),
        ("dumpB", [128, D], BF16, PS, PE2),
        ("ctxa0", [128, H * 65], BF16, PX2, PE1),
        ("ctxa1", [128, H * 65], BF16, PX2, PE1),
        ("stilb", [128, H * 65], F32, PX2, PE1),
        ("sepsb", [128, D], F32, PX1, PX2),
        ("y1e", [128, D], F32, PX1, PE1),
        ("nums", [128, H * 65], F32, PE1, PE1),
        ("attn_t", [128, D], BF16, PE1, PE1),
        ("attn_f", [128, D], BF16, PE1, PE1),
        ("tres", [128, D], F32, PE1, PE1),
        ("u_tm", [128, 8 * D], BF16, PE1, PE2),
        ("u_st", [128, 2 * D], BF16, PE1, PE1),
        ("ufm", [128, 8 * 256], BF16, PE2, PE2),
        ("hfm", [128, 12 * 256], BF16, PE2, PE2),
        ("uyo", [128, D], F32, PE2, PE2),
        ("x2b", [128, D], F32, PE2, PE2),
        ("qlt", [128, D], BF16, PE2, PE2),
        ("xin2", [128, D], F32, PE2, PE2),
        ("rq2", [128, 8 * D], BF16, PX1, PE2),
        ("scrB", [128, D], F32, PS, PE2),
        ("w1a", [128, 4 * DFF], BF16, PA, PE2),
        ("w1b", [128, 4 * DFF], BF16, PE2, PE2),
    ] + [(f"w2p{i}", [128, 8 * D], BF16,
          PE2, PE2)
         for i in range(4)]
    _nby = {n: sh[1] * (2 if dt == BF16 else 4)
            for (n, sh, dt, lo, hi) in _decls}
    import random as _rnd
    _offs = None
    _keys = [lambda d: (-_nby[d[0]], -(d[4] - d[3])),
             lambda d: (-(d[4] - d[3]), -_nby[d[0]]),
             lambda d: -_nby[d[0]] * (d[4] - d[3] + 1)]
    _orders = [sorted(_decls, key=k) for k in _keys]
    _r = _rnd.Random(0)
    for _try in range(3000):
        if _try < len(_orders):
            o = _orders[_try]
        elif _try % 3 == 0:
            o = list(_decls)
            _r.shuffle(o)
            o.sort(key=lambda d: -_nby[d[0]])
        elif _try % 3 == 1:
            o = list(_decls)
            _r.shuffle(o)
            o.sort(key=lambda d: -(d[4] - d[3]))
        else:
            o = list(_decls)
            _r.shuffle(o)
        a2 = Alloc()
        try:
            cand = {}
            for (n, sh, dt, lo, hi) in o:
                cand[n] = a2.place(n, _nby[n], lo, hi)
            _offs = cand
            break
        except AssertionError:
            continue
    assert _offs is not None, "no SBUF packing found"
    import os as _os
    if _os.environ.get("ALLOC_DEBUG"):
        for (n, sh, dt, lo, hi) in sorted(_decls, key=lambda d: _offs[d[0]]):
            print(f"{_offs[n]:7d} +{_nby[n]:6d} -> {_offs[n]+_nby[n]:7d}  "
                  f"[{lo},{hi}] {n}")
    _T = {}
    for (n, sh, dt, lo, hi) in _decls:
        _T[n] = nc.alloc_sbuf_tensor_at(n, sh, dt, offset=_offs[n]).ap()
    ones_rb = _T["ones_rb"]
    ones_rf = _T["ones_rf"]
    ones_cb = _T["ones_cb"]
    n8192b = _T["n8192b"]
    eye_bf = _T["eye_bf"]
    eye_f = _T["eye_f"]
    dsel_bf = _T["dsel_bf"]
    projbd = _T["projbd"]
    proj_f = _T["proj_f"]
    bqc = _T["bqc"]
    bkc = _T["bkc"]
    b1c = _T["b1c"]
    bo_rb = _T["bo_rb"]
    b2_rb = _T["b2_rb"]
    srow = _T["srow"]
    stg = _T["stg"]
    srow_bf = _T["srow_bf"]
    cids = _T["cids"]
    mgk = _T["mgk"]
    flg = _T["flg"]
    statc = _T["statc"]
    cc4 = _T["cc4"]
    ndg = _T["ndg"]
    scr64 = _T["scr64"]
    wdg = _T["wdg"]
    wdg_bf = _T["wdg_bf"]
    ycsb = _T["ycsb"]
    alfb = _T["alfb"]
    beta = _T["beta"]
    eqmax = _T["eqmax"]
    rd16 = _T["rd16"]
    lsc = _T["lsc"]
    g3b = _T["g3b"]
    be3b = _T["be3b"]
    rsb = _T["rsb"]
    g2b = _T["g2b"]
    be2b = _T["be2b"]
    bvb = _T["bvb"]
    rlsb = _T["rlsb"]
    wk_bf = _T["wk_bf"]
    wv_bf = _T["wv_bf"]
    wq_bf = _T["wq_bf"]
    wo_bf = _T["wo_bf"]
    rq_bf = _T["rq_bf"]
    exch = _T["exch"]
    prt = _T["prt"]
    maxac = _T["maxac"]
    xin = _T["xin"]
    ek = _T["ek"]
    kfm = _T["kfm"]
    y1fm = _T["y1fm"]
    y1fm2 = _T["y1fm2"]
    y1t_bf = _T["y1t_bf"]
    vtmp = _T["vtmp"]
    vpr = _T["vpr"]
    ksq = _T["ksq"]
    dumpA = _T["dumpA"]
    dumpB = _T["dumpB"]
    eqm = _T["kfm"]
    ctxa0 = _T["ctxa0"]
    ctxa1 = _T["ctxa1"]
    stilb = _T["stilb"]
    sepsb = _T["sepsb"]
    qfm = _T["y1fm"][:, 0:2048]
    y1fe = _T["y1fm"][:, 2048:4096]
    y1e = _T["y1e"]
    nums = _T["nums"]
    attn_t = _T["attn_t"]
    attn_f = _T["attn_f"]
    tres = _T["tres"]
    u_tm = _T["u_tm"]
    u_st = _T["u_st"]
    ufm = _T["ufm"]
    hfm = _T["hfm"]
    uyo = _T["uyo"]
    x2b = _T["x2b"]
    qlt = _T["qlt"]
    qlf = _T["qlt"]
    xin2 = _T["xin2"]
    rq2 = _T["rq2"]
    scrB = _T["scrB"]
    w1a = _T["w1a"]
    uffb = uyo
    y2q = uyo
    outb = uyo
    scrA = _T["scrB"]
    w1b = _T["w1b"]
    w2p = [_T[f"w2p{i}"] for i in range(4)]



    pctx = nc.alloc_psum_tensor("pctx", [128, 2048], F32).ap()
    pmisc = nc.alloc_psum_tensor("pmisc", [128, 512], F32).ap()
    ptr = nc.alloc_psum_tensor("ptr", [128, 1024], BF16).ap()
    ptA = nc.alloc_psum_tensor("ptA", [128, 512], F32).ap()
    ptB = nc.alloc_psum_tensor("ptB", [128, 512], F32).ap()

    g_ctx, g_misc, g_tr, g_A, g_B = PG(P), PG(P), PG(P), PG(P), PG(P)
    g_ctxb = PG(P)
    g_ctx_all = PG2(g_ctx, g_ctxb)

    te, ve, se, ge, sy = nc.tensor, nc.vector, nc.scalar, nc.gpsimd, nc.sync

    def vsync(ins):
        t = P.inc(ins, "DVEs", 1)
        P.wait(ve, t)
        return t

    def asynca(ins):
        t = P.inc(ins, "ACTs", 1)
        P.wait(se, t)
        return t


    t_cid = P.inc(sy.dma_start(out=cids[:], in_=cid), "Scid", 16)

    t_eyeb = P.inc(ge.dma_start(out=eye_bf[:], in_=eye_d), "Seye", 16)
    t_eyef = P.inc(sy.dma_start(out=eye_f[:], in_=eye_d), "Seyef", 16)
    t_dsel = P.inc(ge.dma_start(out=dsel_bf[:], in_=dsel_d), "Sdsel", 16)
    t_proj = P.inc(sy.dma_start(out=proj_f[:], in_=projTd), "Sproj", 16)
    t_bq = P.inc(sy.dma_start(out=bqc[:], in_=bq128), "Sbq", 16)
    t_bk = P.inc(sy.dma_start(out=bkc[:], in_=bk128), "Sbk", 16)
    t_b1 = P.inc(sy.dma_start(out=b1c[:], in_=b1fm_d), "Sb1", 16)
    t_bo = P.inc(ge.dma_start(out=bo_rb[:], in_=bo_row), "Sbo", 16)
    t_b2 = P.inc(ge.dma_start(out=b2_rb[:], in_=b2_row), "Sb2", 16)

    def wload(dst, src, nch, ncol, sem):
        tt = []
        for c in range(nch):
            tt.append(P.inc(ge.dma_start(
                out=dst[:, c * ncol:(c + 1) * ncol],
                in_=src[c * 128:(c + 1) * 128, :]), sem, 16))
        return tt

    tw_k = wload(wk_bf, wkT, 8, D, "Swk")
    tw_v = wload(wv_bf, wvT, 8, D, "Swv")

    P.wait(ve, t_proj)
    ve.memset(cc4[:, 0:1], LN_EPS)
    ve.memset(cc4[:, 1:2], LNRATIO)
    ve.memset(cc4[:, 2:3], LNEPS)
    ve.memset(cc4[:, 3:4], -1.0)
    i = ve.memset(ones_rb[:], 1.0)
    t_cc4 = P.inc(i, "DVEs", 1)
    ve.memset(ones_rf[:], 1.0)
    ve.memset(ones_cb[:], 1.0)
    ve.memset(n8192b[:], 8192.0)
    ve.memset(mgk[:], FLAG_MAGIC)
    ve.memset(scr64[:], 0.0)
    ve.memset(ndg[:], 0.0)
    ve.memset(projbd[:], 0.0)
    ve.tensor_scalar(projbd[0:64, 0:256], proj_f[:], NORM, None, ALU.mult)
    i = ve.tensor_scalar(projbd[64:128, 256:512], proj_f[:], NORM, None,
                         ALU.mult)
    t_pbd = P.inc(i, "DVEs", 1)

    def row_bcast_f32(row_ap, dst, tok_row):
        g_A.pe(te)
        g_B.pe(te)
        P.wait(te, tok_row)
        te.matmul(ptA[:, 0:512], ones_rf[:], row_ap[0:1, 0:512],
                  start=True, stop=True)
        m = te.matmul(ptB[:, 0:512], ones_rf[:], row_ap[0:1, 512:1024],
                      start=True, stop=True)
        tm = P.inc(m, "PEs", 1)
        g_A.set_pe(tm)
        g_B.set_pe(tm)
        g_A.rd(ve)
        g_B.rd(ve)
        ve.tensor_copy(dst[:, 0:512], ptA[:, 0:512])
        c = ve.tensor_copy(dst[:, 512:1024], ptB[:, 0:512])
        tc = P.inc(c, "DVEs", 1)
        g_A.set_rd(tc)
        g_B.set_rd(tc)
        return tc

    _rb_state = {"n": 0, "pe": [None, None]}

    def load_row_bcast(src_dram, dst):
        sl = _rb_state["n"] % 2
        _rb_state["n"] += 1
        P.wait(sy, _rb_state["pe"][sl])
        tr = P.inc(sy.dma_start(out=srow[0:1, sl * 1088:sl * 1088 + D],
                                in_=src_dram),
                   "Srow" if sl == 0 else "Srow1", 16)
        tc = row_bcast_f32(srow[0:1, sl * 1088:sl * 1088 + D], dst, tr)
        _rb_state["pe"][sl] = g_A.last_pe
        return tc

    t_bvb = load_row_bcast(bv_row, bvb)
    t_g2b = load_row_bcast(g2_row, g2b)
    t_be2b = load_row_bcast(be2_row, be2b)
    t_g3b = load_row_bcast(g3_row, g3b)
    t_be3b = load_row_bcast(be3_row, be3b)
    if DEBUG:
        ddump("d_g3p0", g3b[:, 0:D])

    stg_rd = [None, None]
    for c in range(8):
        sl = c % 2
        P.wait(sy, stg_rd[sl])
        tr = P.inc(sy.dma_start(out=stg[:, sl * D:(sl + 1) * D],
                                in_=rhsT_d[c * 128:(c + 1) * 128, :]),
                   "Sstage" if sl == 0 else "Sxe2", 16)
        P.wait(ve, tr)
        r = ve.reduce_max(statc[:, c:c + 1], stg[:, sl * D:(sl + 1) * D],
                          axis=AXX, apply_absolute_value=True)
        stg_rd[sl] = P.inc(r, "DVEs", 1)
    P.wait(ve, stg_rd[0], stg_rd[1])
    vsync(ve.tensor_scalar(wdg[:, 0:8], statc[:, 0:8], 1e-12, 1.0 / 127.0,
                           ALU.max, ALU.mult))
    vsync(ve.tensor_scalar(scrB[:, 0:8], statc[:, 0:8], 1e-12, None, ALU.max))
    vsync(ve.reciprocal(scrB[:, 8:16], scrB[:, 0:8]))
    i = ve.tensor_scalar(wdg[:, 8:16], scrB[:, 8:16], 127.0, None, ALU.mult)
    t_rls = P.inc(i, "DVEs", 1)

    g_A.pe(te)
    g_B.pe(te)
    P.wait(te, t_rls, t_eyef)
    for c in range(8):
        pt = ptA if c < 4 else ptB
        m = te.matmul(pt[0:1, (c % 4) * 128:(c % 4 + 1) * 128],
                      wdg[:, c:c + 1], eye_f[:], start=True, stop=True)
    t_rows = P.inc(m, "PEs", 1)
    g_A.set_pe(t_rows)
    g_B.set_pe(t_rows)
    g_A.rd(ve)
    g_B.rd(ve)
    ve.tensor_copy(srow[0:1, 0:512], ptA[0:1, 0:512])
    c = ve.tensor_copy(srow[0:1, 512:1024], ptB[0:1, 0:512])
    t_rows2 = P.inc(c, "DVEs", 1)
    g_A.set_rd(t_rows2)
    g_B.set_rd(t_rows2)
    t_rsb = row_bcast_f32(srow, rsb, t_rows2)
    g_A.pe(te)
    g_B.pe(te)
    for c in range(8):
        pt = ptA if c < 4 else ptB
        m = te.matmul(pt[0:1, (c % 4) * 128:(c % 4 + 1) * 128],
                      wdg[:, 8 + c:9 + c], eye_f[:], start=True, stop=True)
    t_rows = P.inc(m, "PEs", 1)
    g_A.set_pe(t_rows)
    g_B.set_pe(t_rows)
    g_A.rd(ve)
    g_B.rd(ve)
    ve.tensor_copy(scrB[0:1, 0:512], ptA[0:1, 0:512])
    c = ve.tensor_copy(scrB[0:1, 512:1024], ptB[0:1, 0:512])
    t_rows2 = P.inc(c, "DVEs", 1)
    g_A.set_rd(t_rows2)
    g_B.set_rd(t_rows2)
    t_rlsb = row_bcast_f32(scrB, rlsb, t_rows2)

    for c in range(8):
        sl = c % 2
        P.wait(sy, stg_rd[sl])
        tr = P.inc(sy.dma_start(out=stg[:, sl * D:(sl + 1) * D],
                                in_=rhs_d[c * 128:(c + 1) * 128, :]),
                   "Sstage" if sl == 0 else "Sxe2", 16)
        P.wait(ve, tr, t_rlsb if c == 0 else None)
        ve.tensor_mul(scrB[:, 0:D], stg[:, sl * D:(sl + 1) * D],
                      rlsb[:, 0:D])
        ve.tensor_scalar(scrB[:, 0:D], scrB[:, 0:D], RMAGIC, None, ALU.add)
        i = ve.tensor_scalar(rq_bf[:, c * D:(c + 1) * D], scrB[:, 0:D],
                             -RMAGIC, None, ALU.add)
        stg_rd[sl] = P.inc(i, "DVEs", 1)
    P.wait(sy, stg_rd[0], stg_rd[1])
    t_rqd = P.inc(sy.dma_start(out=rq_dram, in_=rq_bf[:]), "Srqd", 16)

    def stats_ln(x_ap, out_ap, out_is_bf, prev_act_tok, act_guards=(),
                 dve_guards=()):
        P.wait(ve, *dve_guards)
        i = ve.tensor_scalar(dumpA[:, 0:D], x_ap, 0.0, None, ALU.add,
                             ALU.add, accum_out=statc[:, 0:1])
        tsum = vsync(i)
        P.wait(se, prev_act_tok, *act_guards)
        i = se.activation(dumpB[:, 0:D], x_ap, AF.Square,
                          accum_out=statc[:, 1:2])
        tssq = P.inc(i, "ACTs", 1)
        P.wait(ve, tssq)
        vsync(ve.tensor_scalar(statc[:, 2:3], statc[:, 0:1], 1.0 / D, None,
                               ALU.mult))
        vsync(ve.tensor_scalar(statc[:, 3:4], statc[:, 1:2], 1.0 / D, None,
                               ALU.mult))
        vsync(ve.tensor_mul(statc[:, 4:5], statc[:, 2:3], statc[:, 2:3]))
        i = ve.tensor_sub(statc[:, 5:6], statc[:, 3:4], statc[:, 4:5])
        tvar = vsync(i)
        P.wait(se, tvar)
        i = se.activation(statc[:, 6:7], statc[:, 5:6], AF.Sqrt,
                          bias=cc4[:, 0:1])
        tsd = P.inc(i, "ACTs", 1)
        P.wait(ve, tsd)
        vsync(ve.reciprocal(statc[:, 7:8], statc[:, 6:7]))
        i = ve.tensor_scalar(statc[:, 8:9], statc[:, 2:3], statc[:, 7:8],
                             -1.0, ALU.mult, ALU.mult)
        tnm = P.inc(i, "DVEs", 1)
        P.wait(se, tnm)
        i = se.activation(out_ap, x_ap, AF.Identity, bias=statc[:, 8:9],
                          scale=statc[:, 7:8])
        return P.inc(i, "ACTs", 1)

    xin_rd = [None, None]
    y1t_rd = [None, None]
    apply_tok = None
    ycs_guard = None
    g_dg = None
    g_s64 = None
    g_vpr = None
    g_wdg = None
    g_vprD = None
    g_mx = None
    g_kfm = None
    g_ndg = None

    P.wait(ge, t_rqd, t_pbd)
    tw1a = wload(w1a, w1T, 4, DFF, "Sw1a")

    tev_q = [None] * 8

    def pass1_tile(t, ybuf):
        nonlocal apply_tok
        qd, qs, bf = t // 4, t % 4, t % 2
        qd, qs, bf = t // 4, t % 4, t % 2
        P.wait(sy, xin_rd[bf])
        tx = P.inc(sy.dma_start(out=xin[:, bf * D:(bf + 1) * D],
                                in_=xh[t * 128:(t + 1) * 128, :]),
                   "Sx0" if bf == 0 else "Sx1", 16)
        P.wait(ve, tx)
        P.wait(se, tx)
        apply_tok = stats_ln(xin[:, bf * D:(bf + 1) * D],
                             y1t_bf[:, bf * D:(bf + 1) * D], True, apply_tok,
                             act_guards=(y1t_rd[bf],),
                             dve_guards=(apply_tok,))
        xin_rd[bf] = apply_tok
        g_tr.pe(te)
        P.wait(te, apply_tok)
        for c in range(8):
            mtr = te.transpose(ptr[:, c * 128:(c + 1) * 128],
                               y1t_bf[:, bf * D + c * 128:bf * D + (c + 1) * 128],
                               eye_bf[:])
        ttr = P.inc(mtr, "PEs", 1)
        g_tr.set_pe(ttr)
        g_tr.rd(ve)
        dst = strided(ybuf, 128, 512, 8, 128, qs * 128)
        i = ve.tensor_copy(dst, ptr[:].rearrange("p (c f) -> p c f", f=128))
        tev = P.inc(i, "DVEs", 1)
        g_tr.set_rd(tev)
        tev_q[qd] = tev
        g_misc.pe(te)
        P.wait(te, apply_tok, t_bk if t == 0 else None)
        for c in range(8):
            mm = te.matmul(pmisc[:, 32 + c:33 + c],
                           y1t_bf[:, bf * D + c * 128:bf * D + (c + 1) * 128],
                           ones_cb[:], start=(t == 0 and c == 0),
                           stop=False)
        t_ycs = P.inc(mm, "PEs", 1)
        g_misc.set_pe(t_ycs)
        y1t_rd[bf] = t_ycs

        if DEBUG and t == 0:
            ddump("d_xin", xin[:, 0:D])
            ddump("d_stat", statc[:])
            ddump("d_dmpA", dumpA[:, 0:D])
            ddump("d_dmpB", dumpB[:, 0:D])
            ddump("d_y1", y1t_bf[:, 0:D])
    for t in range(4):
        pass1_tile(t, y1fm)
    for qd in range(8):
        ybuf = y1fm if qd % 2 == 0 else y1fm2
        P.wait(te, tev_q[qd], *([tw_k[-1]] if qd == 0 else []))
        for oc in range(8):
            pt, g = (ptA, g_A) if oc % 2 == 0 else (ptB, g_B)
            g.pe(te)
            for kc in range(8):
                mm = te.matmul(
                    pt[:, 0:512],
                    wk_bf[:, kc * D + oc * 128:kc * D + (oc + 1) * 128],
                    ybuf[:, kc * 512:(kc + 1) * 512],
                    start=(kc == 0), stop=(kc == 7))
            tk = P.inc(mm, "PEs", 1)
            g.set_pe(tk)
            g.rd(se)
            P.wait(se, g_kfm)
            i = se.activation(kfm[:, oc * 512:(oc + 1) * 512], pt[:, 0:512],
                              AF.Identity, bias=bkc[:, oc:oc + 1])
            tke = P.inc(i, "ACTs", 1)
            g.set_rd(tke)
        t_kfm = tke

        if DEBUG and qd == 0:
            ddump("d_kfm", kfm[:, 0:512])
        if qd < 7:
            for t in range(4 * qd + 4, 4 * qd + 8):
                pass1_tile(t, y1fm2 if qd % 2 == 0 else y1fm)
        for qs3 in range(4):
            t3 = qd * 4 + qs3
            src = strided(kfm, 128, 512, 8, 128, qs3 * 128)
            P.wait(ve, t_kfm, g_dg)
            i = ve.tensor_tensor(ksq[:, 0:D], src, src, ALU.mult)
            t_ksq = P.inc(i, "DVEs", 1)
            g_kfm = t_ksq
            g_A.pe(te)
            P.wait(te, t_ksq)
            for c in range(8):
                mm = te.matmul(ptA[:, 0:16],
                               ksq[:, c * 128:(c + 1) * 128],
                               dsel_bf[:, c * 16:(c + 1) * 16],
                               start=(c == 0), stop=(c == 7))
            t_dg = P.inc(mm, "PEs", 1)
            g_dg = t_dg
            g_A.set_pe(t_dg)
            g_A.rd(se)
            P.wait(se, g_vprD)
            i = se.activation(wdg[:, 0:16], ptA[:, 0:16], AF.Exp)
            t_wdg = P.inc(i, "ACTs", 1)
            g_A.set_rd(t_wdg)
            P.wait(ve, t_wdg, g_wdg)
            i = ve.tensor_copy(wdg_bf[:], wdg[:])
            t_wdgb = P.inc(i, "DVEs", 1)

            for og in range(2):
                pt, g = (ptA, g_A) if og == 0 else (ptB, g_B)
                g.pe(te)
                if t3 == 0:
                    P.wait(te, tw_v[-1], t_bvb)
                for kc in range(8):
                    mm = te.matmul(
                        pt[:, 0:512],
                        ybuf[:, kc * 512 + qs3 * 128:kc * 512 + qs3 * 128 + 128],
                        wv_bf[:, kc * D + og * 512:kc * D + og * 512 + 512],
                        start=(kc == 0), stop=(kc == 7))
                tv = P.inc(mm, "PEs", 1)
                g.set_pe(tv)
                g.rd(ve)
                i = ve.scalar_tensor_tensor(
                    vtmp[:, og * 512:(og + 1) * 512], pt[:, 0:512], 0.0,
                    bvb[:, og * 512:(og + 1) * 512], ALU.add, ALU.add)
                tve = P.inc(i, "DVEs", 1)
                g.set_rd(tve)
            P.wait(ve, g_vpr)
            i = ve.tensor_tensor(vpr[:, 0:D], vtmp[:, 0:D],
                                 bcast(wdg, 16, 64), ALU.mult)
            t_vpr = P.inc(i, "DVEs", 1)
            g_vprD = t_vpr

            t_ek = None
            for p in range(8):
                pt, g = (ptA, g_A) if p % 2 == 0 else (ptB, g_B)
                g.pe(te)
                P.wait(te, t_pbd if t3 == 0 else None)
                mm = te.matmul(
                    pt[:, 0:512],
                    kfm[:, p * 512 + qs3 * 128:p * 512 + qs3 * 128 + 128],
                    projbd[:], start=True, stop=True)
                txp = P.inc(mm, "PEs", 1)
                g.set_pe(txp)
                g.rd(se)
                P.wait(se, g_mx)
                i = se.activation(ek[:, p * 512:(p + 1) * 512],
                                  pt[:, 0:512], AF.Exp)
                t_ek = P.inc(i, "ACTs", 1)
                g.set_rd(t_ek)
                P.wait(ve, t_ek)
                if t3 == 0:
                    i = ve.tensor_copy(maxac[:, p * 512:(p + 1) * 512],
                                       ek[:, p * 512:(p + 1) * 512])
                else:
                    i = ve.tensor_max(maxac[:, p * 512:(p + 1) * 512],
                                      maxac[:, p * 512:(p + 1) * 512],
                                      ek[:, p * 512:(p + 1) * 512])
                t_mx = P.inc(i, "DVEs", 1)

            if DEBUG and t3 == 0:
                ddump("d_ek", ek[:, 0:512])
                ddump("d_wdg", wdg[:, 0:16])
            g_mx = t_mx

            g_ctx_all.pe(te)
            P.wait(te, t_ek, t_vpr)
            for h in range(H):
                for j in range(2):
                    mm = te.matmul(
                        pctx[:, h * 128 + j * 64:h * 128 + j * 64 + 64],
                        ek[:, h * 256 + j * 128:h * 256 + j * 128 + 128],
                        vpr[:, h * 64:(h + 1) * 64],
                        start=(qs3 == 0 and h % 4 == 0 and j == 0),
                        stop=(qs3 == 3 and h % 4 == 3 and j == 1))
            t_ctx = P.inc(mm, "PEs", 1)
            g_vpr = t_ctx
            g_ctx_all.set_pe(t_ctx)

            g_misc.pe(te)
            P.wait(te, t_wdgb)
            for h in range(H):
                for j in range(2):
                    mm = te.matmul(
                        pmisc[:, h * 2 + j:h * 2 + j + 1],
                        ek[:, h * 256 + j * 128:h * 256 + j * 128 + 128],
                        wdg_bf[:, h:h + 1],
                        start=False,
                        stop=(t3 == ATILES - 1 and h == H - 1 and j == 1))
            t_ks = P.inc(mm, "PEs", 1)
            g_wdg = t_ks
            g_misc.set_pe(t_ks)

        g_ctx_all.rd(ve)
        if qd == 0:
            i = ve.tensor_copy(exch[:, 0:2048], pctx[:])
        else:
            i = ve.tensor_add(exch[:, 0:2048], exch[:, 0:2048], pctx[:])
        t_ctxev = P.inc(i, "DVEs", 1)
        g_ctx_all.set_rd(t_ctxev)

    g_misc.rd(ve)
    ve.tensor_copy(exch[:, 2048:2080], pmisc[:, 0:32])
    i = ve.tensor_copy(exch[:, 2080:2088], pmisc[:, 32:40])
    t_m1 = P.inc(i, "DVEs", 1)
    g_misc.set_rd(t_m1)
    i = ve.reduce_max(exch[:, 2088:2104],
                      bass.AP(maxac.tensor, maxac.offset,
                              [list(maxac.ap[0]), [MF, H], [1, MF]]),
                      axis=AXX)
    t_aend = P.inc(i, "DVEs", 1)

    pe_endA = P.tok("PEs")
    act_endA = P.tok("ACTs")
    if DEBUG:
        ddump("d_exch", exch[:])
        ddump("d_g3p1", g3b[:, 0:D])
    apply_tok2 = apply_tok
    g_y1e = None
    prev_tu = None
    e1_tev = [None] * 8

    def e1_pass1(pp):
        nonlocal apply_tok2, g_y1e, prev_tu
        for j2 in range(2):
            t1 = pp * 2 + j2
            bf = t1 % 2
            P.wait(sy, xin_rd[bf])
            tx = P.inc(sy.dma_start(out=xin[:, bf * D:(bf + 1) * D],
                                    in_=xe[t1 * 128:(t1 + 1) * 128, :]),
                       "Sx0" if bf == 0 else "Sx1", 16)
            P.wait(ve, tx)
            P.wait(se, tx)
            apply_tok2 = stats_ln(xin[:, bf * D:(bf + 1) * D],
                                  y1t_bf[:, bf * D:(bf + 1) * D], True,
                                  apply_tok2,
                                  act_guards=(y1t_rd[bf], prev_tu),
                                  dve_guards=(apply_tok2,))
            P.wait(se, g_y1e)
            i = se.activation(y1e[:, 0:D] if j2 == 0 else scrB[:, 0:D],
                              xin[:, bf * D:(bf + 1) * D], AF.Identity,
                              bias=statc[:, 8:9], scale=statc[:, 7:8])
            apply_tok2 = P.inc(i, "ACTs", 1)
            xin_rd[bf] = apply_tok2
            g_tr.pe(te)
            P.wait(te, apply_tok2)
            for c in range(8):
                mtr = te.transpose(
                    ptr[:, c * 128:(c + 1) * 128],
                    y1t_bf[:, bf * D + c * 128:bf * D + (c + 1) * 128],
                    eye_bf[:])
            ttr = P.inc(mtr, "PEs", 1)
            g_tr.set_pe(ttr)
            g_tr.rd(ve)
            dst = strided(y1fe, 128, 256, 8, 128, j2 * 128)
            i = ve.tensor_copy(dst, ptr[:].rearrange("p (c f) -> p c f",
                                                     f=128))
            tev = P.inc(i, "DVEs", 1)
            g_tr.set_rd(tev)
            y1t_rd[bf] = tev
            e1_tev[pp] = tev


    e1_pass1(0)

    P.wait(sy, t_aend, t_ctxev, t_cid)
    me_reg = sy.to_reg(sy.value_load(cids[0:1, 0:1]))
    with sy.If_eq(me_reg, 0):
        sy.dma_start(out=shx[0, :, :], in_=exch[:]).then_inc(P.sem("Sexw"), 16)
    with sy.Else():
        sy.dma_start(out=shx[1, :, :], in_=exch[:]).then_inc(P.sem("Sexw"), 16)
    P.cnt["Sexw"][1] += 16
    P.wait(sy, P.tok("Sexw"))
    with sy.If_eq(me_reg, 0):
        sy.dma_start(out=shf[0:1, :], in_=mgk[:]).then_inc(P.sem("Sexf"), 16)
    with sy.Else():
        sy.dma_start(out=shf[1:2, :], in_=mgk[:]).then_inc(P.sem("Sexf"), 16)
    P.cnt["Sexf"][1] += 16
    P.wait(sy, P.tok("Sexf"))

    pcnt = nc.alloc_registers("pcnt", engines=[sy.engine])
    nc.regs_mov(pcnt, 0)
    nc.br("spin_loop", engines=[sy.engine])
    with nc.body("spin_loop", valid_engines=[sy.engine]):
        sy.dma_start(out=flg[0:1, 0:32],
                     in_=shf.rearrange("a b -> (a b)").rearrange(
                         "(o c) -> o c", o=1)).then_inc(P.sem("PL"), 16)
        nc.regs_alu(pcnt, pcnt, 16, op=ALU.add)
        iv = nc.snap(pcnt, engines=[sy.engine], min_val=16)
        sy.wait_ge(P.sem("PL"), iv)
        v0 = sy.value_load(flg[0:1, 0:1])
        v1 = sy.value_load(flg[0:1, 16:17])
        vm = nc.alloc_registers("vm", engines=[sy.engine])
        nc.regs_mov(vm, v0)
        nc.regs_alu(vm, vm, v1, op=ALU.min)
        vmv = nc.snap(vm, engines=[sy.engine])
        nc.br_eq(vmv, FLAG_MAGIC, on_true="spin_end", on_false="spin_loop",
                 engines=[sy.engine])
    nc.switch_bb("spin_end")

    sy.dma_start(out=exch[:], in_=shx[0, :, :]).then_inc(P.sem("Sprt"), 16)
    tp = P.inc(sy.dma_start(out=prt[:], in_=shx[1, :, :]), "Sprt", 16)
    P.cnt["Sprt"][1] += 16
    tp = P.tok("Sprt")
    P.wait(ve, tp)
    ve.tensor_add(exch[:, 0:2088], exch[:, 0:2088], prt[:, 0:2088])
    i = ve.tensor_max(exch[:, 2088:2104], exch[:, 2088:2104],
                      prt[:, 2088:2104])
    t_comb = P.inc(i, "DVEs", 1)


    if DEBUG:
        ddump("d_comb", exch[:])
        ddump("d_g3p2", g3b[:, 0:D])
    g_A.pe(te)
    P.wait(te, t_comb)
    mm = te.transpose(ptA[0:16, 0:128], exch[:, 2088:2104], eye_f[:])
    t_kt = P.inc(mm, "PEs", 1)
    g_A.set_pe(t_kt)
    g_A.rd(ve)
    i = ve.reduce_max(statc[0:16, 9:10], ptA[0:16, 0:128], axis=AXX)
    t_km = vsync(i)
    g_A.set_rd(t_km)
    P.wait(se, t_km)
    asynca(se.activation(statc[0:16, 10:11], statc[0:16, 9:10], AF.Ln))
    i = se.activation(statc[0:16, 11:12], statc[0:16, 10:11], AF.Exp,
                      bias=cc4[0:16, 1:2], scale=cc4[0:16, 3:4])
    t_alf = P.inc(i, "ACTs", 1)
    g_B.pe(te)
    P.wait(te, t_alf)
    mm = te.matmul(ptB[0:1, 0:16], statc[0:16, 11:12], eye_f[0:16, 0:16],
                   start=True, stop=True)
    t_ar = P.inc(mm, "PEs", 1)
    g_B.set_pe(t_ar)
    g_B.rd(ve)
    i = ve.tensor_copy(srow[0:1, 1024:1040], ptB[0:1, 0:16])
    t_arw = P.inc(i, "DVEs", 1)
    g_B.set_rd(t_arw)
    g_B.pe(te)
    P.wait(te, t_arw)
    mm = te.matmul(ptB[:, 0:16], ones_rf[:], srow[0:1, 1024:1040],
                   start=True, stop=True)
    t_ab = P.inc(mm, "PEs", 1)
    g_B.set_pe(t_ab)
    g_B.rd(ve)
    i = ve.tensor_copy(alfb[:], ptB[:, 0:16])
    t_alfb = P.inc(i, "DVEs", 1)
    g_B.set_rd(t_alfb)

    P.wait(ve, t_comb)
    ve.tensor_copy(ycsb[:], exch[:, 2080:2088])
    i = ve.tensor_copy(srow_bf[0:1, 0:D], bvb[0:1, 0:D])
    t_ycsb = P.inc(i, "DVEs", 1)
    g_A.pe(te)
    g_B.pe(te)
    P.wait(te, t_ycsb)
    for c in range(8):
        for og in range(2):
            pt = ptA if og == 0 else ptB
            mm = te.matmul(pt[0:1, 0:512], ycsb[:, c:c + 1],
                           wv_bf[:, c * D + og * 512:c * D + og * 512 + 512],
                           start=(c == 0), stop=False)
    for og in range(2):
        pt = ptA if og == 0 else ptB
        mm = te.matmul(pt[0:1, 0:512], n8192b[:],
                       srow_bf[0:1, og * 512:og * 512 + 512],
                       start=False, stop=True)
    t_s = P.inc(mm, "PEs", 1)
    g_A.set_pe(t_s)
    g_B.set_pe(t_s)
    g_A.rd(ve)
    g_B.rd(ve)
    ve.tensor_scalar(srow[0:1, 0:512], ptA[0:1, 0:512], REPS, None, ALU.mult)
    i = ve.tensor_scalar(srow[0:1, 512:1024], ptB[0:1, 0:512], REPS, None,
                         ALU.mult)
    t_sr = P.inc(i, "DVEs", 1)
    g_A.set_rd(t_sr)
    g_B.set_rd(t_sr)
    t_seps = row_bcast_f32(srow, sepsb, t_sr)

    ctxa = [ctxa0, ctxa1]
    t_ctxa = None
    P.wait(ve, t_comb, t_alfb, t_seps)
    for h in range(H):
        for j in range(2):
            i = ve.scalar_tensor_tensor(
                ctxa[j][:, h * 65:h * 65 + 64],
                exch[:, h * 128 + j * 64:h * 128 + j * 64 + 64],
                alfb[:, h:h + 1],
                sepsb[:, h * 64:(h + 1) * 64], ALU.mult, ALU.add)
    t_ctxa = P.inc(i, "DVEs", 1)
    for h in range(H):
        for j in range(2):
            i = ve.tensor_scalar(ctxa[j][:, h * 65 + 64:h * 65 + 65],
                                 exch[:, 2048 + h * 2 + j:2049 + h * 2 + j],
                                 alfb[:, h:h + 1], REPS * 8192.0,
                                 ALU.mult, ALU.add)
    t_ctxa = P.inc(i, "DVEs", 1)

    if DEBUG:
        ddump2("d_ctxa", ctxa0[:], 0)
        ddump2("d_ctxa", ctxa1[:], H * 65)
    P.wait(ge, t_ctxa, P.tok("PEs"), pe_endA, act_endA, t_aend, t_comb,
           t_rqd)
    tw_q = wload(wq_bf, wqT, 8, D, "Swq")
    tw_o = wload(wo_bf, woT, 8, D, "Swo")
    t_rq2 = P.inc(ge.dma_start(out=rq2[:], in_=rq_dram), "Srq2", 16)
    tw2 = []

    g_A.pe(te)
    g_B.pe(te)
    g_misc.pe(te)
    P.wait(te, t_ctxa)
    for j in range(2):
        te.matmul(ptA[0:1, 0:512], ones_cb[:], ctxa[j][:, 0:512],
                  start=(j == 0), stop=(j == 1))
        te.matmul(ptB[0:1, 0:512], ones_cb[:], ctxa[j][:, 512:1024],
                  start=(j == 0), stop=(j == 1))
        mm = te.matmul(pmisc[0:1, 0:16], ones_cb[:], ctxa[j][:, 1024:1040],
                       start=(j == 0), stop=(j == 1))
    t_st = P.inc(mm, "PEs", 1)
    g_A.set_pe(t_st)
    g_B.set_pe(t_st)
    g_misc.set_pe(t_st)
    g_A.rd(ve)
    g_B.rd(ve)
    g_misc.rd(ve)
    ve.tensor_copy(srow[0:1, 0:512], ptA[0:1, 0:512])
    ve.tensor_copy(srow[0:1, 512:1024], ptB[0:1, 0:512])
    i = ve.tensor_copy(srow[0:1, 1024:1040], pmisc[0:1, 0:16])
    t_str = P.inc(i, "DVEs", 1)
    g_A.set_rd(t_str)
    g_B.set_rd(t_str)
    g_misc.set_rd(t_str)
    g_A.pe(te)
    g_B.pe(te)
    g_misc.pe(te)
    P.wait(te, t_str)
    te.matmul(ptA[:, 0:512], ones_rf[:], srow[0:1, 0:512], start=True,
              stop=True)
    te.matmul(ptB[:, 0:512], ones_rf[:], srow[0:1, 512:1024], start=True,
              stop=True)
    mm = te.matmul(pmisc[:, 0:16], ones_rf[:], srow[0:1, 1024:1040],
                   start=True, stop=True)
    t_stb = P.inc(mm, "PEs", 1)
    g_A.set_pe(t_stb)
    g_B.set_pe(t_stb)
    g_misc.set_pe(t_stb)
    g_A.rd(ve)
    g_B.rd(ve)
    g_misc.rd(ve)
    ve.tensor_copy(stilb[:, 0:512], ptA[:, 0:512])
    ve.tensor_copy(stilb[:, 512:1024], ptB[:, 0:512])
    i = ve.tensor_copy(stilb[:, 1024:1040], pmisc[:, 0:16])
    t_stilb = P.inc(i, "DVEs", 1)
    g_A.set_rd(t_stilb)
    g_B.set_rd(t_stilb)
    g_misc.set_rd(t_stilb)

    if DEBUG:
        ddump("d_stil", stilb[:])
    u_done = []
    ust_rd = [None, None]
    g_qfm = None
    g_eq = None
    g_att = None
    if DEBUG:
        ddump("d_g3p3", g3b[:, 0:D])
    for pp in range(ETILES // 2):
        if DEBUG and pp == 1:
            ddump("d_g3p4", g3b[:, 0:D])
        if pp > 0:
            e1_pass1(pp)
        for oc in range(8):
            pt, g = (ptA, g_A) if oc % 2 == 0 else (ptB, g_B)
            g.pe(te)
            P.wait(te, e1_tev[pp],
                   *([tw_q[-1]] if pp == 0 else []))
            for kc in range(8):
                mm = te.matmul(
                    pt[:, 0:256],
                    wq_bf[:, kc * D + oc * 128:kc * D + (oc + 1) * 128],
                    y1fe[:, kc * 256:(kc + 1) * 256],
                    start=(kc == 0), stop=(kc == 7))
            tq = P.inc(mm, "PEs", 1)
            g.set_pe(tq)
            g.rd(se)
            P.wait(se, g_qfm)
            i = se.activation(qfm[:, oc * 256:(oc + 1) * 256], pt[:, 0:256],
                              AF.Identity, bias=bqc[:, oc:oc + 1])
            t_qfm = P.inc(i, "ACTs", 1)
            g.set_rd(t_qfm)

        for j2 in range(2):
            t1 = pp * 2 + j2
            y1res = y1e if j2 == 0 else scrB
            src = strided(qfm, 128, 256, 8, 128, j2 * 128)
            P.wait(ve, t_qfm, g_dg)
            i = ve.tensor_tensor(ksq[:, 0:D], src, src, ALU.mult)
            t_qsq = P.inc(i, "DVEs", 1)
            g_qfm = t_qsq
            g_A.pe(te)
            P.wait(te, t_qsq)
            for c in range(8):
                mm = te.matmul(ptA[:, 0:16],
                               ksq[:, c * 128:(c + 1) * 128],
                               dsel_bf[:, c * 16:(c + 1) * 16],
                               start=(c == 0), stop=(c == 7))
            t_dg = P.inc(mm, "PEs", 1)
            g_dg = t_dg
            g_A.set_pe(t_dg)
            g_A.rd(se)
            P.wait(se, g_eq)
            i = se.activation(wdg[:, 0:16], ptA[:, 0:16], AF.Exp,
                              scale=cc4[:, 3:4], bias=cc4[:, 2:3])
            t_edq = P.inc(i, "ACTs", 1)
            g_A.set_rd(t_edq)

            t_eq = None
            for p in range(8):
                pt, g = (ptA, g_A) if p % 2 == 0 else (ptB, g_B)
                g.pe(te)
                mm = te.matmul(
                    pt[:, 0:512],
                    qfm[:, p * 256 + j2 * 128:p * 256 + j2 * 128 + 128],
                    projbd[:], start=True, stop=True)
                txp = P.inc(mm, "PEs", 1)
                g.set_pe(txp)
                g.rd(se)
                P.wait(se, g_eq)
                i = se.activation(ek[:, p * 512:(p + 1) * 512], pt[:, 0:512],
                                  AF.Exp)
                t_eq = P.inc(i, "ACTs", 1)
                g.set_rd(t_eq)
                P.wait(ve, t_eq)
                i = ve.reduce_max(
                    eqmax[:, 2 * p:2 * p + 2],
                    bass.AP(ek.tensor, ek.offset + p * 512,
                            [list(ek.ap[0]), [256, 2], [1, 256]]),
                    axis=AXX)
                i = P.inc(i, "DVEs", 1)
                if p == 7:
                    P.wait(ve, i)

            if DEBUG and t1 == 0:
                ddump("d_eq", ek[:, 0:512])
                ddump("d_qfm", qfm[:, 0:512])
                ddump("d_y1fe", y1fe[:, 0:512])
                ddump("d_y1e", y1e[:, 0:D])
            P.wait(ve, t_edq)
            i2 = ve.tensor_mul(beta[:], eqmax[:], wdg[:, 0:16])
            t_beta = vsync(i2)
            g_eq = t_beta

            if DEBUG and t1 == 0:
                ddump("d_beta", beta[:])
            for r in range(4):
                g_tr.pe(te)
                P.wait(te, t_eq)
                for k in range(8):
                    hh, jj = (r * 8 + k) // 2, (r * 8 + k) % 2
                    mtr = te.transpose(
                        ptr[:, k * 128:(k + 1) * 128],
                        ek[:, hh * 256 + jj * 128:hh * 256 + jj * 128 + 128],
                        eye_bf[:])
                ttr = P.inc(mtr, "PEs", 1)
                g_tr.set_pe(ttr)
                g_tr.rd(ve)
                i = ve.tensor_copy(eqm[:, r * 1024:(r + 1) * 1024], ptr[:])
                t_eqm = P.inc(i, "DVEs", 1)
                g_tr.set_rd(t_eqm)

            g_ctx_all.pe(te)
            P.wait(te, t_eqm, t_stilb)
            for h in range(H):
                for j in range(2):
                    mm = te.matmul(
                        pctx[:, h * 128:h * 128 + 65],
                        eqm[:, (h * 2 + j) * 128:(h * 2 + j + 1) * 128],
                        ctxa[j][:, h * 65:(h + 1) * 65],
                        start=(h % 4 == 0 and j == 0),
                        stop=(h % 4 == 3 and j == 1))
            t_oa = P.inc(mm, "PEs", 1)
            g_ctx_all.set_pe(t_oa)
            g_ctx_all.rd(ve)
            P.wait(ve, t_beta)
            for h in range(H):
                i = ve.scalar_tensor_tensor(
                    nums[:, h * 65:(h + 1) * 65], stilb[:, h * 65:(h + 1) * 65],
                    beta[:, h:h + 1], pctx[:, h * 128:h * 128 + 65],
                    ALU.mult, ALU.add)
            t_num = P.inc(i, "DVEs", 1)
            g_ctx_all.set_rd(t_num)

            if DEBUG and t1 == 0:
                ddump("d_nums", nums[:])
            den_ap = bass.AP(nums.tensor, nums.offset + 64,
                             [list(nums.ap[0]), [65, 16]])
            vsync(ve.reciprocal(rd16[:], den_ap))
            P.wait(ve, g_att)
            for h in range(H):
                i = ve.tensor_scalar(attn_t[:, h * 64:(h + 1) * 64],
                                     nums[:, h * 65:h * 65 + 64],
                                     rd16[:, h:h + 1], None, ALU.mult)
            t_att = P.inc(i, "DVEs", 1)

            g_tr.pe(te)
            P.wait(te, t_att)
            for c in range(8):
                mtr = te.transpose(ptr[:, c * 128:(c + 1) * 128],
                                   attn_t[:, c * 128:(c + 1) * 128], eye_bf[:])
            ttr = P.inc(mtr, "PEs", 1)
            g_att = ttr
            g_tr.set_pe(ttr)
            g_tr.rd(ve)
            i = ve.tensor_copy(attn_f[:], ptr[:])
            t_atf = P.inc(i, "DVEs", 1)
            g_tr.set_rd(t_atf)

            for og in range(2):
                pt, g = (ptA, g_A) if og == 0 else (ptB, g_B)
                g.pe(te)
                P.wait(te, t_atf, *([tw_o[-1], t_bo] if t1 == 0 else []))
                te.matmul(pt[:, 0:512], ones_rb[:],
                          bo_rb[0:1, og * 512:og * 512 + 512],
                          start=True, stop=False)
                for c in range(8):
                    mm = te.matmul(
                        pt[:, 0:512], attn_f[:, c * 128:(c + 1) * 128],
                        wo_bf[:, c * D + og * 512:c * D + og * 512 + 512],
                        start=False, stop=(c == 7))
                t_wo = P.inc(mm, "PEs", 1)
                g.set_pe(t_wo)
                g.rd(ve)
                i = ve.scalar_tensor_tensor(
                    tres[:, og * 512:og * 512 + 512], pt[:, 0:512], 0.0,
                    y1res[:, og * 512:og * 512 + 512], ALU.add, ALU.add)
                t_tr2 = P.inc(i, "DVEs", 1)
                g.set_rd(t_tr2)
            g_y1e = t_tr2

            if DEBUG and t1 == 0:
                ddump("d_tres", tres[:, 0:D])
            apply_tok2 = stats_ln(tres[:, 0:D], dumpB[:, 0:D], True,
                                  apply_tok2,
                                  act_guards=(t_tr2, prev_tu))
            P.wait(ve, apply_tok2, t_g2b if t1 == 0 else None,
                   t_be2b if t1 == 0 else None,
                   ust_rd[t1 % 2] if t1 >= 8 else None)
            ve.scalar_tensor_tensor(dumpA[:, 0:D], dumpB[:, 0:D], 1.0,
                                    g2b[:, 0:D], ALU.mult, ALU.mult)
            udst = (u_tm[:, t1 * D:(t1 + 1) * D] if t1 < 8 else
                    u_st[:, (t1 % 2) * D:(t1 % 2 + 1) * D])
            i = ve.tensor_add(udst, dumpA[:, 0:D], be2b[:, 0:D])
            tu = P.inc(i, "DVEs", 1)
            prev_tu = tu
            if t1 >= 8:
                P.wait(ge, tu)
                ust_rd[t1 % 2] = P.inc(ge.dma_start(
                    out=u_dram[(t1 - 8) * 128:(t1 - 7) * 128, :],
                    in_=u_st[:, (t1 % 2) * D:(t1 % 2 + 1) * D]),
                    "Sus0" if t1 % 2 == 0 else "Sus1", 16)
            u_done.append(tu)
            if DEBUG and t1 == 0:
                ddump("d_u", u_tm[:, 0:D])

    pe_endE1 = P.tok("PEs")
    act_endE1 = P.tok("ACTs")
    dve_endE1 = P.tok("DVEs")
    P.wait(te, ust_rd[0], ust_rd[1])
    P.wait(ve, ust_rd[0], ust_rd[1])
    P.wait(se, ust_rd[0], ust_rd[1])
    P.wait(ge, ust_rd[0], ust_rd[1])
    P.wait(ge, pe_endE1, act_endE1, dve_endE1)
    tw1b = []
    for c in range(4):
        tw1b.append(P.inc(ge.dma_start(
            out=w1b[:, c * DFF:(c + 1) * DFF],
            in_=w1T[(4 + c) * 128:(5 + c) * 128, :]), "Sw1b", 16))
    P.wait(sy, pe_endE1, act_endE1, dve_endE1, ust_rd[0], ust_rd[1])
    for pc in range(4):
        for c in range(8):
            ch = pc * 8 + c
            tw2.append(P.inc(ge.dma_start(
                out=w2p[pc][:, c * D:(c + 1) * D],
                in_=w2T[ch * 128:(ch + 1) * 128, :]), f"Sw2{pc}", 16))

    apply_tok3 = apply_tok2
    xin_rd = [None, None]
    g_out = None
    g_ql = None
    g_x2 = None
    g_uffA = None
    g_y2 = None
    u_tr_tok = [None] * 16
    uld = [None] * 8
    NPRE = 12
    HS = 12
    t_h = [None] * HS
    t_ufm_q = [None]
    t_relu_a = [None] * 32

    def w1_chunk(oc, pp, t_ufm):
        pt, g = (ptA, g_A) if oc % 2 == 0 else (ptB, g_B)
        slot = oc % HS
        g.pe(te)
        P.wait(te, t_h[slot], t_ufm,
               *([tw1a[-1], tw1b[-1]] if pp == 0 and oc == 0 else []))
        for kc in range(8):
            wsrc = w1a if kc < 4 else w1b
            mm = te.matmul(
                pt[:, 0:256],
                wsrc[:, (kc % 4) * DFF + oc * 128:
                     (kc % 4) * DFF + (oc + 1) * 128],
                ufm[:, kc * 256:(kc + 1) * 256],
                start=(kc == 0), stop=(kc == 7))
        th = P.inc(mm, "PEs", 1)
        g.set_pe(th)
        g.rd(se)
        i = se.activation(hfm[:, slot * 256:(slot + 1) * 256],
                          pt[:, 0:256], AF.Relu, bias=b1c[:, oc:oc + 1])
        t_relu_a[oc] = P.inc(i, "ACTs", 1)
        g.set_rd(t_relu_a[oc])

    def ff_half(oc, pp, j2):
        slot = oc % HS
        gcx = g_ctx if j2 == 0 else g_ctxb
        gcx.pe(te)
        P.wait(te, t_relu_a[oc], tw2[(oc // 8) * 8 + 7] if pp == 0 else None)
        for og in range(2):
            mm = te.matmul(
                pctx[:, j2 * 1024 + og * 512:j2 * 1024 + og * 512 + 512],
                hfm[:, slot * 256 + j2 * 128:slot * 256 + j2 * 128 + 128],
                w2p[oc // 8][:, (oc % 8) * D + og * 512:
                             (oc % 8) * D + og * 512 + 512],
                start=False, stop=(oc == 31))
        t_ff = P.inc(mm, "PEs", 1)
        gcx.set_pe(t_ff)
        t_h[slot] = t_ff

    def b2_half(pp, j2):
        gcx = g_ctx if j2 == 0 else g_ctxb
        gcx.pe(te)
        P.wait(te, t_ufm_q[0], *([t_b2] if pp == 0 and j2 == 0 else []))
        for og in range(2):
            mm = te.matmul(pctx[:, j2 * 1024 + og * 512:
                                j2 * 1024 + og * 512 + 512],
                           ones_rb[:], b2_rb[0:1, og * 512:og * 512 + 512],
                           start=True, stop=False)
        P.inc(mm, "PEs", 1)

    def chain_tile(pp, j2):
        nonlocal apply_tok3, g_out, g_ql, g_x2, g_uffA, g_y2
        t1 = pp * 2 + j2
        bf = t1 % 2
        P.wait(sy, xin_rd[0])
        tx = P.inc(sy.dma_start(out=xin2[:, 0:D],
                                in_=xe[t1 * 128:(t1 + 1) * 128, :]),
                   "Sxe2", 16)
        gcx = g_ctx if j2 == 0 else g_ctxb
        gcx.rd(ve)
        P.wait(ve, g_uffA, g_out,
               uld[t1 % 8] if t1 >= 8 else None)
        i = ve.scalar_tensor_tensor(uffb[:, 0:D],
                                    pctx[:, j2 * 1024:(j2 + 1) * 1024],
                                    0.0, u_tm[:, (t1 % 8) * D:
                                              (t1 % 8 + 1) * D],
                                    ALU.add, ALU.add)
        t_uff = P.inc(i, "DVEs", 1)
        gcx.set_rd(t_uff)
        if t1 < 8:
            P.wait(sy, u_tr_tok[t1], t_uff,
                   P.tok("Sus0") if t1 == 0 else None,
                   P.tok("Sus1") if t1 == 0 else None)
            uld[t1] = P.inc(sy.dma_start(
                out=u_tm[:, t1 * D:(t1 + 1) * D],
                in_=u_dram[t1 * 128:(t1 + 1) * 128, :]),
                "Su%d" % t1, 16)
        if DEBUG and t1 == 0:
            ddump("d_uff", uffb[:, 0:D])
        apply_tok3 = stats_ln(uffb[:, 0:D], scrA[:, 0:D], False,
                              apply_tok3, act_guards=(t_uff, g_x2),
                              dve_guards=(t_uff,))
        g_uffA = apply_tok3
        P.wait(ve, tx, xin_rd[0],
               t_be3b if t1 == 0 else None)
        i = ve.tensor_add(xin2[:, 0:D], xin2[:, 0:D], be3b[:, 0:D])
        P.inc(i, "DVEs", 1)
        P.wait(ve, apply_tok3, g_x2,
               t_g3b if t1 == 0 else None)
        ve.scalar_tensor_tensor(x2b[:, 0:D], scrA[:, 0:D], 1.0,
                                g3b[:, 0:D], ALU.mult, ALU.mult)
        i = ve.tensor_add(x2b[:, 0:D], x2b[:, 0:D], xin2[:, 0:D])
        t_x2 = P.inc(i, "DVEs", 1)
        xin_rd[0] = t_x2
        if DEBUG and t1 == 0:
            ddump("d_x2", x2b[:, 0:D])
            ddump("d_xh3", scrA[:, 0:D])
            ddump("d_xin2", xin2[:, 0:D])
            ddump("d_g3b", g3b[:, 0:D])
            ddump("d_be3b", be3b[:, 0:D])
        apply_tok3 = stats_ln(x2b[:, 0:D], y2q[:, 0:D], False, apply_tok3,
                              act_guards=(t_x2, g_y2),
                              dve_guards=(t_x2,))
        g_x2 = apply_tok3
        P.wait(ve, apply_tok3)
        i = ve.reduce_max(lsc[:, 0:1], y2q[:, 0:D], axis=AXX,
                          apply_absolute_value=True)
        vsync(i)
        vsync(ve.tensor_scalar(lsc[:, 1:2], lsc[:, 0:1], 1e-12,
                               1.0 / 127.0, ALU.max, ALU.mult))
        vsync(ve.reciprocal(lsc[:, 4:5], lsc[:, 1:2]))
        P.wait(ve, g_ql)
        ve.tensor_scalar(scrB[:, 0:D], y2q[:, 0:D], lsc[:, 4:5], RMAGIC,
                         ALU.mult, ALU.add)
        i = ve.tensor_scalar(qlt[:, 0:D], scrB[:, 0:D], -RMAGIC, None,
                             ALU.add)
        t_ql = P.inc(i, "DVEs", 1)
        g_y2 = t_ql
        if DEBUG and t1 == 0:
            ddump("d_ql", qlt[:, 0:D])
            ddump("d_lsc", lsc[:, 0:8])
        g_tr.pe(te)
        P.wait(te, t_ql)
        for c in range(8):
            mtr = te.transpose(ptr[:, c * 128:(c + 1) * 128],
                               qlt[:, c * 128:(c + 1) * 128], eye_bf[:])
        ttr = P.inc(mtr, "PEs", 1)
        g_ql = ttr
        g_tr.set_pe(ttr)
        g_tr.rd(ve)
        i = ve.tensor_copy(qlf[:], ptr[:])
        t_qlf = P.inc(i, "DVEs", 1)
        g_tr.set_rd(t_qlf)
        gcx.pe(te)
        P.wait(te, t_qlf, t_rq2 if pp == 0 else None)
        for og in range(2):
            for c in range(8):
                mm = te.matmul(
                    pctx[:, j2 * 1024 + og * 512:j2 * 1024 + og * 512 + 512],
                    qlf[:, c * 128:(c + 1) * 128],
                    rq2[:, c * D + og * 512:c * D + og * 512 + 512],
                    start=(c == 0), stop=(c == 7))
        t_acc = P.inc(mm, "PEs", 1)
        gcx.set_pe(t_acc)
        gcx.rd(ve)
        P.wait(ve, g_out)
        i = ve.scalar_tensor_tensor(outb[:, 0:D],
                                    pctx[:, j2 * 1024:(j2 + 1) * 1024],
                                    lsc[:, 1:2], rsb[:, 0:D],
                                    ALU.mult, ALU.mult)
        t_dq = P.inc(i, "DVEs", 1)
        gcx.set_rd(t_dq)
        i = ve.tensor_add(outb[:, 0:D], outb[:, 0:D], x2b[:, 0:D])
        t_out = P.inc(i, "DVEs", 1)
        P.wait(ge, t_out)
        g_out = P.inc(ge.dma_start(out=y_out[t1 * 128:(t1 + 1) * 128, :],
                                   in_=outb[:, 0:D]), "OUT", 16)


    for pp in range(ETILES // 2):
        if DEBUG and pp == 0:
            ddump("d_g3p5", g3b[:, 0:D])
        for j2 in range(2):
            t1 = pp * 2 + j2
            sl = t1 % 8
            g_tr.pe(te)
            P.wait(te, u_done[t1] if t1 < 8 else uld[sl])
            for c in range(8):
                mtr = te.transpose(
                    ptr[:, c * 128:(c + 1) * 128],
                    u_tm[:, sl * D + c * 128:sl * D + (c + 1) * 128],
                    eye_bf[:])
            ttr = P.inc(mtr, "PEs", 1)
            u_tr_tok[t1] = ttr
            g_tr.set_pe(ttr)
            g_tr.rd(ve)
            if t1 == 0:
                P.wait(ve, ust_rd[0], ust_rd[1])
            dst = strided(ufm, 128, 256, 8, 128, j2 * 128)
            i = ve.tensor_copy(dst, ptr[:].rearrange("p (c f) -> p c f",
                                                     f=128))
            t_ufm = P.inc(i, "DVEs", 1)
            g_tr.set_rd(t_ufm)

        t_ufm_q[0] = t_ufm
        for oc in range(3):
            w1_chunk(oc, pp, t_ufm)
        if pp > 0:
            chain_tile(pp - 1, 0)
        for oc in range(3, NPRE):
            w1_chunk(oc, pp, t_ufm)
        b2_half(pp, 0)
        for oc in range(NPRE):
            ff_half(oc, pp, 0)
        if pp > 0:
            chain_tile(pp - 1, 1)
        b2_half(pp, 1)
        for oc in range(NPRE):
            ff_half(oc, pp, 1)
        for oc in range(NPRE, 32):
            w1_chunk(oc, pp, t_ufm)
            ff_half(oc, pp, 0)
            ff_half(oc, pp, 1)
    chain_tile(ETILES // 2 - 1, 0)
    chain_tile(ETILES // 2 - 1, 1)

    ge.wait_ge(P.sem("OUT"), P.cnt["OUT"][1])
    stack.close()
    return nc


def _host_inputs():
    plans = []
    for c in range(8):
        b = c // 4
        j = c % 4
        half = j % 2
        eoff = (j // 2) * 2048
        a0 = half * AHALF
        e0 = a0 + eoff
        plans.append((b, a0, e0))
    return plans


def kernel(L0_attn_video, L0_attn_Wq, L0_attn_bq, L0_attn_Wk, L0_attn_bk,
           L0_attn_Wv, L0_attn_bv, L0_attn_Wo, L0_attn_bo, L0_attn_proj,
           L0_attn_W1, L0_attn_b1, L0_attn_W2, L0_attn_b2, L0_attn_g2,
           L0_attn_be2, L0_attn_g3, L0_attn_be3, L0_ffn_rhs):
    x = np.ascontiguousarray(np.asarray(L0_attn_video, np.float32)
                             .reshape(2, 8192, D))
    f32c = lambda a: np.ascontiguousarray(np.asarray(a, np.float32))
    wq = f32c(L0_attn_Wq); wk = f32c(L0_attn_Wk); wv = f32c(L0_attn_Wv)
    wo = f32c(L0_attn_Wo); w1 = f32c(L0_attn_W1); w2 = f32c(L0_attn_W2)
    proj = f32c(L0_attn_proj); rhs = f32c(L0_ffn_rhs)

    def col128(v, nch):
        return np.ascontiguousarray(
            np.asarray(v, np.float32).reshape(nch, 128).T)

    def row(v):
        return np.ascontiguousarray(np.asarray(v, np.float32)[None, :])

    dsel = np.zeros((128, 128), np.float32)
    for c in range(8):
        dsel[0:64, c * 16 + 2 * c] = -0.0625
        dsel[64:128, c * 16 + 2 * c + 1] = -0.0625

    common = {
        "wqT": np.ascontiguousarray(wq.T), "wkT": np.ascontiguousarray(wk.T),
        "wvT": np.ascontiguousarray(wv.T), "woT": np.ascontiguousarray(wo.T),
        "w1T": np.ascontiguousarray(w1.T), "w2T": np.ascontiguousarray(w2.T),
        "projTd": np.ascontiguousarray(proj.T),
        "rhs": rhs, "rhsT": np.ascontiguousarray(rhs.T),
        "bq128": col128(L0_attn_bq, 8), "bk128": col128(L0_attn_bk, 8),
        "b1fm": col128(L0_attn_b1, 32),
        "bv_row": row(L0_attn_bv), "bo_row": row(L0_attn_bo),
        "b2_row": row(L0_attn_b2), "g2_row": row(L0_attn_g2),
        "be2_row": row(L0_attn_be2), "g3_row": row(L0_attn_g3),
        "be3_row": row(L0_attn_be3),
        "eye": np.eye(128, dtype=np.float32),
        "dsel": dsel,
    }

    key = "v1"
    if key not in _NC_CACHE:
        _NC_CACHE[key] = build_nc()
    nc = _NC_CACHE[key]

    in_maps = []
    plans = _host_inputs()
    for c in range(8):
        b, a0, e0 = plans[c]
        m = dict(common)
        m["xh"] = np.ascontiguousarray(x[b, a0:a0 + AHALF])
        m["xe"] = np.ascontiguousarray(x[b, e0:e0 + ETOK])
        m["cid"] = np.full((1, 1), c % 2, np.int32)
        in_maps.append(m)

    res = run_bass_kernel_spmd(nc, in_maps, core_ids=list(range(8)))
    out = np.zeros((2, 8192, D), np.float32)
    for c in range(8):
        b, a0, e0 = plans[c]
        out[b, e0:e0 + ETOK] = res.results[c]["y"]
    return out.reshape(2, 8192, D)
